# version 08
# baseline (speedup 1.0000x reference)
import numpy as np
import ml_dtypes
import contextlib

import concourse.bass as bass
import concourse.mybir as mybir
from concourse.bass_utils import run_bass_kernel_spmd

F32 = mybir.dt.float32
BF16 = mybir.dt.bfloat16
U8 = mybir.dt.uint8
AF = mybir.ActivationFunctionType
ALU = mybir.AluOpType
AX = mybir.AxisListType.X
bfdt = ml_dtypes.bfloat16

B, L, D = 4, 4096, 2048
NST, DTR = 64, 128
NN = 32
NG = 16
NH, NKV, HD = 16, 4, 128
KVD = NKV * HD
NE, EH = 8, 1024
I_FFN = 8192
OWN = 2048
E4_DBUF = True
E3_DBUF = True
EPS = 1e-5
F32EPS = float(np.finfo(np.float32).eps)


class SemC:
    def __init__(self, h):
        self.h = h
        self.v = 0

    def inc(self, inst, n=1):
        self.v += n
        return inst.then_inc(self.h, n)


CONSTS_F32 = [1e-5, -0.5, -1.0, 1.0, 0.0883883476483184405, float(np.finfo(np.float32).eps),
              1.0 / 128.0, 1e-24, 1e-30, -128.0, 1.0 / D]


def emit_const_pool(nc, blk, st, sem):
    cp = st.enter_context(nc.sbuf_tensor("constpool", [128, len(CONSTS_F32)], F32))

    @blk.gpsimd
    def _(g):
        last = None
        for i, val in enumerate(CONSTS_F32):
            last = g.memset(cp[:, i:i + 1], val)
            nc.const_aps.register(F32, val, cp[:, i:i + 1])
        sem.inc(last, 1)
    return cp


class Ph:
    def __init__(self, nc, blk, st, tag):
        self.nc, self.blk, self.st, self.tag = nc, blk, st, tag

    def sb(self, name, shape, dt):
        return self.st.enter_context(self.nc.sbuf_tensor(self.tag + name, shape, dt))

    def ps(self, name, shape, dt=F32):
        return self.st.enter_context(self.nc.psum_tensor(self.tag + name, shape, dt))

    def sem(self, name):
        return SemC(self.st.enter_context(self.nc.semaphore(self.tag + name)))



def emit_phase_m(ph, io, dbg=False):
    nc, blk = ph.nc, ph.blk
    TS = 512
    NT = L // TS
    LC = 1024
    NCH = L // LC

    sems = {k: ph.sem(k) for k in
            ["ld", "gd", "pre", "st1", "st2", "yst", "mm", "v", "a"]}
    s_ld, s_gd, s_pre, s_st1, s_st2, s_yst, s_mm, s_v, s_a = (
        sems["ld"], sems["gd"], sems["pre"], sems["st1"], sems["st2"],
        sems["yst"], sems["mm"], sems["v"], sems["a"])

    aneg = ph.sb("aneg", [128, NG * NN], F32)
    mk = {}

    with contextlib.ExitStack() as st1:
        p1 = Ph(nc, blk, st1, "m1_")
        ones1 = p1.sb("ones1", [1, 128], BF16)
        ones128 = p1.sb("ones128", [128, 1], BF16)
        x32 = p1.sb("x32", [128, 16 * TS], F32)
        xbf = p1.sb("xbf", [128, 16 * TS], BF16)
        xsq = p1.sb("xsq", [128, 16 * TS], BF16)
        xn2sb = p1.sb("xn2", [128, 16 * TS], BF16)
        scr = p1.sb("scr", [128, TS], F32)
        rows = p1.sb("rows", [1, 9 * TS], F32)
        rowbf = p1.sb("rowbf", [1, 2 * TS], BF16)
        xprojsb = p1.sb("xproj", [128, 16 * 192], BF16)
        dtprojsb = p1.sb("dtproj", [128, 2048], BF16)
        dtbias_sb = p1.sb("dtbias", [128, 16], F32)
        nwsb = p1.sb("nw", [128, 16], F32)
        nbsb = p1.sb("nb", [128, 16], F32)
        drsb = p1.sb("dr", [128, TS], BF16)
        bcsb = p1.sb("bc", [64, TS], BF16)
        azsb = [p1.sb(f"az{g}", [128, TS], F32) for g in range(8)]
        rzsb = [p1.sb(f"rz{g}", [128, TS], F32) for g in range(8)]
        dsbuf = [p1.sb(f"ds{g}", [128, TS], BF16) for g in range(8)]
        usbuf = [p1.sb(f"us{g}", [128, TS], BF16) for g in range(8)]

        MU, EX2, VAR, RST1, VAR2, RST2, ATOT, BTOT, B1R = range(9)
        R = lambda k: rows[:, k * TS:(k + 1) * TS]
        ABF, BBF = range(2)
        RB = lambda k: rowbf[:, k * TS:(k + 1) * TS]

        p_r1 = p1.ps("pr1", [1, TS])
        p_r2 = p1.ps("pr2", [1, TS])
        p_p1 = p1.ps("pp1", [128, TS])
        p_p2 = p1.ps("pp2", [64, TS])
        p_rep = [p1.ps(f"prep{k}", [128, TS]) for k in range(4)]

        @blk.sync
        def _(sync):
            s_ld.inc(sync.dma_start(out=nwsb[:], in_=io["nw"]), 16)
            s_ld.inc(sync.dma_start(out=nbsb[:], in_=io["nb"]), 16)
            s_ld.inc(sync.dma_start(out=dtbias_sb[:], in_=io["dtbias"]), 16)

        @blk.gpsimd
        def _(g):
            if "_cpool" in io:
                for ci, val in enumerate(CONSTS_F32):
                    g.memset(io["_cpool"][:, ci:ci + 1], val)
                    nc.const_aps.register(F32, val, io["_cpool"][:, ci:ci + 1])
            g.memset(ones1[:], 1.0)
            g.memset(ones128[:], 1.0)
            for c in range(16):
                s_pre.inc(g.dma_start(
                    out=xprojsb[:, c * 192:(c + 1) * 192],
                    in_=io["xproj_t"][c * 128:(c + 1) * 128, :]), 16)
            s_pre.inc(g.dma_start(out=dtprojsb[:], in_=io["dtproj_t"]), 16)
            s_pre.inc(g.dma_start(out=aneg[:], in_=io["alog"]), 16)
            s_pre.inc(g.memset(scr[0:1, 0:1], 0.0), 1)
        mk["mpre"] = s_pre.v

        @blk.scalar
        def _(s):
            s.wait_ge(s_pre.h, mk["mpre"])
            s_a.inc(s.activation(scr[:], aneg[:, 0:TS], AF.Exp), 1)
            s.wait_ge(s_a.h, s_a.v)
            s_a.inc(s.mul(aneg[:, 0:TS], scr[:], -1.0), 1)
            s.wait_ge(s_a.h, s_a.v)
        mk["aneg_a"] = s_a.v

        for i in range(NT):
            @blk.sync
            def _(sync, i=i):
                if i > 0:
                    sync.wait_ge(s_v.h, mk[("m1v", i - 1)])
                for c in range(16):
                    s_ld.inc(sync.dma_start(
                        out=x32[:, c * TS:(c + 1) * TS],
                        in_=io["x_t"][c * 128:(c + 1) * 128, TS * i:TS * (i + 1)]), 16)
            mk[("xld", i)] = s_ld.v

            @blk.gpsimd
            def _(g, i=i):
                if i > 0:
                    g.wait_ge(s_v.h, mk[("m1v", i - 1)])
                for c in range(16):
                    s_gd.inc(g.dma_start(
                        out=xbf[:, c * TS:(c + 1) * TS],
                        in_=io["x_t"][c * 128:(c + 1) * 128, TS * i:TS * (i + 1)]), 16)
            mk[("xbld", i)] = s_gd.v

            @blk.scalar
            def _(s, i=i):
                s.wait_ge(s_ld.h, mk[("xld", i)])
                if i > 0:
                    s.wait_ge(s_mm.h, mk[("sums", i - 1)])
                last = None
                for c in range(16):
                    last = s.activation(xsq[:, c * TS:(c + 1) * TS],
                                        x32[:, c * TS:(c + 1) * TS], AF.Square)
                s_a.inc(last, 1)
            mk[("sq", i)] = s_a.v

            @blk.tensor
            def _(t, i=i):
                if i == 0:
                    t.wait_ge(s_pre.h, mk["mpre"])
                t.wait_ge(s_gd.h, mk[("xbld", i)])
                for c in range(16):
                    t.matmul(p_r1[:], ones128[:], xbf[:, c * TS:(c + 1) * TS],
                             start=(c == 0), stop=(c == 15))
                t.wait_ge(s_a.h, mk[("sq", i)])
                last = None
                for c in range(16):
                    last = t.matmul(p_r2[:], ones128[:], xsq[:, c * TS:(c + 1) * TS],
                                    start=(c == 0), stop=(c == 15))
                s_mm.inc(last, 1)
            mk[("sums", i)] = s_mm.v

            @blk.vector
            def _(v, i=i):
                v.wait_ge(s_mm.h, mk[("sums", i)])
                v.tensor_scalar(R(MU), p_r1[:], 1.0 / D, None, ALU.mult)
                v.tensor_scalar(R(EX2), p_r2[:], 1.0 / D, None, ALU.mult)
                v.tensor_mul(R(VAR), R(MU), R(MU))
                s_v.inc(v.tensor_sub(R(VAR), R(EX2), R(VAR)), 1)
            mk[("var", i)] = s_v.v

            @blk.scalar
            def _(s, i=i):
                s.wait_ge(s_v.h, mk[("var", i)])
                s_a.inc(s.activation(R(RST1), R(VAR), AF.Ln, bias=EPS), 1)
                s.wait_ge(s_a.h, s_a.v)
                s_a.inc(s.activation(R(RST1), R(RST1), AF.Exp, scale=-0.5), 1)
            mk[("rst1", i)] = s_a.v

            @blk.vector
            def _(v, i=i):
                v.wait_ge(s_a.h, mk[("rst1", i)])
                v.tensor_mul(R(VAR2), R(RST1), R(RST1))
                s_v.inc(v.tensor_mul(R(VAR2), R(VAR), R(VAR2)), 1)
            mk[("var2", i)] = s_v.v

            @blk.scalar
            def _(s, i=i):
                s.wait_ge(s_v.h, mk[("var2", i)])
                s_a.inc(s.activation(R(RST2), R(VAR2), AF.Ln, bias=EPS), 1)
                s.wait_ge(s_a.h, s_a.v)
                s_a.inc(s.activation(R(RST2), R(RST2), AF.Exp, scale=-0.5), 1)
            mk[("rst2", i)] = s_a.v

            @blk.vector
            def _(v, i=i):
                v.wait_ge(s_a.h, mk[("rst2", i)])
                v.tensor_mul(R(ATOT), R(RST1), R(RST2))
                v.tensor_mul(R(BTOT), R(MU), R(ATOT))
                v.tensor_copy(RB(ABF), R(ATOT))
                s_v.inc(v.tensor_copy(RB(BBF), R(BTOT)), 1)
            mk[("rbf", i)] = s_v.v

            @blk.tensor
            def _(t, i=i):
                t.wait_ge(s_v.h, mk[("rbf", i)])
                if i > 0:
                    t.wait_ge(s_v.h, mk[("m2v", i - 1, 1)])
                t.matmul(p_rep[2][:], ones1[:], RB(ABF), start=True, stop=True)
                last = t.matmul(p_rep[3][:], ones1[:], RB(BBF), start=True, stop=True)
                s_mm.inc(last, 1)
            mk[("reps", i)] = s_mm.v

            @blk.vector
            def _(v, i=i):
                v.wait_ge(s_mm.h, mk[("reps", i)])
                if i > 0:
                    v.wait_ge(s_mm.h, mk[("proj", i - 1)])
                last = None
                for c in range(16):
                    v.tensor_mul(scr[:], x32[:, c * TS:(c + 1) * TS], p_rep[2][:])
                    v.tensor_sub(scr[:], scr[:], p_rep[3][:])
                    last = v.tensor_scalar(xn2sb[:, c * TS:(c + 1) * TS], scr[:],
                                           nwsb[:, c:c + 1], nbsb[:, c:c + 1],
                                           ALU.mult, ALU.add)
                s_v.inc(last, 1)
            mk[("m1v", i)] = s_v.v

            if dbg:
                @blk.gpsimd
                def _(g, i=i):
                    g.wait_ge(s_v.h, mk[("m1v", i)])
                    for c in range(16):
                        s_st2.inc(g.dma_start(
                            out=io["xn2_d"][c * 128:(c + 1) * 128, TS * i:TS * (i + 1)],
                            in_=xn2sb[:, c * TS:(c + 1) * TS]), 16)
                    if i == 0 and "rows_d" in io:
                        s_st2.inc(g.dma_start(out=io["rows_d"], in_=rows[:]), 16)
                        s_st2.inc(g.dma_start(out=io["cpool_d"],
                                             in_=io["_cpool"][0:1, :]), 16)
                mk[("xnst", i)] = s_st2.v

            @blk.tensor
            def _(t, i=i):
                t.wait_ge(s_v.h, mk[("m1v", i)])
                for c in range(16):
                    t.matmul(p_p1[:], xprojsb[:, c * 192:c * 192 + 128],
                             xn2sb[:, c * TS:(c + 1) * TS], start=(c == 0), stop=(c == 15))
                last = None
                for c in range(16):
                    last = t.matmul(p_p2[:], xprojsb[:, c * 192 + 128:c * 192 + 192],
                                    xn2sb[:, c * TS:(c + 1) * TS],
                                    start=(c == 0), stop=(c == 15))
                s_mm.inc(last, 1)
            mk[("proj", i)] = s_mm.v

            @blk.vector
            def _(v, i=i):
                v.wait_ge(s_mm.h, mk[("proj", i)])
                if i > 0:
                    v.wait_ge(s_mm.h, mk[("dt", i - 1, 15)])
                    v.wait_ge(s_st1.h, mk[("mst", i - 1, 1)])
                v.tensor_copy(drsb[:], p_p1[:])
                s_v.inc(v.tensor_copy(bcsb[:], p_p2[:]), 1)
            mk[("drv", i)] = s_v.v

            for w in range(2):
                for g8 in range(8):
                    gg = 8 * w + g8

                    @blk.tensor
                    def _(t, i=i, gg=gg, w=w, g8=g8):
                        if gg == 0:
                            t.wait_ge(s_v.h, mk[("drv", i)])
                        if gg >= 4:
                            t.wait_ge(s_v.h, mk[("dz", i, gg - 4)])
                        elif i > 0:
                            t.wait_ge(s_v.h, mk[("dz", i - 1, 15 - (3 - gg))])
                        last = t.matmul(p_rep[gg % 4][:],
                                        dtprojsb[:, gg * 128:(gg + 1) * 128],
                                        drsb[:], start=True, stop=True)
                        s_mm.inc(last, 1)
                    mk[("dt", i, gg)] = s_mm.v

                    @blk.vector
                    def _(v, i=i, gg=gg, w=w, g8=g8):
                        v.wait_ge(s_mm.h, mk[("dt", i, gg)])
                        if w == 1 and g8 == 0:
                            v.wait_ge(s_a.h, mk[("m2a", i, 0)])
                            v.wait_ge(s_st1.h, mk[("mst", i, 0)])
                        pp = p_rep[gg % 4]
                        v.tensor_single_scalar(rzsb[g8][:], pp[:],
                                               dtbias_sb[:, gg:gg + 1], ALU.add)
                        v.tensor_single_scalar(azsb[g8][:], rzsb[g8][:], -1.0,
                                               ALU.mult)
                        v.tensor_max(azsb[g8][:], azsb[g8][:], rzsb[g8][:])
                        s_v.inc(v.tensor_relu(rzsb[g8][:], rzsb[g8][:]), 1)
                    mk[("dz", i, gg)] = s_v.v

                @blk.scalar
                def _(s, i=i, w=w):
                    s.wait_ge(s_v.h, mk[("dz", i, 8 * w + 7)])
                    last = None
                    for g8 in range(8):
                        last = s.activation(azsb[g8][:], azsb[g8][:], AF.Exp, scale=-1.0)
                    s_a.inc(last, 1)
                    s.wait_ge(s_a.h, s_a.v)
                    for g8 in range(8):
                        last = s.activation(azsb[g8][:], azsb[g8][:], AF.Ln, bias=1.0)
                    s_a.inc(last, 1)
                mk[("m2a", i, w)] = s_a.v

                @blk.vector
                def _(v, i=i, w=w):
                    v.wait_ge(s_a.h, mk[("m2a", i, w)])
                    if i > 0 and w == 0:
                        v.wait_ge(s_st1.h, mk[("mst", i - 1, 1)])
                    last = None
                    for g8 in range(8):
                        gg = 8 * w + g8
                        v.tensor_add(dsbuf[g8][:], rzsb[g8][:], azsb[g8][:])
                        last = v.tensor_mul(usbuf[g8][:], dsbuf[g8][:],
                                            xn2sb[:, gg * TS:(gg + 1) * TS])
                    s_v.inc(last, 1)
                mk[("m2v", i, w)] = s_v.v

                @blk.gpsimd
                def _(g, i=i, w=w):
                    g.wait_ge(s_v.h, mk[("m2v", i, w)])
                    for g8 in range(8):
                        gg = 8 * w + g8
                        s_st1.inc(g.dma_start(
                            out=io["delta_d"][gg * 128:(gg + 1) * 128, TS * i:TS * (i + 1)],
                            in_=dsbuf[g8][:]), 16)
                        s_st1.inc(g.dma_start(
                            out=io["u_d"][gg * 128:(gg + 1) * 128, TS * i:TS * (i + 1)],
                            in_=usbuf[g8][:]), 16)
                    if w == 1:
                        s_st1.inc(g.dma_start(out=io["bc_d"][:, TS * i:TS * (i + 1)],
                                              in_=bcsb[:]), 16)
                mk[("mst", i, w)] = s_st1.v

    with contextlib.ExitStack() as st4:
        p4 = Ph(nc, blk, st4, "m4_")
        deltasb = p4.sb("delta", [128, NG * LC], BF16)
        usb4 = p4.sb("u", [128, NG * LC], BF16)
        dA = p4.sb("dA", [128, NG * LC], F32)
        dU = p4.sb("dU", [128, LC], BF16)
        hbuf = p4.sb("h", [128, LC], BF16)
        tsc = p4.sb("tsc", [128, LC], BF16)
        brep = p4.sb("brep", [128, LC], BF16)
        crep = p4.sb("crep", [128, LC], BF16)
        ysub = p4.sb("ysub", [128, NG * LC], F32)
        state = p4.sb("state", [128, NG * NN], F32)

        for lc in range(NCH):
            @blk.sync
            def _(sync, lc=lc):
                if lc == 0:
                    sync.wait_ge(s_st1.h, mk[("mst", NT - 1, 1)])
                else:
                    sync.wait_ge(s_v.h, mk[("nd", lc - 1, NN - 1)])
                for gg in range(NG):
                    s_ld.inc(sync.dma_start(
                        out=deltasb[:, gg * LC:(gg + 1) * LC],
                        in_=io["delta_d"][gg * 128:(gg + 1) * 128,
                                          lc * LC:(lc + 1) * LC]), 16)
                    s_ld.inc(sync.dma_start(
                        out=usb4[:, gg * LC:(gg + 1) * LC],
                        in_=io["u_d"][gg * 128:(gg + 1) * 128,
                                      lc * LC:(lc + 1) * LC]), 16)
            mk[("m4ld", lc)] = s_ld.v

            for n in range(NN):
                @blk.gpsimd
                def _(g, lc=lc, n=n):
                    if lc == 0 and n == 0:
                        g.wait_ge(s_st1.h, mk[("mst", NT - 1, 1)])
                    if not (lc == 0 and n == 0):
                        pn, plc = (n - 1, lc) if n >= 1 else (NN - 1, lc - 1)
                        g.wait_ge(s_v.h, mk[("nd", plc, pn)])
                    s_gd.inc(g.dma_start(
                        out=brep[:],
                        in_=io["bc_d"][n:n + 1, lc * LC:(lc + 1) * LC]
                        .partition_broadcast(128)), 16)
                    s_gd.inc(g.dma_start(
                        out=crep[:],
                        in_=io["bc_d"][NN + n:NN + n + 1, lc * LC:(lc + 1) * LC]
                        .partition_broadcast(128)), 16)
                mk[("bc", lc, n)] = s_gd.v

                for w in range(2):
                    @blk.scalar
                    def _(s, lc=lc, n=n, w=w):
                        if n == 0 and w == 0:
                            s.wait_ge(s_ld.h, mk[("m4ld", lc)])
                            if lc == 0:
                                s.wait_ge(s_a.h, mk["aneg_a"])
                        if not (lc == 0 and n == 0):
                            pn, plc = (n - 1, lc) if n >= 1 else (NN - 1, lc - 1)
                            s.wait_ge(s_v.h, mk[("nh" if w == 0 else "nd", plc, pn)])
                        last = None
                        for g8 in range(8):
                            gg = 8 * w + g8
                            last = s.activation(
                                dA[:, gg * LC:(gg + 1) * LC],
                                deltasb[:, gg * LC:(gg + 1) * LC], AF.Exp,
                                scale=aneg[:, gg * NN + n:gg * NN + n + 1])
                        s_a.inc(last, 1)
                    mk[("dA", lc, n, w)] = s_a.v

                    @blk.vector
                    def _(v, lc=lc, n=n, w=w):
                        v.wait_ge(s_a.h, mk[("dA", lc, n, w)])
                        if w == 0:
                            v.wait_ge(s_gd.h, mk[("bc", lc, n)])
                            if n % 8 == 0 and not (lc == 0 and n == 0):
                                psub, plc = (n // 8 - 1, lc) if n >= 8 else (3, lc - 1)
                                v.wait_ge(s_yst.h, mk[("yst", plc, psub)])
                        last = None
                        for g8 in range(8):
                            gg = 8 * w + g8
                            v.tensor_mul(dU[:], usb4[:, gg * LC:(gg + 1) * LC], brep[:])
                            idx = gg * NN + n
                            init = 0.0 if lc == 0 else state[:, idx:idx + 1]
                            v.tensor_tensor_scan(hbuf[:], dA[:, gg * LC:(gg + 1) * LC],
                                                 dU[:], init, ALU.mult, ALU.add)
                            v.tensor_copy(state[:, idx:idx + 1], hbuf[:, LC - 1:LC])
                            ysl = ysub[:, gg * LC:(gg + 1) * LC]
                            if n % 8 == 0:
                                last = v.tensor_mul(ysl, hbuf[:], crep[:])
                            else:
                                v.tensor_mul(tsc[:], hbuf[:], crep[:])
                                last = v.tensor_add(ysl, ysl, tsc[:])
                        s_v.inc(last, 1)
                    if w == 0:
                        mk[("nh", lc, n)] = s_v.v
                    else:
                        mk[("nd", lc, n)] = s_v.v

                if n % 8 == 7:
                    @blk.gpsimd
                    def _(g, lc=lc, n=n):
                        g.wait_ge(s_v.h, mk[("nd", lc, n)])
                        if n >= 8:
                            g.wait_ge(s_yst.h, mk[("yst", lc, n // 8 - 1)])
                        first = (n == 7)
                        for gg in range(NG):
                            kw = {} if first else {"accum_op": ALU.add}
                            src = ysub[:, gg * LC:(gg + 1) * LC].rearrange(
                                "p (nb c) -> p nb c", c=128)
                            for r in range(2):
                                dst = io["y_rs"][r * D + gg * 128:
                                                 r * D + (gg + 1) * 128,
                                                 lc * 512:(lc + 1) * 512].rearrange(
                                    "p (nb c) -> p nb c", c=128)
                                s_yst.inc(g.dma_start(
                                    out=dst, in_=src[:, r::2, :], **kw), 16)
                    mk[("yst", lc, n // 8)] = s_yst.v

    @blk.sync
    def _(sync):
        sync.wait_ge(s_yst.h, mk[("yst", NCH - 1, 3)])
        sync.wait_ge(s_st1.h, mk[("mst", NT - 1, 1)])
        if "aneg_d" in io:
            s_ld.inc(sync.dma_start(out=io["aneg_d"], in_=aneg[:]), 16)
            sync.wait_ge(s_ld.h, s_ld.v)

    return mk, sems


def declare_m_io(nc, dbg=False):
    io = {}
    io["x_t"] = nc.dram_tensor("x_t", [D, L], F32, kind="ExternalInput").ap()
    io["nw"] = nc.dram_tensor("nw", [128, 16], F32, kind="ExternalInput").ap()
    io["nb"] = nc.dram_tensor("nb", [128, 16], F32, kind="ExternalInput").ap()
    io["xproj_t"] = nc.dram_tensor("xproj_t", [D, 192], F32, kind="ExternalInput").ap()
    io["dtproj_t"] = nc.dram_tensor("dtproj_t", [DTR, D], F32, kind="ExternalInput").ap()
    io["dtbias"] = nc.dram_tensor("dtbias", [128, 16], F32, kind="ExternalInput").ap()
    io["alog"] = nc.dram_tensor("alog", [128, NG * NN], F32, kind="ExternalInput").ap()
    io["y_rs"] = nc.dram_tensor("y_rs", [2 * D, OWN], F32, kind="ExternalOutput").ap()
    kd = "ExternalOutput" if dbg else "Internal"
    io["delta_d"] = nc.dram_tensor("delta_d", [D, L], BF16, kind=kd).ap()
    io["u_d"] = nc.dram_tensor("u_d", [D, L], BF16, kind=kd).ap()
    io["bc_d"] = nc.dram_tensor("bc_d", [64, L], BF16, kind=kd).ap()
    if dbg:
        io["xn2_d"] = nc.dram_tensor("xn2_d", [D, L], BF16, kind=kd).ap()
        io["rows_d"] = nc.dram_tensor("rows_d", [1, 9 * 512], F32, kind=kd).ap()
        io["cpool_d"] = nc.dram_tensor("cpool_d", [1, len(CONSTS_F32)], F32, kind=kd).ap()
        io["aneg_d"] = nc.dram_tensor("aneg_d", [128, NG * NN], F32, kind=kd).ap()
    return io


def build_phase_m(dbg=False):
    nc = bass.Bass("TRN2", target_bir_lowering=False, debug=False)
    io = declare_m_io(nc, dbg)
    with contextlib.ExitStack() as st:
        with nc.Block() as blk:
            ph = Ph(nc, blk, st, "m_")
            io["_cpool"] = ph.sb("cpool", [128, len(CONSTS_F32)], F32)[:]
            emit_phase_m(ph, io, dbg)
    return nc



def byteplane(w_int32):
    return np.ascontiguousarray(w_int32.view(np.uint8)[..., 0::4])


def own_l_index(par):
    lb = np.arange(16) * 2 + par
    return (lb[:, None] * 128 + np.arange(128)[None, :]).reshape(-1)


def prep_m_inputs(inputs, cid):
    b, par = cid // 2, cid % 2
    n0 = par * NN
    d = {}
    d["x_t"] = np.ascontiguousarray(inputs["L0_attn_x"][b].T)
    d["nw"] = np.ascontiguousarray(inputs["L0_attn_norm_w"].reshape(16, 128).T)
    d["nb"] = np.ascontiguousarray(inputs["L0_attn_norm_b"].reshape(16, 128).T)
    xp = inputs["L0_attn_x_proj_w"]
    d["xproj_t"] = np.ascontiguousarray(np.concatenate(
        [xp[0:128].T, xp[128 + n0:128 + n0 + NN].T, xp[192 + n0:192 + n0 + NN].T],
        axis=1))
    d["dtproj_t"] = np.ascontiguousarray(inputs["L0_attn_dt_proj_w"].T)
    d["dtbias"] = np.ascontiguousarray(inputs["L0_attn_dt_proj_b"].reshape(16, 128).T)
    d["alog"] = np.ascontiguousarray(
        inputs["L0_attn_A_log"][:, n0:n0 + NN].reshape(NG, 128, NN)
        .transpose(1, 0, 2).reshape(128, NG * NN))
    return d





def emit_ln_tile(blk, bufs, sems, mk, tag, i, src_dram, war_marks=(),
                 dst_stride=512, dst_off=0):
    TS = 512
    x32, xbf, xsq, xnbf = bufs["x32"], bufs["xbf"], bufs["xsq"], bufs["xnbf"]
    scr, rows, rowbf = bufs["scr"], bufs["rows"], bufs["rowbf"]
    p_r1, p_r2, p_a, p_b = bufs["p_r1"], bufs["p_r2"], bufs["p_a"], bufs["p_b"]
    ones1, ones128 = bufs["ones1"], bufs["ones128"]
    s_ld, s_gd, s_mm, s_v, s_a = (sems["ld"], sems["gd"], sems["mm"],
                                  sems["v"], sems["a"])
    MU, VAR, RST, B1 = 0, 1, 2, 3
    R = lambda k: rows[:, k * TS:(k + 1) * TS]
    RB = lambda k: rowbf[:, k * TS:(k + 1) * TS]

    @blk.sync
    def _(sync):
        for sem, key in war_marks:
            sync.wait_ge(sem.h, mk[key])
        for c in range(16):
            s_ld.inc(sync.dma_start(
                out=x32[:, c * TS:(c + 1) * TS],
                in_=src_dram[c * 128:(c + 1) * 128, TS * i:TS * (i + 1)]), 16)
    mk[(tag, "xld", i)] = s_ld.v

    @blk.gpsimd
    def _(g):
        for sem, key in war_marks:
            g.wait_ge(sem.h, mk[key])
        for c in range(16):
            s_gd.inc(g.dma_start(
                out=xbf[:, c * TS:(c + 1) * TS],
                in_=src_dram[c * 128:(c + 1) * 128, TS * i:TS * (i + 1)]), 16)
    mk[(tag, "xbld", i)] = s_gd.v

    @blk.scalar
    def _(s):
        s.wait_ge(s_ld.h, mk[(tag, "xld", i)])
        for sem, key in war_marks:
            s.wait_ge(sem.h, mk[key])
        last = None
        for c in range(16):
            last = s.activation(xsq[:, c * TS:(c + 1) * TS],
                                x32[:, c * TS:(c + 1) * TS], AF.Square)
        s_a.inc(last, 1)
    mk[(tag, "sq", i)] = s_a.v

    @blk.tensor
    def _(t):
        t.wait_ge(s_gd.h, mk[(tag, "xbld", i)])
        for sem, key in war_marks:
            t.wait_ge(sem.h, mk[key])
        for c in range(16):
            t.matmul(p_r1[:], ones128[:], xbf[:, c * TS:(c + 1) * TS],
                     start=(c == 0), stop=(c == 15))
        t.wait_ge(s_a.h, mk[(tag, "sq", i)])
        last = None
        for c in range(16):
            last = t.matmul(p_r2[:], ones128[:], xsq[:, c * TS:(c + 1) * TS],
                            start=(c == 0), stop=(c == 15))
        s_mm.inc(last, 1)
    mk[(tag, "sums", i)] = s_mm.v

    @blk.vector
    def _(v):
        v.wait_ge(s_mm.h, mk[(tag, "sums", i)])
        v.tensor_scalar(R(MU), p_r1[:], 1.0 / D, None, ALU.mult)
        v.tensor_scalar(R(VAR), p_r2[:], 1.0 / D, None, ALU.mult)
        v.tensor_mul(R(B1), R(MU), R(MU))
        s_v.inc(v.tensor_sub(R(VAR), R(VAR), R(B1)), 1)
    mk[(tag, "var", i)] = s_v.v

    @blk.scalar
    def _(s):
        s.wait_ge(s_v.h, mk[(tag, "var", i)])
        s_a.inc(s.activation(R(RST), R(VAR), AF.Ln, bias=EPS), 1)
        s.wait_ge(s_a.h, s_a.v)
        s_a.inc(s.activation(R(RST), R(RST), AF.Exp, scale=-0.5), 1)
    mk[(tag, "rst", i)] = s_a.v

    @blk.vector
    def _(v):
        v.wait_ge(s_a.h, mk[(tag, "rst", i)])
        v.tensor_mul(R(B1), R(MU), R(RST))
        v.tensor_copy(RB(0), R(RST))
        s_v.inc(v.tensor_copy(RB(1), R(B1)), 1)
    mk[(tag, "rbf", i)] = s_v.v

    @blk.tensor
    def _(t):
        t.wait_ge(s_v.h, mk[(tag, "rbf", i)])
        t.matmul(p_a[:], ones1[:], RB(0), start=True, stop=True)
        last = t.matmul(p_b[:], ones1[:], RB(1), start=True, stop=True)
        s_mm.inc(last, 1)
    mk[(tag, "reps", i)] = s_mm.v

    @blk.vector
    def _(v):
        v.wait_ge(s_mm.h, mk[(tag, "reps", i)])
        last = None
        for c in range(16):
            v.tensor_mul(scr[:], x32[:, c * TS:(c + 1) * TS], p_a[:])
            last = v.tensor_sub(
                xnbf[:, c * dst_stride + dst_off:c * dst_stride + dst_off + TS],
                scr[:], p_b[:])
        s_v.inc(last, 1)
    mk[(tag, "ln", i)] = s_v.v
    return (tag, "ln", i)


def mk_ln_bufs(p, xsq=None):
    TS = 512
    return {
        "x32": p.sb("x32", [128, 16 * TS], F32),
        "xbf": p.sb("xbf", [128, 16 * TS], BF16),
        "xsq": xsq if xsq is not None else p.sb("xsq", [128, 16 * TS], BF16),
        "xnbf": p.sb("xnbf", [128, 16 * TS], BF16),
        "scr": p.sb("scr", [128, TS], F32),
        "rows": p.sb("rows", [1, 4 * TS], F32),
        "rowbf": p.sb("rowbf", [1, 2 * TS], BF16),
        "p_r1": p.ps("pr1", [1, TS]),
        "p_r2": p.ps("pr2", [1, TS]),
        "p_a": p.ps("pa", [128, TS]),
        "p_b": p.ps("pb", [128, TS]),
    }


def emit_phase_n(ph, io, dbg=False):
    nc, blk = ph.nc, ph.blk
    TS = 512
    NT = OWN // TS

    sems = {k: ph.sem(k) for k in ["ld", "gd", "pre", "wt", "st", "mm", "v", "a",
                                   "wtB", "e4A", "e4B"]}
    s_ld, s_gd, s_pre, s_wt, s_st, s_mm, s_v, s_a = (
        sems["ld"], sems["gd"], sems["pre"], sems["wt"], sems["st"],
        sems["mm"], sems["v"], sems["a"])
    s_wtB, s_e4A, s_e4B = sems["wtB"], sems["e4A"], sems["e4B"]
    mk = {}

    ones1 = ph.sb("ones1", [1, 128], BF16)
    ones128 = ph.sb("ones128", [128, 1], BF16)
    m128row = ph.sb("m128row", [1, 128], BF16)
    dtile = ph.sb("dtile", [128, 16], F32)
    gscsb = ph.sb("gsc", [128, 64], F32)
    uscsb = ph.sb("usc", [128, 64], F32)
    dscsb = ph.sb("dsc", [128, 16], F32)

    @blk.sync
    def _(sync):
        s_pre.inc(sync.dma_start(out=dtile[:], in_=io["D_tile"]), 16)
        s_pre.inc(sync.dma_start(out=gscsb[:], in_=io["gsc"]), 16)
        s_pre.inc(sync.dma_start(out=uscsb[:], in_=io["usc"]), 16)
        s_pre.inc(sync.dma_start(out=dscsb[:], in_=io["dsc"]), 16)

    @blk.gpsimd
    def _(g):
        if "_cpool" in io:
            for ci, val in enumerate(CONSTS_F32):
                g.memset(io["_cpool"][:, ci:ci + 1], val)
                nc.const_aps.register(F32, val, io["_cpool"][:, ci:ci + 1])
        g.memset(ones1[:], 1.0)
        g.memset(ones128[:], 1.0)
        s_pre.inc(g.memset(m128row[:], -128.0), 1)
    mk["pre"] = s_pre.v

    def pre_wait(eng):
        eng.wait_ge(s_pre.h, mk["pre"])

    with contextlib.ExitStack() as st1:
        p1 = Ph(nc, blk, st1, "n1_")
        bufs = mk_ln_bufs(p1)
        bufs.update(ones1=ones1, ones128=ones128)
        zbf = p1.sb("zbf", [128, 16 * TS], BF16)
        ystg = p1.sb("ystg", [128, 2 * TS], F32)
        wstr = p1.sb("wstr", [128, 2 * 16 * 1024], BF16)
        ra32 = p1.sb("ra32", [128, 8 * TS], F32)
        p_o = [p1.ps(f"po{k}", [128, TS]) for k in range(4)]

        for i in range(NT):
            war = [(s_pre, "pre")] if i == 0 else [(s_st, ("n1", "rast", i - 1, 1))]
            emit_ln_tile(blk, bufs, sems, mk, "n1", i, io["x_own"], war)

            for c in range(16):
                @blk.gpsimd
                def _(g, i=i, c=c):
                    if not (i == 0 and c == 0):
                        pc = (i, c - 1) if c else (i - 1, 15)
                        g.wait_ge(s_v.h, mk[("n1", "zc", *pc)])
                    s_gd.inc(g.dma_start(
                        out=ystg[:, 0:TS],
                        in_=io["y0own"][c * 128:(c + 1) * 128, TS * i:TS * (i + 1)]), 16)
                    if io.get("y1own") is not None:
                        s_gd.inc(g.dma_start(
                            out=ystg[:, TS:2 * TS],
                            in_=io["y1own"][c * 128:(c + 1) * 128,
                                            TS * i:TS * (i + 1)]), 16)
                mk[("n1", "yld", i, c)] = s_gd.v

                @blk.vector
                def _(v, i=i, c=c):
                    if c == 0:
                        v.wait_ge(s_v.h, mk[("n1", "ln", i)])
                        if i == 0:
                            v.wait_ge(s_pre.h, mk["pre"])
                    v.wait_ge(s_gd.h, mk[("n1", "yld", i, c)])
                    last = v.scalar_tensor_tensor(
                        zbf[:, c * TS:(c + 1) * TS],
                        bufs["xnbf"][:, c * TS:(c + 1) * TS],
                        dtile[:, c:c + 1], ystg[:, 0:TS], ALU.mult, ALU.add)
                    if io.get("y1own") is not None:
                        last = v.tensor_add(zbf[:, c * TS:(c + 1) * TS],
                                            zbf[:, c * TS:(c + 1) * TS],
                                            ystg[:, TS:2 * TS])
                    s_v.inc(last, 1)
                mk[("n1", "zc", i, c)] = s_v.v
            mk[("n1", "zv", i)] = s_v.v

            for oh in range(2):
                if i == 0:
                    @blk.gpsimd
                    def _(g, oh=oh):
                        for half in range(2):
                            s_wt.inc(g.dma_start(
                                out=wstr[:, (oh * 2 + half) * 8192:
                                         (oh * 2 + half + 1) * 8192],
                                in_=io["opr"][:, (oh * 2 + half) * 8192:
                                              (oh * 2 + half + 1) * 8192]), 16)
                    mk[("n1", "wld", oh)] = s_wt.v

                for q in range(2):
                    @blk.tensor
                    def _(t, i=i, oh=oh, q=q):
                        if q == 0:
                            if i == 0:
                                t.wait_ge(s_wt.h, mk[("n1", "wld", oh)])
                            t.wait_ge(s_v.h, mk[("n1", "zv", i)])
                        if i > 0 or oh > 0 or q > 0:
                            pq = (i, oh, 0) if q else ((i, oh - 1, 1) if oh else (i - 1, 1, 1))
                            t.wait_ge(s_v.h, mk[("n1", "evac", *pq)])
                        last = None
                        for ot4 in range(4):
                            ot = q * 4 + ot4
                            for c in range(16):
                                last = t.matmul(
                                    p_o[ot4][:],
                                    wstr[:, (oh * 16 + c) * 1024 + ot * 128:
                                         (oh * 16 + c) * 1024 + (ot + 1) * 128],
                                    zbf[:, c * TS:(c + 1) * TS],
                                    start=(c == 0), stop=(c == 15))
                        s_mm.inc(last, 1)
                    mk[("n1", "mmq", i, oh, q)] = s_mm.v
                    if q == 1:
                        mk[("n1", "mm", i, oh)] = s_mm.v

                    @blk.vector
                    def _(v, i=i, oh=oh, q=q):
                        v.wait_ge(s_mm.h, mk[("n1", "mmq", i, oh, q)])
                        if q == 0:
                            if oh == 1:
                                v.wait_ge(s_st.h, mk[("n1", "rast", i, 0)])
                            elif i > 0:
                                v.wait_ge(s_st.h, mk[("n1", "rast", i - 1, 1)])
                        last = None
                        for ot4 in range(4):
                            ot = q * 4 + ot4
                            go = oh * 8 + ot
                            last = v.tensor_add(ra32[:, ot * TS:(ot + 1) * TS],
                                                p_o[ot4][:],
                                                bufs["x32"][:, go * TS:(go + 1) * TS])
                        s_v.inc(last, 1)
                    mk[("n1", "evac", i, oh, q)] = s_v.v

                @blk.gpsimd
                def _(g, i=i, oh=oh):
                    g.wait_ge(s_v.h, mk[("n1", "evac", i, oh, 1)])
                    for ot in range(8):
                        go = oh * 8 + ot
                        s_st.inc(g.dma_start(
                            out=io["resA_d"][go * 128:(go + 1) * 128,
                                             TS * i:TS * (i + 1)],
                            in_=ra32[:, ot * TS:(ot + 1) * TS]), 16)
                mk[("n1", "rast", i, oh)] = s_st.v

    with contextlib.ExitStack() as st2:
        p2 = Ph(nc, blk, st2, "n2_")
        hsb = p2.sb("hsb", [128, 64 * TS], BF16)
        bufs2 = mk_ln_bufs(p2, xsq=hsb)
        bufs2.update(ones1=ones1, ones128=ones128)
        gsb = p2.sb("gsb", [128, 4 * TS], BF16)
        usb = p2.sb("usb", [128, 4 * TS], BF16)
        wstr2 = p2.sb("wstr2", [128, 16 * 1024], BF16)
        trow = p2.sb("trow", [1, 4 * TS], BF16)
        r132 = p2.sb("r132", [128, 4 * TS], F32)
        p_f = [p2.ps(f"pf{k}", [128, TS]) for k in range(4)]

        for i in range(NT):
            war = [(s_st, ("n1", "rast", NT - 1, 1))] if i == 0 else \
                  [(s_st, ("n2", "r1st", i - 1, 3))]
            emit_ln_tile(blk, bufs2, sems, mk, "n2", i, io["resA_d"], war)

            @blk.tensor
            def _(t, i=i):
                t.wait_ge(s_v.h, mk[("n2", "ln", i)])
                last = None
                for c in range(16):
                    last = t.matmul(bufs2["p_r1"][:], ones128[:],
                                    bufs2["xnbf"][:, c * TS:(c + 1) * TS],
                                    start=(c == 0), stop=(c == 15))
                s_mm.inc(last, 1)
            mk[("n2", "trow", i)] = s_mm.v

            @blk.vector
            def _(v, i=i):
                v.wait_ge(s_mm.h, mk[("n2", "trow", i)])
                R3 = bufs2["rows"][:, 3 * TS:4 * TS]
                v.tensor_copy(trow[:, 0:TS], bufs2["p_r1"][:])
                v.tensor_copy(R3, trow[:, 0:TS])
                v.tensor_sub(R3, bufs2["p_r1"][:], R3)
                s_v.inc(v.tensor_copy(trow[:, TS:2 * TS], R3), 1)
            mk[("n2", "trowv", i)] = s_v.v

            for band in range(16):
                for mat in range(2):
                    wsrc = io["wg_r"] if mat == 0 else io["wu_r"]
                    wseq = mk.get(("n2", "wseq_ctr"), 0)
                    mk[("n2", "wseq_ctr")] = wseq + 1
                    slot = wseq % 2
                    s_w = s_wtB if slot == 0 else s_e4A

                    @blk.gpsimd
                    def _(g, i=i, band=band, mat=mat, wsrc=wsrc, wseq=wseq,
                          slot=slot, s_w=s_w):
                        if ("n2", "wmm_seq", wseq - 2) in mk:
                            g.wait_ge(s_mm.h, mk[("n2", "wmm_seq", wseq - 2)])
                        s_w.inc(g.dma_start(
                            out=wstr2[:, slot * 8192:(slot + 1) * 8192],
                            in_=wsrc[:, band * 8192:(band + 1) * 8192]), 16)
                    mk[("n2", "gwld", i, band, mat)] = s_w.v

                    @blk.tensor
                    def _(t, i=i, band=band, mat=mat, wseq=wseq, slot=slot,
                          s_w=s_w):
                        t.wait_ge(s_w.h, mk[("n2", "gwld", i, band, mat)])
                        if mat == 0 and band == 0:
                            t.wait_ge(s_v.h, mk[("n2", "trowv", i)])
                            if ("n2", "dev_prev") in mk:
                                t.wait_ge(s_v.h, mk[("n2", "dev_prev")])
                        if ("n2", "gev_prev") in mk:
                            t.wait_ge(s_a.h, mk[("n2", "gev_prev")])
                        last = None
                        for ot in range(4):
                            for c in range(16):
                                t.matmul(p_f[ot][:],
                                         wstr2[:, slot * 8192 + c * 512 + ot * 128:
                                               slot * 8192 + c * 512 + (ot + 1) * 128],
                                         bufs2["xnbf"][:, c * TS:(c + 1) * TS],
                                         start=(c == 0), stop=False)
                            t.matmul(p_f[ot][:], m128row[:], trow[0:1, 0:TS],
                                     start=False, stop=False)
                            last = t.matmul(p_f[ot][:], m128row[:], trow[0:1, TS:2 * TS],
                                            start=False, stop=True)
                        s_mm.inc(last, 1)
                        mk[("n2", "wmm_seq", wseq)] = s_mm.v
                    mk[("n2", "gmm", i, band, mat)] = s_mm.v
                    mk[("n2", "gmm_prev")] = s_mm.v

                    @blk.scalar
                    def _(s, i=i, band=band, mat=mat):
                        s.wait_ge(s_mm.h, mk[("n2", "gmm", i, band, mat)])
                        if mat == 1 and ("n2", "hprev") in mk:
                            s.wait_ge(s_v.h, mk[("n2", "hprev")])
                        last = None
                        for ot in range(4):
                            it = band * 4 + ot
                            if mat == 0:
                                last = s.activation(gsb[:, ot * TS:(ot + 1) * TS],
                                                    p_f[ot][:], AF.Silu,
                                                    scale=gscsb[:, it:it + 1])
                            else:
                                last = s.activation(usb[:, ot * TS:(ot + 1) * TS],
                                                    p_f[ot][:], AF.Copy,
                                                    scale=uscsb[:, it:it + 1])
                        s_a.inc(last, 1)
                    mk[("n2", "gev", i, band, mat)] = s_a.v
                    mk[("n2", "gev_prev")] = s_a.v

                @blk.vector
                def _(v, i=i, band=band):
                    v.wait_ge(s_a.h, mk[("n2", "gev", i, band, 1)])
                    last = None
                    for ot in range(4):
                        it = band * 4 + ot
                        last = v.tensor_mul(hsb[:, it * TS:(it + 1) * TS],
                                            gsb[:, ot * TS:(ot + 1) * TS],
                                            usb[:, ot * TS:(ot + 1) * TS])
                    s_v.inc(last, 1)
                mk[("n2", "h", i, band)] = s_v.v
                mk[("n2", "hprev")] = s_v.v

                @blk.tensor
                def _(t, i=i, band=band):
                    t.wait_ge(s_v.h, mk[("n2", "h", i, band)])
                    last = None
                    for ot in range(4):
                        it = band * 4 + ot
                        last = t.matmul(bufs2["p_r2"][:], ones128[:],
                                        hsb[:, it * TS:(it + 1) * TS],
                                        start=(band == 0 and ot == 0),
                                        stop=(band == 15 and ot == 3))
                    s_mm.inc(last, 1)
                mk[("n2", "hsum", i, band)] = s_mm.v

            @blk.vector
            def _(v, i=i):
                v.wait_ge(s_mm.h, mk[("n2", "hsum", i, 15)])
                R3 = bufs2["rows"][:, 3 * TS:4 * TS]
                v.tensor_copy(trow[:, 2 * TS:3 * TS], bufs2["p_r2"][:])
                v.tensor_copy(R3, trow[:, 2 * TS:3 * TS])
                v.tensor_sub(R3, bufs2["p_r2"][:], R3)
                s_v.inc(v.tensor_copy(trow[:, 3 * TS:4 * TS], R3), 1)
            mk[("n2", "hsv", i)] = s_v.v

            for hq in range(4):
                for iq in range(4):
                    wseq = mk.get(("n2", "wseq_ctr"), 0)
                    mk[("n2", "wseq_ctr")] = wseq + 1
                    slot = wseq % 2
                    s_w = s_wtB if slot == 0 else s_e4A

                    @blk.gpsimd
                    def _(g, i=i, hq=hq, iq=iq, wseq=wseq, slot=slot, s_w=s_w):
                        if ("n2", "wmm_seq", wseq - 2) in mk:
                            g.wait_ge(s_mm.h, mk[("n2", "wmm_seq", wseq - 2)])
                        s_w.inc(g.dma_start(
                            out=wstr2[:, slot * 8192:(slot + 1) * 8192],
                            in_=io["wd_r"][:, (hq * 4 + iq) * 8192:
                                           (hq * 4 + iq + 1) * 8192]), 16)
                    mk[("n2", "dwld", i, hq, iq)] = s_w.v

                    @blk.tensor
                    def _(t, i=i, hq=hq, iq=iq, wseq=wseq, slot=slot, s_w=s_w):
                        t.wait_ge(s_w.h, mk[("n2", "dwld", i, hq, iq)])
                        if iq == 0:
                            t.wait_ge(s_v.h, mk[("n2", "hsv", i)])
                            if ("n2", "dev_prev") in mk:
                                t.wait_ge(s_v.h, mk[("n2", "dev_prev")])
                        last = None
                        for ot in range(4):
                            for j in range(16):
                                git = iq * 16 + j
                                last = t.matmul(
                                    p_f[ot][:],
                                    wstr2[:, slot * 8192 + j * 512 + ot * 128:
                                          slot * 8192 + j * 512 + (ot + 1) * 128],
                                    hsb[:, git * TS:(git + 1) * TS],
                                    start=(iq == 0 and j == 0), stop=False)
                            if iq == 3:
                                t.matmul(p_f[ot][:], m128row[:],
                                         trow[0:1, 2 * TS:3 * TS],
                                         start=False, stop=False)
                                last = t.matmul(p_f[ot][:], m128row[:],
                                                trow[0:1, 3 * TS:4 * TS],
                                                start=False, stop=True)
                        s_mm.inc(last, 1)
                        mk[("n2", "wmm_seq", wseq)] = s_mm.v
                        mk[("n2", "gmm_prev")] = s_mm.v
                    mk[("n2", "dmm", i, hq, iq)] = s_mm.v

                @blk.vector
                def _(v, i=i, hq=hq):
                    v.wait_ge(s_mm.h, mk[("n2", "dmm", i, hq, 3)])
                    if ("n2", "r1st_prev") in mk:
                        v.wait_ge(s_st.h, mk[("n2", "r1st_prev")])
                    last = None
                    for ot in range(4):
                        ho = hq * 4 + ot
                        last = v.scalar_tensor_tensor(
                            r132[:, ot * TS:(ot + 1) * TS], p_f[ot][:],
                            dscsb[:, ho:ho + 1],
                            bufs2["x32"][:, ho * TS:(ho + 1) * TS],
                            ALU.mult, ALU.add)
                    s_v.inc(last, 1)
                mk[("n2", "dev", i, hq)] = s_v.v
                mk[("n2", "dev_prev")] = s_v.v

                @blk.gpsimd
                def _(g, i=i, hq=hq):
                    g.wait_ge(s_v.h, mk[("n2", "dev", i, hq)])
                    for ot in range(4):
                        ho = hq * 4 + ot
                        s_st.inc(g.dma_start(
                            out=io["res1"][ho * 128:(ho + 1) * 128, TS * i:TS * (i + 1)],
                            in_=r132[:, ot * TS:(ot + 1) * TS]), 16)
                mk[("n2", "r1st", i, hq)] = s_st.v
                mk[("n2", "r1st_prev")] = s_st.v

    with contextlib.ExitStack() as st3:
        p3 = Ph(nc, blk, st3, "n3_")
        bufs3 = mk_ln_bufs(p3)
        bufs3.update(ones1=ones1, ones128=ones128)
        for i in range(NT):
            war = [(s_st, ("n2", "r1st", NT - 1, 3))] if i == 0 else \
                  [(s_st, ("n3", "xast", i - 1))]
            emit_ln_tile(blk, bufs3, sems, mk, "n3", i, io["res1"], war)

            @blk.gpsimd
            def _(g, i=i):
                g.wait_ge(s_v.h, mk[("n3", "ln", i)])
                for c in range(16):
                    s_st.inc(g.dma_start(
                        out=io["xa_own"][c * 128:(c + 1) * 128, TS * i:TS * (i + 1)],
                        in_=bufs3["xnbf"][:, c * TS:(c + 1) * TS]), 16)
            mk[("n3", "xast", i)] = s_st.v

    @blk.sync
    def _(sync):
        sync.wait_ge(s_st.h, mk[("n3", "xast", NT - 1)])

    return mk, sems


def declare_n_io(nc, dbg=False):
    io = {}
    io["x_own"] = nc.dram_tensor("x_own", [D, OWN], F32, kind="ExternalInput").ap()
    io["y0own"] = nc.dram_tensor("y0own", [D, OWN], F32, kind="ExternalInput").ap()
    io["y1own"] = nc.dram_tensor("y1own", [D, OWN], F32, kind="ExternalInput").ap()
    io["D_tile"] = nc.dram_tensor("D_tile", [128, 16], F32, kind="ExternalInput").ap()
    io["opr"] = nc.dram_tensor("opr", [128, 32768], F32, kind="ExternalInput").ap()
    io["wg_r"] = nc.dram_tensor("wg_r", [128, 131072], U8, kind="ExternalInput").ap()
    io["wu_r"] = nc.dram_tensor("wu_r", [128, 131072], U8, kind="ExternalInput").ap()
    io["wd_r"] = nc.dram_tensor("wd_r", [128, 131072], U8, kind="ExternalInput").ap()
    io["gsc"] = nc.dram_tensor("gsc", [128, 64], F32, kind="ExternalInput").ap()
    io["usc"] = nc.dram_tensor("usc", [128, 64], F32, kind="ExternalInput").ap()
    io["dsc"] = nc.dram_tensor("dsc", [128, 16], F32, kind="ExternalInput").ap()
    io["res1"] = nc.dram_tensor("res1", [D, OWN], F32, kind="ExternalOutput").ap()
    io["xa_own"] = nc.dram_tensor("xa_own", [D, OWN], BF16, kind="ExternalOutput").ap()
    kd = "ExternalOutput" if dbg else "Internal"
    io["resA_d"] = nc.dram_tensor("resA_d", [D, OWN], F32, kind=kd).ap()
    return io


def build_phase_n(dbg=False):
    nc = bass.Bass("TRN2", target_bir_lowering=False, debug=False)
    io = declare_n_io(nc, dbg)
    with contextlib.ExitStack() as st:
        with nc.Block() as blk:
            ph = Ph(nc, blk, st, "n_")
            io["_cpool"] = ph.sb("cpool", [128, len(CONSTS_F32)], F32)[:]
            emit_phase_n(ph, io, dbg)
    return nc


def prep_n_inputs(inputs, cid, y_pair):
    b, par = cid // 2, cid % 2
    idx = own_l_index(par)
    d = {}
    d["x_own"] = np.ascontiguousarray(inputs["L0_attn_x"][b].T[:, idx])
    d["y0own"] = np.ascontiguousarray(np.asarray(y_pair[0])[par * D:(par + 1) * D])
    d["y1own"] = np.ascontiguousarray(np.asarray(y_pair[1])[par * D:(par + 1) * D])
    d["D_tile"] = np.ascontiguousarray(inputs["L0_attn_D_param"].reshape(16, 128).T)
    d["opr"] = np.ascontiguousarray(
        inputs["L0_attn_out_proj_w"].T.reshape(16, 128, 2, 1024)
        .transpose(1, 2, 0, 3).reshape(128, 32768))
    d["wg_r"] = np.ascontiguousarray(
        byteplane(inputs["L0_ffn_gate_w"]).T.reshape(16, 128, 16, 512)
        .transpose(1, 2, 0, 3).reshape(128, 131072))
    d["wu_r"] = np.ascontiguousarray(
        byteplane(inputs["L0_ffn_up_w"]).T.reshape(16, 128, 16, 512)
        .transpose(1, 2, 0, 3).reshape(128, 131072))
    d["wd_r"] = np.ascontiguousarray(
        byteplane(inputs["L0_ffn_down_w"]).T.reshape(4, 16, 128, 4, 512)
        .transpose(2, 3, 0, 1, 4).reshape(128, 131072))
    d["gsc"] = np.ascontiguousarray(inputs["L0_ffn_gate_scale"].reshape(64, 128).T)
    d["usc"] = np.ascontiguousarray(inputs["L0_ffn_up_scale"].reshape(64, 128).T)
    d["dsc"] = np.ascontiguousarray(inputs["L0_ffn_down_scale"].reshape(16, 128).T)
    return d


SOFTMAX_SCALE = 0.08838834764831845
NEG_SQRT_HD = -11.313708498984761


def emit_phase_a(ph, io, dbg=False):
    nc, blk = ph.nc, ph.blk
    TS = 512
    NT = OWN // TS

    sems = {k: ph.sem(k) for k in ["ld", "gd", "pre", "wt", "st", "mm", "v", "a",
                                   "wtB", "e4A", "e4B"]}
    s_ld, s_gd, s_pre, s_wt, s_st, s_mm, s_v, s_a = (
        sems["ld"], sems["gd"], sems["pre"], sems["wt"], sems["st"],
        sems["mm"], sems["v"], sems["a"])
    s_wtB, s_e4A, s_e4B = sems["wtB"], sems["e4A"], sems["e4B"]
    mk = {}

    ones1 = ph.sb("ones1", [1, 128], BF16)
    ones128 = ph.sb("ones128", [128, 1], BF16)
    identbf = ph.sb("identbf", [128, 128], BF16)
    ksb = [ph.sb(f"ksb{k}", [128, OWN], BF16) for k in range(NKV)]
    vfm = [ph.sb(f"vfm{k}", [128, OWN], BF16) for k in range(NKV)]
    vtm = [ph.sb(f"vtm{k}", [128, OWN], BF16) for k in range(NKV)]
    xat = ph.sb("xat", [128, 16 * TS], BF16)
    wstr = ph.sb("wstr", [128, 16 * 1024], BF16)
    lora = ph.sb("lora", [128, 16 * 16 * 3], BF16)
    lorb = ph.sb("lorb", [16, 512 * 2], BF16)
    qbsb = ph.sb("qbsb", [16, D], BF16)
    qgsb = ph.sb("qgsb", [1, 16], F32)
    mnegsb = ph.sb("mneg", [128, 16], F32)
    cmaskbf = ph.sb("cmaskbf", [128, 8 * TS], BF16)
    coskq = ph.sb("coskq", [32, 2 * L], F32)
    coskk = ph.sb("coskk", [32, 2 * OWN], F32)
    rowq = ph.sb("rowq", [1, 4 * TS], F32)
    rowbfq = ph.sb("rowbfq", [1, TS], BF16)
    t16 = ph.sb("t16", [16, TS], BF16)
    qlt = ph.sb("qlt", [16, 8 * TS], BF16)
    ev32 = ph.sb("ev32", [128, TS], BF16)
    sqst = ph.sb("sqst", [128, TS], BF16)
    ropesc = ph.sb("ropesc", [32, 4 * TS], F32)
    qout = ph.sb("qout", [128, TS], BF16)
    qtile = ph.sb("qtile", [128, 2 * TS], BF16)
    est = ph.sb("est", [128, 3 * TS], BF16)
    denbf = ph.sb("denbf", [1, TS], BF16)
    pvbf = ph.sb("pvbf", [128, TS], BF16)

    p_kv = ph.ps("pkv", [128, TS])
    p_16 = ph.ps("p16", [16, TS])
    p_row = ph.ps("prow", [1, TS])
    p_rep = ph.ps("prep", [128, TS])
    p_tr = ph.ps("ptr", [128, 128], BF16)
    p_st = [ph.ps(f"pst{k}", [128, TS]) for k in range(2)]
    p_pv = p_rep
    p_den = p_row

    @blk.sync
    def _(sync):
        s_pre.inc(sync.dma_start(out=qgsb[:], in_=io["qg"]), 16)
        s_pre.inc(sync.dma_start(out=coskq[:, 0:L], in_=io["cosq"]), 16)
        s_pre.inc(sync.dma_start(out=coskq[:, L:2 * L], in_=io["sinq"]), 16)
        s_pre.inc(sync.dma_start(out=coskk[:, 0:OWN], in_=io["cosk"]), 16)
        s_pre.inc(sync.dma_start(out=coskk[:, OWN:2 * OWN], in_=io["sink"]), 16)

    @blk.gpsimd
    def _(g):
        if "_cpool" in io:
            for ci, val in enumerate(CONSTS_F32):
                g.memset(io["_cpool"][:, ci:ci + 1], val)
                nc.const_aps.register(F32, val, io["_cpool"][:, ci:ci + 1])
        g.memset(ones1[:], 1.0)
        g.memset(ones128[:], 1.0)
        s_pre.inc(g.dma_start(out=identbf[:], in_=io["ident"]), 16)
        for c in range(16):
            s_pre.inc(g.dma_start(out=lora[:, c * 16:(c + 1) * 16],
                                  in_=io["ka"][c * 128:(c + 1) * 128, :]), 16)
            s_pre.inc(g.dma_start(out=lora[:, (16 + c) * 16:(17 + c) * 16],
                                  in_=io["va"][c * 128:(c + 1) * 128, :]), 16)
            s_pre.inc(g.dma_start(out=lora[:, (32 + c) * 16:(33 + c) * 16],
                                  in_=io["qa"][c * 128:(c + 1) * 128, :]), 16)
        s_pre.inc(g.dma_start(out=lorb[:, 0:512], in_=io["kb"]), 16)
        s_pre.inc(g.dma_start(out=lorb[:, 512:1024], in_=io["vb"]), 16)
        s_pre.inc(g.dma_start(out=qbsb[:], in_=io["qb"]), 16)
        for m in range(8):
            s_pre.inc(g.dma_start(out=cmaskbf[:, m * TS:(m + 1) * TS],
                                  in_=io["cmask"][m * 128:(m + 1) * 128, :]), 16)
    mk["pre"] = s_pre.v

    @blk.vector
    def _(v):
        v.wait_ge(s_pre.h, mk["pre"])
        v.tensor_single_scalar(rowq[0:1, 0:16], qgsb[:], -1.0, ALU.mult)
        v.tensor_max(rowq[0:1, 0:16], rowq[0:1, 0:16], qgsb[:])
        s_v.inc(v.tensor_single_scalar(rowq[0:1, 0:16], rowq[0:1, 0:16],
                                       NEG_SQRT_HD, ALU.mult), 1)
    mk["mnegv"] = s_v.v

    @blk.gpsimd
    def _(g):
        g.wait_ge(s_v.h, mk["mnegv"])
        s_gd.inc(g.dma_start(out=io["mneg_d"], in_=rowq[0:1, 0:16]), 16)
        g.wait_ge(s_gd.h, s_gd.v)
        s_pre.inc(g.dma_start(out=mnegsb[:],
                              in_=io["mneg_d"].partition_broadcast(128)), 16)
    mk["mnegbc"] = s_pre.v

    def rope_apply(v, dst, cof, sif):
        x1 = dst[0:32, :]
        x2 = dst[32:64, :]
        v.tensor_copy(ropesc[:, 0:TS], x2)
        v.tensor_mul(ropesc[:, TS:2 * TS], x1, cof)
        v.tensor_mul(ropesc[:, 2 * TS:3 * TS], ropesc[:, 0:TS], sif)
        v.tensor_mul(ropesc[:, 3 * TS:4 * TS], ropesc[:, 0:TS], cof)
        v.tensor_mul(ropesc[:, 0:TS], x1, sif)
        v.tensor_add(x1, ropesc[:, TS:2 * TS], ropesc[:, 2 * TS:3 * TS])
        v.tensor_sub(ropesc[:, TS:2 * TS], ropesc[:, 3 * TS:4 * TS],
                     ropesc[:, 0:TS])
        return v.tensor_copy(x2, ropesc[0:32, TS:2 * TS])

    @blk.gpsimd
    def _(g):
        g.wait_ge(s_pre.h, mk["pre"])
        for c in range(16):
            s_wt.inc(g.dma_start(out=wstr[:, c * 1024:c * 1024 + 512],
                                 in_=io["wk"][c * 128:(c + 1) * 128, :]), 16)
            s_wt.inc(g.dma_start(out=wstr[:, c * 1024 + 512:(c + 1) * 1024],
                                 in_=io["wv"][c * 128:(c + 1) * 128, :]), 16)
    mk[("a1", "wld")] = s_wt.v

    for ti in range(NT):
        @blk.gpsimd
        def _(g, ti=ti):
            if ti > 0:
                g.wait_ge(s_mm.h, mk[("a1", "xaused", ti - 1)])
            for c in range(16):
                s_gd.inc(g.dma_start(
                    out=xat[:, c * TS:(c + 1) * TS],
                    in_=io["xa_own"][c * 128:(c + 1) * 128, ti * TS:(ti + 1) * TS]), 16)
        mk[("a1", "xald", ti)] = s_gd.v

        for mat in range(2):
            @blk.tensor
            def _(t, ti=ti, mat=mat):
                if ti == 0 and mat == 0:
                    t.wait_ge(s_wt.h, mk[("a1", "wld")])
                t.wait_ge(s_gd.h, mk[("a1", "xald", ti)])
                if ("a1", "t16ev") in mk:
                    t.wait_ge(s_v.h, mk[("a1", "t16ev")])
                last = None
                for c in range(16):
                    last = t.matmul(p_16[:],
                                    lora[:, (mat * 16 + c) * 16:(mat * 16 + c + 1) * 16],
                                    xat[:, c * TS:(c + 1) * TS],
                                    start=(c == 0), stop=(c == 15))
                s_mm.inc(last, 1)
            mk[("a1", "t16mm", ti, mat)] = s_mm.v

            @blk.vector
            def _(v, ti=ti, mat=mat):
                v.wait_ge(s_mm.h, mk[("a1", "t16mm", ti, mat)])
                s_v.inc(v.tensor_copy(t16[:], p_16[:]), 1)
            mk[("a1", "t16ev")] = s_v.v
            mk[("a1", "t16", ti, mat)] = s_v.v

            for kvh in range(NKV):
                @blk.tensor
                def _(t, ti=ti, mat=mat, kvh=kvh):
                    if ("a1", "evprev") in mk:
                        t.wait_ge(s_a.h, mk[("a1", "evprev")])
                    for c in range(16):
                        t.matmul(p_kv[:],
                                 wstr[:, c * 1024 + mat * 512 + kvh * 128:
                                      c * 1024 + mat * 512 + (kvh + 1) * 128],
                                 xat[:, c * TS:(c + 1) * TS],
                                 start=(c == 0), stop=False)
                    t.wait_ge(s_v.h, mk[("a1", "t16", ti, mat)])
                    last = t.matmul(p_kv[:],
                                    lorb[:, mat * 512 + kvh * 128:
                                         mat * 512 + (kvh + 1) * 128],
                                    t16[:], start=False, stop=True)
                    s_mm.inc(last, 1)
                mk[("a1", "mm", ti, mat, kvh)] = s_mm.v
                if kvh == NKV - 1 and mat == 1:
                    mk[("a1", "xaused", ti)] = s_mm.v

                if mat == 0:
                    @blk.scalar
                    def _(s, ti=ti, kvh=kvh):
                        s.wait_ge(s_mm.h, mk[("a1", "mm", ti, 0, kvh)])
                        if ("sqused",) in mk:
                            s.wait_ge(s_mm.h, mk[("sqused",)])
                        if ("ev32used",) in mk:
                            s.wait_ge(s_v.h, mk[("ev32used",)])
                        s.activation(sqst[:], p_kv[:], AF.Square)
                        s_a.inc(s.copy(ev32[:], p_kv[:]), 1)
                    mk[("a1", "kev", ti, kvh)] = s_a.v
                    mk[("a1", "evprev")] = s_a.v

                    @blk.tensor
                    def _(t, ti=ti, kvh=kvh):
                        t.wait_ge(s_a.h, mk[("a1", "kev", ti, kvh)])
                        last = t.matmul(p_row[:], ones128[:], sqst[:],
                                        start=True, stop=True)
                        s_mm.inc(last, 1)
                    mk[("a1", "ksqs", ti, kvh)] = s_mm.v
                    mk[("sqused",)] = s_mm.v

                    @blk.scalar
                    def _(s, ti=ti, kvh=kvh):
                        s.wait_ge(s_mm.h, mk[("a1", "ksqs", ti, kvh)])
                        if ("rbfprev",) in mk:
                            s.wait_ge(s_v.h, mk[("rbfprev",)])
                        s_a.inc(s.activation(rowq[0:1, TS:2 * TS], p_row[:], AF.Ln,
                                             scale=1.0 / 128.0, bias=F32EPS), 1)
                        s.wait_ge(s_a.h, s_a.v)
                        s_a.inc(s.activation(rowq[0:1, TS:2 * TS],
                                             rowq[0:1, TS:2 * TS],
                                             AF.Exp, scale=-0.5), 1)
                    mk[("a1", "krst", ti, kvh)] = s_a.v

                    @blk.vector
                    def _(v, ti=ti, kvh=kvh):
                        v.wait_ge(s_a.h, mk[("a1", "krst", ti, kvh)])
                        if ("repprev",) in mk:
                            v.wait_ge(s_mm.h, mk[("repprev",)])
                        s_v.inc(v.tensor_copy(rowbfq[:], rowq[0:1, TS:2 * TS]), 1)
                    mk[("a1", "krbf", ti, kvh)] = s_v.v
                    mk[("rbfprev",)] = s_v.v

                    @blk.tensor
                    def _(t, ti=ti, kvh=kvh):
                        t.wait_ge(s_v.h, mk[("a1", "krbf", ti, kvh)])
                        last = t.matmul(p_rep[:], ones1[:], rowbfq[:],
                                        start=True, stop=True)
                        s_mm.inc(last, 1)
                    mk[("a1", "krep", ti, kvh)] = s_mm.v
                    mk[("repprev",)] = s_mm.v

                    @blk.vector
                    def _(v, ti=ti, kvh=kvh):
                        v.wait_ge(s_mm.h, mk[("a1", "krep", ti, kvh)])
                        dst = ksb[kvh][:, ti * TS:(ti + 1) * TS]
                        v.tensor_mul(dst, ev32[:], p_rep[:])
                        last = rope_apply(v, dst,
                                          coskk[:, ti * TS:(ti + 1) * TS],
                                          coskk[:, OWN + ti * TS:OWN + (ti + 1) * TS])
                        s_v.inc(last, 1)
                    mk[("a1", "kdone", ti, kvh)] = s_v.v
                    mk[("ev32used",)] = s_v.v
                else:
                    @blk.scalar
                    def _(s, ti=ti, kvh=kvh):
                        s.wait_ge(s_mm.h, mk[("a1", "mm", ti, 1, kvh)])
                        last = s.copy(vfm[kvh][:, ti * TS:(ti + 1) * TS], p_kv[:])
                        s_a.inc(last, 1)
                    mk[("a1", "vev", ti, kvh)] = s_a.v
                    mk[("a1", "evprev")] = s_a.v

                    @blk.gpsimd
                    def _(g, ti=ti, kvh=kvh):
                        g.wait_ge(s_a.h, mk[("a1", "vev", ti, kvh)])
                        s_st.inc(g.dma_start(
                            out=io["vfm_d"][kvh * 128:(kvh + 1) * 128,
                                            ti * TS:(ti + 1) * TS],
                            in_=vfm[kvh][:, ti * TS:(ti + 1) * TS]), 16)
                    mk[("a1", "vst", ti, kvh)] = s_st.v

                    for jb in range(4):
                        @blk.tensor
                        def _(t, ti=ti, kvh=kvh, jb=jb):
                            t.wait_ge(s_a.h, mk[("a1", "vev", ti, kvh)])
                            if ("a1", "trev") in mk:
                                t.wait_ge(s_v.h, mk[("a1", "trev")])
                            last = t.transpose(
                                p_tr[:],
                                vfm[kvh][:, ti * TS + jb * 128:
                                         ti * TS + (jb + 1) * 128],
                                identbf[:])
                            s_mm.inc(last, 1)
                        mk[("a1", "tr", ti, kvh, jb)] = s_mm.v

                        @blk.vector
                        def _(v, ti=ti, kvh=kvh, jb=jb):
                            v.wait_ge(s_mm.h, mk[("a1", "tr", ti, kvh, jb)])
                            last = v.tensor_copy(
                                vtm[kvh][:, ti * TS + jb * 128:
                                         ti * TS + (jb + 1) * 128],
                                p_tr[:])
                            s_v.inc(last, 1)
                        mk[("a1", "trev")] = s_v.v

    for hh in range(2):
        @blk.gpsimd
        def _(g, hh=hh):
            if hh == 0:
                g.wait_ge(s_mm.h, mk[("a1", "xaused", NT - 1)])
            else:
                g.wait_ge(s_mm.h, mk[("a2", "wqused", 0)])
            for c in range(16):
                s_wt.inc(g.dma_start(
                    out=wstr[:, c * 1024:(c + 1) * 1024],
                    in_=io["wq"][c * 128:(c + 1) * 128,
                                 hh * 1024:(hh + 1) * 1024]), 16)
        mk[("a2", "wld", hh)] = s_wt.v

        for r in range(2):
            for ti in range(NT):
                @blk.gpsimd
                def _(g, hh=hh, r=r, ti=ti):
                    if not (hh == 0 and r == 0 and ti == 0):
                        g.wait_ge(s_mm.h, mk[("a2", "xaused_prev")])
                    elif ("a1", "xaused", NT - 1) in mk:
                        g.wait_ge(s_mm.h, mk[("a1", "xaused", NT - 1)])
                    for c in range(16):
                        s_gd.inc(g.dma_start(
                            out=xat[:, c * TS:(c + 1) * TS],
                            in_=io["xa_all"][r * D + c * 128:r * D + (c + 1) * 128,
                                             ti * TS:(ti + 1) * TS]), 16)
                mk[("a2", "xald", hh, r, ti)] = s_gd.v

                if hh == 0:
                    @blk.tensor
                    def _(t, r=r, ti=ti):
                        t.wait_ge(s_gd.h, mk[("a2", "xald", 0, r, ti)])
                        if ("a2", "qltev") in mk:
                            t.wait_ge(s_v.h, mk[("a2", "qltev")])
                        last = None
                        for c in range(16):
                            last = t.matmul(p_16[:],
                                            lora[:, (32 + c) * 16:(33 + c) * 16],
                                            xat[:, c * TS:(c + 1) * TS],
                                            start=(c == 0), stop=(c == 15))
                        s_mm.inc(last, 1)
                    mk[("a2", "qlmm", r, ti)] = s_mm.v

                    @blk.vector
                    def _(v, r=r, ti=ti):
                        v.wait_ge(s_mm.h, mk[("a2", "qlmm", r, ti)])
                        s_v.inc(v.tensor_copy(
                            qlt[:, (r * NT + ti) * TS:(r * NT + ti + 1) * TS],
                            p_16[:]), 1)
                    mk[("a2", "qltev")] = s_v.v
                    mk[("a2", "qlt", r, ti)] = s_v.v

                for h4 in range(8):
                    h = hh * 8 + h4

                    @blk.tensor
                    def _(t, hh=hh, r=r, ti=ti, h4=h4, h=h):
                        if h4 == 0:
                            t.wait_ge(s_wt.h, mk[("a2", "wld", hh)])
                        if ("a2", "qevprev") in mk:
                            t.wait_ge(s_a.h, mk[("a2", "qevprev")])
                        if h4 == 0:
                            t.wait_ge(s_v.h, mk[("a2", "qlt", r, ti)])
                        for c in range(16):
                            t.matmul(p_kv[:],
                                     wstr[:, c * 1024 + h4 * 128:
                                          c * 1024 + (h4 + 1) * 128],
                                     xat[:, c * TS:(c + 1) * TS],
                                     start=(c == 0), stop=False)
                        last = t.matmul(p_kv[:],
                                        qbsb[:, h * 128:(h + 1) * 128],
                                        qlt[:, (r * NT + ti) * TS:
                                            (r * NT + ti + 1) * TS],
                                        start=False, stop=True)
                        s_mm.inc(last, 1)
                        if h4 == 7:
                            mk[("a2", "wqused", hh)] = s_mm.v
                            mk[("a2", "xaused_prev")] = s_mm.v
                    mk[("a2", "qmm", hh, r, ti, h4)] = s_mm.v

                    @blk.scalar
                    def _(s, hh=hh, r=r, ti=ti, h4=h4):
                        s.wait_ge(s_mm.h, mk[("a2", "qmm", hh, r, ti, h4)])
                        if ("sqused",) in mk:
                            s.wait_ge(s_mm.h, mk[("sqused",)])
                        if ("ev32used",) in mk:
                            s.wait_ge(s_v.h, mk[("ev32used",)])
                        s.activation(sqst[:], p_kv[:], AF.Square)
                        s_a.inc(s.copy(ev32[:], p_kv[:]), 1)
                        mk[("a2", "qevprev")] = s_a.v
                    mk[("a2", "qev", hh, r, ti, h4)] = s_a.v

                    @blk.tensor
                    def _(t, hh=hh, r=r, ti=ti, h4=h4):
                        t.wait_ge(s_a.h, mk[("a2", "qev", hh, r, ti, h4)])
                        last = t.matmul(p_row[:], ones128[:], sqst[:],
                                        start=True, stop=True)
                        s_mm.inc(last, 1)
                        mk[("sqused",)] = s_mm.v
                    mk[("a2", "qsqs", hh, r, ti, h4)] = s_mm.v

                    @blk.scalar
                    def _(s, hh=hh, r=r, ti=ti, h4=h4):
                        s.wait_ge(s_mm.h, mk[("a2", "qsqs", hh, r, ti, h4)])
                        if ("rbfprev",) in mk:
                            s.wait_ge(s_v.h, mk[("rbfprev",)])
                        s_a.inc(s.activation(rowq[0:1, TS:2 * TS], p_row[:], AF.Ln,
                                             scale=1.0 / 128.0, bias=F32EPS), 1)
                        s.wait_ge(s_a.h, s_a.v)
                        s_a.inc(s.activation(rowq[0:1, TS:2 * TS],
                                             rowq[0:1, TS:2 * TS],
                                             AF.Exp, scale=-0.5), 1)
                    mk[("a2", "qrst", hh, r, ti, h4)] = s_a.v

                    @blk.vector
                    def _(v, hh=hh, r=r, ti=ti, h4=h4, h=h):
                        v.wait_ge(s_a.h, mk[("a2", "qrst", hh, r, ti, h4)])
                        if ("repprev",) in mk:
                            v.wait_ge(s_mm.h, mk[("repprev",)])
                        v.tensor_single_scalar(rowq[0:1, TS:2 * TS],
                                               rowq[0:1, TS:2 * TS],
                                               qgsb[0:1, h:h + 1], ALU.mult)
                        s_v.inc(v.tensor_copy(rowbfq[:], rowq[0:1, TS:2 * TS]), 1)
                    mk[("a2", "qrbf", hh, r, ti, h4)] = s_v.v
                    mk[("rbfprev",)] = s_v.v

                    @blk.tensor
                    def _(t, hh=hh, r=r, ti=ti, h4=h4):
                        t.wait_ge(s_v.h, mk[("a2", "qrbf", hh, r, ti, h4)])
                        last = t.matmul(p_rep[:], ones1[:], rowbfq[:],
                                        start=True, stop=True)
                        s_mm.inc(last, 1)
                    mk[("a2", "qrep", hh, r, ti, h4)] = s_mm.v
                    mk[("repprev",)] = s_mm.v

                    @blk.vector
                    def _(v, hh=hh, r=r, ti=ti, h4=h4):
                        v.wait_ge(s_mm.h, mk[("a2", "qrep", hh, r, ti, h4)])
                        if ("a2", "qoutst") in mk:
                            v.wait_ge(s_st.h, mk[("a2", "qoutst")])
                        v.tensor_mul(qout[:], ev32[:], p_rep[:])
                        last = rope_apply(
                            v, qout[:],
                            coskq[:, r * OWN + ti * TS:r * OWN + (ti + 1) * TS],
                            coskq[:, L + r * OWN + ti * TS:L + r * OWN + (ti + 1) * TS])
                        s_v.inc(last, 1)
                    mk[("a2", "qrope", hh, r, ti, h4)] = s_v.v
                    mk[("ev32used",)] = s_v.v

                    @blk.gpsimd
                    def _(g, hh=hh, r=r, ti=ti, h4=h4, h=h):
                        g.wait_ge(s_v.h, mk[("a2", "qrope", hh, r, ti, h4)])
                        s_st.inc(g.dma_start(
                            out=io["qd"][h * 128:(h + 1) * 128,
                                         r * OWN + ti * TS:r * OWN + (ti + 1) * TS],
                            in_=qout[:]), 16)
                        mk[("a2", "qoutst")] = s_st.v
                    mk[("a2", "qst", hh, r, ti, h4)] = s_st.v

    gq = 0
    for h in range(NH):
        kvh = h // 4
        for r in range(2):
            for jt in range(NT):
                nblk = 4 * jt + 4
                slot = gq % 2

                @blk.gpsimd
                def _(g, h=h, r=r, jt=jt, slot=slot, gq=gq):
                    if gq == 0:
                        g.wait_ge(s_st.h, mk[("a2", "qst", 1, 1, NT - 1, 7)])
                    if gq >= 2:
                        g.wait_ge(s_mm.h, mk[("a3", "qused", gq - 2)])
                    s_gd.inc(g.dma_start(
                        out=qtile[:, slot * TS:(slot + 1) * TS],
                        in_=io["qd"][h * 128:(h + 1) * 128,
                                     r * OWN + jt * TS:r * OWN + (jt + 1) * TS]), 16)
                mk[("a3", "qld", gq)] = s_gd.v

                for i in range(nblk):
                    @blk.tensor
                    def _(t, h=h, r=r, jt=jt, i=i, slot=slot, gq=gq, kvh=kvh,
                          nblk=nblk):
                        if i == 0:
                            t.wait_ge(s_gd.h, mk[("a3", "qld", gq)])
                            if gq == 0:
                                t.wait_ge(s_pre.h, mk["mnegbc"])
                                t.wait_ge(s_v.h, mk[("a1", "kdone", NT - 1, NKV - 1)])
                                t.wait_ge(s_v.h, mk[("a1", "trev")])
                        if ("a3", "ts", gq, i - 2) in mk:
                            t.wait_ge(s_v.h, mk[("a3", "ts", gq, i - 2)])
                        if i < 2 and gq > 0:
                            t.wait_ge(s_v.h, mk[("a3", "lastts", gq - 1)])
                        last = t.matmul(p_st[i % 2][:],
                                        ksb[kvh][:, i * 128:(i + 1) * 128],
                                        qtile[:, slot * TS:(slot + 1) * TS],
                                        start=True, stop=True)
                        s_mm.inc(last, 1)
                        if i == nblk - 1:
                            mk[("a3", "qused", gq)] = s_mm.v
                    mk[("a3", "st", gq, i)] = s_mm.v

                    @blk.vector
                    def _(v, h=h, i=i, gq=gq, nblk=nblk):
                        v.wait_ge(s_mm.h, mk[("a3", "st", gq, i)])
                        if ("a3", "dpv", gq, i - 2) in mk:
                            v.wait_ge(s_mm.h, mk[("a3", "dpv", gq, i - 2)])
                        elif i < 2 and gq > 0:
                            v.wait_ge(s_mm.h, mk[("a3", "lastdpv", gq - 1)])
                        if ("a3", "exp", gq, i - 2) in mk:
                            v.wait_ge(s_a.h, mk[("a3", "exp", gq, i - 2)])
                        elif i < 2 and gq > 0:
                            v.wait_ge(s_a.h, mk[("a3", "lastexp", gq - 1)])
                        last = v.tensor_scalar(
                            est[:, (i % 2) * TS:(i % 2 + 1) * TS],
                            p_st[i % 2][:], SOFTMAX_SCALE,
                            mnegsb[:, h:h + 1], ALU.mult, ALU.add)
                        s_v.inc(last, 1)
                    mk[("a3", "ts", gq, i)] = s_v.v
                    if i == nblk - 1:
                        mk[("a3", "lastts", gq)] = s_v.v

                    @blk.scalar
                    def _(s, i=i, gq=gq, nblk=nblk):
                        s.wait_ge(s_v.h, mk[("a3", "ts", gq, i)])
                        last = s.activation(est[:, (i % 2) * TS:(i % 2 + 1) * TS],
                                            est[:, (i % 2) * TS:(i % 2 + 1) * TS],
                                            AF.Exp)
                        s_a.inc(last, 1)
                        if i == nblk - 1:
                            mk[("a3", "lastexp", gq)] = s_a.v
                    mk[("a3", "exp", gq, i)] = s_a.v

                    e = i - 4 * jt
                    if e >= 0:
                        @blk.vector
                        def _(v, r=r, i=i, e=e, gq=gq):
                            v.wait_ge(s_a.h, mk[("a3", "exp", gq, i)])
                            if ("a3", "mask", gq, i - 1) in mk:
                                v.wait_ge(s_mm.h, mk[("a3", "dpv", gq, i - 1)])
                            elif gq > 0:
                                v.wait_ge(s_mm.h, mk[("a3", "lastdpv", gq - 1)])
                            last = v.tensor_mul(est[:, 2 * TS:3 * TS],
                                                est[:, (i % 2) * TS:(i % 2 + 1) * TS],
                                                cmaskbf[:, (r * 4 + e) * TS:
                                                        (r * 4 + e + 1) * TS])
                            s_v.inc(last, 1)
                        mk[("a3", "mask", gq, i)] = s_v.v

                    @blk.tensor
                    def _(t, i=i, e=e, gq=gq, kvh=kvh, nblk=nblk):
                        if e >= 0:
                            t.wait_ge(s_v.h, mk[("a3", "mask", gq, i)])
                            src = est[:, 2 * TS:3 * TS]
                        else:
                            t.wait_ge(s_a.h, mk[("a3", "exp", gq, i)])
                            src = est[:, (i % 2) * TS:(i % 2 + 1) * TS]
                        if i == 0:
                            if ("a3", "pvev", gq - 1) in mk:
                                t.wait_ge(s_v.h, mk[("a3", "pvev", gq - 1)])
                                t.wait_ge(s_v.h, mk[("a3", "denev", gq - 1)])
                        t.matmul(p_den[:], ones128[:], src,
                                 start=(i == 0), stop=(i == nblk - 1))
                        last = t.matmul(p_pv[:],
                                        vtm[kvh][:, i * 128:(i + 1) * 128],
                                        src, start=(i == 0), stop=(i == nblk - 1))
                        s_mm.inc(last, 1)
                        if e >= 0:
                            mk[("a3", "mused", gq, i)] = s_mm.v
                        if i == nblk - 1:
                            mk[("a3", "lastdpv", gq)] = s_mm.v
                    mk[("a3", "dpv", gq, i)] = s_mm.v

                @blk.vector
                def _(v, gq=gq):
                    v.wait_ge(s_mm.h, mk[("a3", "lastdpv", gq)])
                    if ("a3", "denst", gq - 1) in mk:
                        v.wait_ge(s_st.h, mk[("a3", "denst", gq - 1)])
                    s_v.inc(v.tensor_copy(denbf[:], p_den[:]), 1)
                mk[("a3", "denev", gq)] = s_v.v

                @blk.vector
                def _(v, gq=gq):
                    v.wait_ge(s_mm.h, mk[("a3", "lastdpv", gq)])
                    if ("a3", "denst", gq - 1) in mk:
                        v.wait_ge(s_st.h, mk[("a3", "denst", gq - 1)])
                    s_v.inc(v.tensor_copy(pvbf[:], p_pv[:]), 1)
                mk[("a3", "pvev", gq)] = s_v.v

                @blk.gpsimd
                def _(g, h=h, r=r, jt=jt, gq=gq):
                    g.wait_ge(s_v.h, mk[("a3", "denev", gq)])
                    g.wait_ge(s_v.h, mk[("a3", "pvev", gq)])
                    s_st.inc(g.dma_start(
                        out=io["pv_d"][r * 2064 + h * 128:r * 2064 + (h + 1) * 128,
                                       jt * TS:(jt + 1) * TS],
                        in_=pvbf[:]), 16)
                    s_st.inc(g.dma_start(
                        out=io["pv_d"][r * 2064 + 2048 + h:r * 2064 + 2048 + h + 1,
                                       jt * TS:(jt + 1) * TS],
                        in_=denbf[:]), 16)
                mk[("a3", "denst", gq)] = s_st.v
                gq += 1

    @blk.sync
    def _(sync):
        sync.wait_ge(s_st.h, mk[("a3", "denst", gq - 1)])
        sync.wait_ge(s_st.h, mk[("a1", "vst", NT - 1, NKV - 1)])

    return mk, sems


def declare_a_io(nc, dbg=False):
    io = {}
    io["xa_own"] = nc.dram_tensor("xa_own", [D, OWN], BF16, kind="ExternalInput").ap()
    io["xa_all"] = nc.dram_tensor("xa_all", [2 * D, OWN], BF16, kind="ExternalInput").ap()
    io["wq"] = nc.dram_tensor("wq", [D, D], F32, kind="ExternalInput").ap()
    io["wk"] = nc.dram_tensor("wk", [D, KVD], F32, kind="ExternalInput").ap()
    io["wv"] = nc.dram_tensor("wv", [D, KVD], F32, kind="ExternalInput").ap()
    io["qa"] = nc.dram_tensor("qa", [D, 16], F32, kind="ExternalInput").ap()
    io["qb"] = nc.dram_tensor("qb", [16, D], F32, kind="ExternalInput").ap()
    io["ka"] = nc.dram_tensor("ka", [D, 16], F32, kind="ExternalInput").ap()
    io["kb"] = nc.dram_tensor("kb", [16, KVD], F32, kind="ExternalInput").ap()
    io["va"] = nc.dram_tensor("va", [D, 16], F32, kind="ExternalInput").ap()
    io["vb"] = nc.dram_tensor("vb", [16, KVD], F32, kind="ExternalInput").ap()
    io["qg"] = nc.dram_tensor("qg", [1, 16], F32, kind="ExternalInput").ap()
    io["cosq"] = nc.dram_tensor("cosq", [32, L], F32, kind="ExternalInput").ap()
    io["sinq"] = nc.dram_tensor("sinq", [32, L], F32, kind="ExternalInput").ap()
    io["cosk"] = nc.dram_tensor("cosk", [32, OWN], F32, kind="ExternalInput").ap()
    io["sink"] = nc.dram_tensor("sink", [32, OWN], F32, kind="ExternalInput").ap()
    io["cmask"] = nc.dram_tensor("cmask", [8 * 128, 512], F32, kind="ExternalInput").ap()
    io["ident"] = nc.dram_tensor("ident", [128, 128], F32, kind="ExternalInput").ap()
    io["pv_d"] = nc.dram_tensor("pv_d", [2 * 2064, OWN], BF16, kind="ExternalOutput").ap()
    io["vfm_d"] = nc.dram_tensor("vfm_d", [KVD, OWN], BF16, kind="ExternalOutput").ap()
    kd = "ExternalOutput" if dbg else "Internal"
    io["qd"] = nc.dram_tensor("qd", [D, 2 * OWN], BF16, kind=kd).ap()
    io["mneg_d"] = nc.dram_tensor("mneg_d", [1, 16], F32, kind="Internal").ap()
    return io


def build_phase_a(dbg=False):
    nc = bass.Bass("TRN2", target_bir_lowering=False, debug=False)
    io = declare_a_io(nc, dbg)
    with contextlib.ExitStack() as st:
        with nc.Block() as blk:
            ph = Ph(nc, blk, st, "a_")
            io["_cpool"] = ph.sb("cpool", [128, len(CONSTS_F32)], F32)[:]
            emit_phase_a(ph, io, dbg)
    return nc


def prep_a_inputs(inputs, cid, xa_pair):
    b, par = cid // 2, cid % 2
    d = {}
    d["xa_own"] = np.ascontiguousarray(np.asarray(xa_pair[par]))
    d["xa_all"] = np.ascontiguousarray(
        np.concatenate([np.asarray(xa_pair[0]), np.asarray(xa_pair[1])], axis=0))
    d["wq"] = inputs["L1_attn_Wq"]
    d["wk"] = inputs["L1_attn_Wk"]
    d["wv"] = inputs["L1_attn_Wv"]
    d["qa"] = inputs["L1_attn_q_A"]
    d["qb"] = inputs["L1_attn_q_B"]
    d["ka"] = inputs["L1_attn_k_A"]
    d["kb"] = inputs["L1_attn_k_B"]
    d["va"] = inputs["L1_attn_v_A"]
    d["vb"] = inputs["L1_attn_v_B"]
    d["qg"] = np.ascontiguousarray(inputs["L1_attn_q_gain"].reshape(1, 16))
    inv = 1.0 / 10000.0 ** (np.arange(0, 64, 2, dtype=np.float64) / 64.0)
    lq = np.concatenate([own_l_index(0), own_l_index(1)])
    fq = lq[None, :].astype(np.float64) * inv[:, None]
    d["cosq"] = np.cos(fq).astype(np.float32)
    d["sinq"] = np.sin(fq).astype(np.float32)
    lk = own_l_index(par)
    fk = lk[None, :].astype(np.float64) * inv[:, None]
    d["cosk"] = np.cos(fk).astype(np.float32)
    d["sink"] = np.sin(fk).astype(np.float32)
    cm = np.zeros((2, 4, 128, 512), np.float32)
    kk = np.arange(128)[:, None]
    qq = np.arange(128)[None, :]
    for r in range(2):
        for e in range(4):
            for m in range(4):
                a = 2 * e + par
                bq = 2 * m + r
                if a < bq:
                    cm[r, e, :, m * 128:(m + 1) * 128] = 1.0
                elif a == bq:
                    cm[r, e, :, m * 128:(m + 1) * 128] = (kk <= qq)
    d["cmask"] = cm.reshape(8 * 128, 512)
    d["ident"] = np.eye(128, dtype=np.float32)
    return d



def emit_phase_am(ph, io, dbg=False):
    nc, blk = ph.nc, ph.blk
    TS = 512
    NT = OWN // TS

    sems = {k: ph.sem(k) for k in ["ld", "gd", "pre", "wt", "st", "mm", "v", "a",
                                   "wtB", "e4A", "e4B"]}
    s_ld, s_gd, s_pre, s_wt, s_st, s_mm, s_v, s_a = (
        sems["ld"], sems["gd"], sems["pre"], sems["wt"], sems["st"],
        sems["mm"], sems["v"], sems["a"])
    s_wtB, s_e4A, s_e4B = sems["wtB"], sems["e4A"], sems["e4B"]
    mk = {}

    ones1 = ph.sb("ones1", [1, 128], BF16)
    ones128 = ph.sb("ones128", [128, 1], BF16)
    denh = ph.sb("denh", [1, 16 * TS], F32)
    recbf = ph.sb("recbf", [1, 16 * TS], BF16)
    denstg = ph.sb("denstg", [1, 2 * TS], BF16)
    pvstg = ph.sb("pvstg", [128, 2 * TS], BF16)
    pvsum = ph.sb("pvsum", [128, TS], BF16)
    vft = ph.sb("vft", [128, 4 * TS], BF16)
    vsqr = ph.sb("vsqr", [1, 4 * TS], F32)
    sqstg = ph.sb("sqstg", [128, TS], BF16)
    tpstg = ph.sb("tpstg", [128, TS], BF16)
    srowbf = ph.sb("srowbf", [1, TS], BF16)
    srow32 = ph.sb("srow32", [1, TS], F32)
    ypbuf = ph.sb("ypbuf", [128, 16 * TS], BF16)
    wstr = ph.sb("wstr", [128, 2 * 16 * 1024], BF16)
    r232 = ph.sb("r232", [128, 8 * TS], F32)
    res1t = ph.sb("res1t", [128, 16 * TS], F32)
    pbsb = ph.sb("pbsb", [16, D], BF16)
    pasb = ph.sb("pasb", [128, 16 * 16], BF16)

    p_16 = ph.ps("p16", [16, TS])
    p_row = ph.ps("prow", [1, TS])
    p_rep = ph.ps("prep", [128, TS])
    p_o = [ph.ps(f"po{k}", [128, TS]) for k in range(4)]

    @blk.gpsimd
    def _(g):
        if "_cpool" in io:
            for ci, val in enumerate(CONSTS_F32):
                g.memset(io["_cpool"][:, ci:ci + 1], val)
                nc.const_aps.register(F32, val, io["_cpool"][:, ci:ci + 1])
        g.memset(ones1[:], 1.0)
        g.memset(ones128[:], 1.0)
        for c in range(16):
            s_pre.inc(g.dma_start(out=pasb[:, c * 16:(c + 1) * 16],
                                  in_=io["pa"][c * 128:(c + 1) * 128, :]), 16)
        s_pre.inc(g.dma_start(out=pbsb[:], in_=io["pb"]), 16)
    mk["pre"] = s_pre.v

    for ti in range(NT):
        @blk.sync
        def _(sync, ti=ti):
            if ti > 0:
                sync.wait_ge(s_st.h, mk[("am", "r2st", ti - 1, 1)])
            for c in range(16):
                s_ld.inc(sync.dma_start(
                    out=res1t[:, c * TS:(c + 1) * TS],
                    in_=io["res1"][c * 128:(c + 1) * 128, ti * TS:(ti + 1) * TS]), 16)
        mk[("am", "r1ld", ti)] = s_ld.v

        @blk.gpsimd
        def _(g, ti=ti):
            if ti > 0:
                g.wait_ge(s_v.h, mk[("am", "vfused", ti - 1)])
            for k in range(NKV):
                s_gd.inc(g.dma_start(
                    out=vft[:, k * TS:(k + 1) * TS],
                    in_=io["vfm_d"][k * 128:(k + 1) * 128, ti * TS:(ti + 1) * TS]), 16)
        mk[("am", "vfld", ti)] = s_gd.v

        for h in range(16):
            @blk.gpsimd
            def _(g, ti=ti, h=h):
                if not (ti == 0 and h == 0):
                    ph_, pt = (h - 1, ti) if h else (15, ti - 1)
                    g.wait_ge(s_v.h, mk[("am", "denv", pt, ph_)])
                s_gd.inc(g.dma_start(
                    out=denstg[0:1, 0:TS],
                    in_=io["pv_a"][2048 + h:2049 + h, ti * TS:(ti + 1) * TS]), 16)
                if io.get("pv_b") is not None:
                    s_gd.inc(g.dma_start(
                        out=denstg[0:1, TS:2 * TS],
                        in_=io["pv_b"][2048 + h:2049 + h, ti * TS:(ti + 1) * TS]), 16)
            mk[("am", "denld", ti, h)] = s_gd.v

            @blk.vector
            def _(v, ti=ti, h=h):
                v.wait_ge(s_gd.h, mk[("am", "denld", ti, h)])
                if ti > 0 and h == 0:
                    v.wait_ge(s_mm.h, mk[("am", "recused", ti - 1)])
                if io.get("pv_b") is not None:
                    s_v.inc(v.tensor_add(denh[0:1, h * TS:(h + 1) * TS],
                                         denstg[0:1, 0:TS],
                                         denstg[0:1, TS:2 * TS]), 1)
                else:
                    s_v.inc(v.tensor_copy(denh[0:1, h * TS:(h + 1) * TS],
                                          denstg[0:1, 0:TS]), 1)
            mk[("am", "denv", ti, h)] = s_v.v

        @blk.vector
        def _(v, ti=ti):
            v.wait_ge(s_v.h, mk[("am", "denv", ti, 15)])
            v.tensor_scalar_add(denh[:], denh[:], 1e-30)
            v.reciprocal(denh[:], denh[:])
            s_v.inc(v.tensor_copy(recbf[:], denh[:]), 1)
        mk[("am", "rec", ti)] = s_v.v

        for k in range(NKV):
            @blk.scalar
            def _(s, ti=ti, k=k):
                s.wait_ge(s_gd.h, mk[("am", "vfld", ti)])
                if ("am", "sqused") in mk:
                    s.wait_ge(s_mm.h, mk[("am", "sqused")])
                s_a.inc(s.activation(sqstg[:], vft[:, k * TS:(k + 1) * TS],
                                     AF.Square), 1)
            mk[("am", "vsq", ti, k)] = s_a.v

            @blk.tensor
            def _(t, ti=ti, k=k):
                t.wait_ge(s_a.h, mk[("am", "vsq", ti, k)])
                if ("am", "vsqrv_prev") in mk:
                    t.wait_ge(s_v.h, mk[("am", "vsqrv_prev")])
                last = t.matmul(p_row[:], ones128[:], sqstg[:],
                                start=True, stop=True)
                s_mm.inc(last, 1)
                mk[("am", "sqused")] = s_mm.v
            mk[("am", "vsqs", ti, k)] = s_mm.v

            @blk.vector
            def _(v, ti=ti, k=k):
                v.wait_ge(s_mm.h, mk[("am", "vsqs", ti, k)])
                v.tensor_scalar_max(vsqr[0:1, k * TS:(k + 1) * TS], p_row[:], 1e-24)
                s_v.inc(v.reciprocal(vsqr[0:1, k * TS:(k + 1) * TS],
                                     vsqr[0:1, k * TS:(k + 1) * TS]), 1)
                mk[("am", "vsqrv_prev")] = s_v.v
            mk[("am", "vsqr", ti, k)] = s_v.v

        for h in range(16):
            kvh = h // 4

            @blk.gpsimd
            def _(g, ti=ti, h=h):
                if not (ti == 0 and h == 0):
                    ph_, pt = (h - 1, ti) if h else (15, ti - 1)
                    g.wait_ge(s_v.h, mk[("am", "pvused", pt, ph_)])
                s_gd.inc(g.dma_start(
                    out=pvstg[:, 0:TS],
                    in_=io["pv_a"][h * 128:(h + 1) * 128, ti * TS:(ti + 1) * TS]), 16)
                if io.get("pv_b") is not None:
                    s_gd.inc(g.dma_start(
                        out=pvstg[:, TS:2 * TS],
                        in_=io["pv_b"][h * 128:(h + 1) * 128,
                                       ti * TS:(ti + 1) * TS]), 16)
            mk[("am", "pvld", ti, h)] = s_gd.v

            @blk.vector
            def _(v, ti=ti, h=h):
                v.wait_ge(s_gd.h, mk[("am", "pvld", ti, h)])
                v.wait_ge(s_v.h, mk[("am", "rec", ti)])
                if io.get("pv_b") is not None:
                    s_v.inc(v.tensor_add(pvsum[:], pvstg[:, 0:TS],
                                         pvstg[:, TS:2 * TS]), 1)
                else:
                    s_v.inc(v.tensor_copy(pvsum[:], pvstg[:, 0:TS]), 1)
                mk[("am", "pvused", ti, h)] = s_v.v
            mk[("am", "pvv", ti, h)] = s_v.v

            @blk.tensor
            def _(t, ti=ti, h=h):
                t.wait_ge(s_v.h, mk[("am", "pvv", ti, h)])
                if ("am", "repused") in mk:
                    t.wait_ge(s_v.h, mk[("am", "repused")])
                last = t.matmul(p_rep[:], ones1[:],
                                recbf[0:1, h * TS:(h + 1) * TS],
                                start=True, stop=True)
                s_mm.inc(last, 1)
            mk[("am", "rep1", ti, h)] = s_mm.v
            mk[("am", "recused", ti)] = s_mm.v

            @blk.vector
            def _(v, ti=ti, h=h, kvh=kvh):
                v.wait_ge(s_mm.h, mk[("am", "rep1", ti, h)])
                yslot = ypbuf[:, h * TS:(h + 1) * TS]
                v.tensor_mul(yslot, pvsum[:], p_rep[:])
                s_v.inc(v.tensor_mul(tpstg[:], yslot,
                                     vft[:, kvh * TS:(kvh + 1) * TS]), 1)
                mk[("am", "repused")] = s_v.v
            mk[("am", "ymul", ti, h)] = s_v.v

            @blk.tensor
            def _(t, ti=ti, h=h):
                t.wait_ge(s_v.h, mk[("am", "ymul", ti, h)])
                last = t.matmul(p_row[:], ones128[:], tpstg[:],
                                start=True, stop=True)
                s_mm.inc(last, 1)
                mk[("am", "sqused")] = s_mm.v
            mk[("am", "dot", ti, h)] = s_mm.v

            @blk.vector
            def _(v, ti=ti, h=h, kvh=kvh):
                v.wait_ge(s_mm.h, mk[("am", "dot", ti, h)])
                v.tensor_mul(srow32[:], p_row[:], vsqr[0:1, kvh * TS:(kvh + 1) * TS])
                s_v.inc(v.tensor_copy(srowbf[:], srow32[:]), 1)
            mk[("am", "srow", ti, h)] = s_v.v

            @blk.tensor
            def _(t, ti=ti, h=h):
                t.wait_ge(s_v.h, mk[("am", "srow", ti, h)])
                last = t.matmul(p_rep[:], ones1[:], srowbf[:],
                                start=True, stop=True)
                s_mm.inc(last, 1)
            mk[("am", "rep2", ti, h)] = s_mm.v

            @blk.vector
            def _(v, ti=ti, h=h, kvh=kvh):
                v.wait_ge(s_mm.h, mk[("am", "rep2", ti, h)])
                yslot = ypbuf[:, h * TS:(h + 1) * TS]
                v.tensor_mul(tpstg[:], vft[:, kvh * TS:(kvh + 1) * TS], p_rep[:])
                s_v.inc(v.tensor_sub(yslot, yslot, tpstg[:]), 1)
                mk[("am", "repused")] = s_v.v
                if h == 15:
                    mk[("am", "vfused", ti)] = s_v.v
            mk[("am", "yp", ti, h)] = s_v.v
        mk[("am", "ypall", ti)] = s_v.v

        @blk.tensor
        def _(t, ti=ti):
            t.wait_ge(s_v.h, mk[("am", "ypall", ti)])
            if ti == 0:
                t.wait_ge(s_pre.h, mk["pre"])
            if ("am", "t16used") in mk:
                t.wait_ge(s_v.h, mk[("am", "t16used")])
            last = None
            for c in range(16):
                last = t.matmul(p_16[:], pasb[:, c * 16:(c + 1) * 16],
                                ypbuf[:, c * TS:(c + 1) * TS],
                                start=(c == 0), stop=(c == 15))
            s_mm.inc(last, 1)
        mk[("am", "lmm", ti)] = s_mm.v

        @blk.vector
        def _(v, ti=ti):
            v.wait_ge(s_mm.h, mk[("am", "lmm", ti)])
            s_v.inc(v.tensor_copy(io["_t16b"][:], p_16[:]), 1)
            mk[("am", "t16used")] = s_v.v
        mk[("am", "lev", ti)] = s_v.v

        for oh in range(2):
            if ti == 0:
                @blk.gpsimd
                def _(g, oh=oh):
                    for half in range(2):
                        s_wt.inc(g.dma_start(
                            out=wstr[:, (oh * 2 + half) * 8192:
                                     (oh * 2 + half + 1) * 8192],
                            in_=io["wpr"][:, (oh * 2 + half) * 8192:
                                          (oh * 2 + half + 1) * 8192]), 16)
                mk[("am", "wld", oh)] = s_wt.v

            for q in range(2):
                @blk.tensor
                def _(t, ti=ti, oh=oh, q=q):
                    if q == 0:
                        if ti == 0:
                            t.wait_ge(s_wt.h, mk[("am", "wld", oh)])
                        t.wait_ge(s_v.h, mk[("am", "lev", ti)])
                    if ti > 0 or oh > 0 or q > 0:
                        pq = (ti, oh, 0) if q else ((ti, oh - 1, 1) if oh
                                                    else (ti - 1, 1, 1))
                        t.wait_ge(s_v.h, mk[("am", "evac", *pq)])
                    last = None
                    for ot4 in range(4):
                        ot = q * 4 + ot4
                        go = oh * 8 + ot
                        for c in range(16):
                            t.matmul(p_o[ot4][:],
                                     wstr[:, (oh * 16 + c) * 1024 + ot * 128:
                                          (oh * 16 + c) * 1024 + (ot + 1) * 128],
                                     ypbuf[:, c * TS:(c + 1) * TS],
                                     start=(c == 0), stop=False)
                        last = t.matmul(p_o[ot4][:],
                                        pbsb[:, go * 128:(go + 1) * 128],
                                        io["_t16b"][:], start=False, stop=True)
                    s_mm.inc(last, 1)
                mk[("am", "mmq", ti, oh, q)] = s_mm.v
                if q == 1:
                    mk[("am", "mm", ti, oh)] = s_mm.v

                @blk.vector
                def _(v, ti=ti, oh=oh, q=q):
                    v.wait_ge(s_mm.h, mk[("am", "mmq", ti, oh, q)])
                    v.wait_ge(s_ld.h, mk[("am", "r1ld", ti)])
                    if q == 0:
                        if oh == 1:
                            v.wait_ge(s_st.h, mk[("am", "r2st", ti, 0)])
                        elif ti > 0:
                            v.wait_ge(s_st.h, mk[("am", "r2st", ti - 1, 1)])
                    last = None
                    for ot4 in range(4):
                        ot = q * 4 + ot4
                        go = oh * 8 + ot
                        last = v.tensor_add(r232[:, ot * TS:(ot + 1) * TS],
                                            p_o[ot4][:],
                                            res1t[:, go * TS:(go + 1) * TS])
                    s_v.inc(last, 1)
                mk[("am", "evac", ti, oh, q)] = s_v.v

            @blk.gpsimd
            def _(g, ti=ti, oh=oh):
                g.wait_ge(s_v.h, mk[("am", "evac", ti, oh, 1)])
                for ot in range(8):
                    go = oh * 8 + ot
                    s_st.inc(g.dma_start(
                        out=io["res2"][go * 128:(go + 1) * 128, TS * ti:TS * (ti + 1)],
                        in_=r232[:, ot * TS:(ot + 1) * TS]), 16)
            mk[("am", "r2st", ti, oh)] = s_st.v

    @blk.sync
    def _(sync):
        sync.wait_ge(s_st.h, mk[("am", "r2st", NT - 1, 1)])

    return mk, sems


def declare_am_io(nc, dbg=False):
    io = {}
    io["pv_a"] = nc.dram_tensor("pv_a", [2064, OWN], BF16, kind="ExternalInput").ap()
    io["pv_b"] = nc.dram_tensor("pv_b", [2064, OWN], BF16, kind="ExternalInput").ap()
    io["vfm_d"] = nc.dram_tensor("vfm_d", [KVD, OWN], BF16, kind="ExternalInput").ap()
    io["res1"] = nc.dram_tensor("res1", [D, OWN], F32, kind="ExternalInput").ap()
    io["wpr"] = nc.dram_tensor("wpr", [128, 32768], F32, kind="ExternalInput").ap()
    io["pa"] = nc.dram_tensor("pa", [D, 16], F32, kind="ExternalInput").ap()
    io["pb"] = nc.dram_tensor("pb", [16, D], F32, kind="ExternalInput").ap()
    io["res2"] = nc.dram_tensor("res2", [D, OWN], F32, kind="ExternalOutput").ap()
    return io


def build_phase_am(dbg=False):
    nc = bass.Bass("TRN2", target_bir_lowering=False, debug=False)
    io = declare_am_io(nc, dbg)
    with contextlib.ExitStack() as st:
        with nc.Block() as blk:
            ph = Ph(nc, blk, st, "am_")
            io["_cpool"] = ph.sb("cpool", [128, len(CONSTS_F32)], F32)[:]
            io["_t16b"] = ph.sb("t16b", [16, 512], BF16)[:]
            emit_phase_am(ph, io, dbg)
    return nc


def prep_am_inputs(inputs, cid, pv_pair, vfm_mine, res1_mine):
    b, par = cid // 2, cid % 2
    d = {}
    d["res1"] = np.ascontiguousarray(np.asarray(res1_mine))
    d["pv_a"] = np.ascontiguousarray(np.asarray(pv_pair[par])[par * 2064:(par + 1) * 2064])
    d["pv_b"] = np.ascontiguousarray(np.asarray(pv_pair[1 - par])[par * 2064:(par + 1) * 2064])
    d["vfm_d"] = np.ascontiguousarray(np.asarray(vfm_mine))
    d["wpr"] = np.ascontiguousarray(
        np.asarray(inputs["L1_attn_Wproj"]).reshape(16, 128, 2, 1024)
        .transpose(1, 2, 0, 3).reshape(128, 32768))
    d["pa"] = inputs["L1_attn_p_A"]
    d["pb"] = inputs["L1_attn_p_B"]
    return d



def emit_phase_e(ph, io, dbg=False):
    nc, blk = ph.nc, ph.blk
    TS = 512
    TB = 1024
    NTB = OWN // TB

    sems = {k: ph.sem(k) for k in ["ld", "gd", "pre", "wt", "st", "mm", "v", "a",
                                   "wtB", "e4A", "e4B"]}
    s_ld, s_gd, s_pre, s_wt, s_st, s_mm, s_v, s_a = (
        sems["ld"], sems["gd"], sems["pre"], sems["wt"], sems["st"],
        sems["mm"], sems["v"], sems["a"])
    s_wtB, s_e4A, s_e4B = sems["wtB"], sems["e4A"], sems["e4B"]
    mk = {}

    ones1 = ph.sb("ones1", [1, 128], BF16)
    ones128 = ph.sb("ones128", [128, 1], BF16)
    identbf = ph.sb("identbf", [128, 128], BF16)
    gatesb = ph.sb("gatesb", [128, 16 * 8], BF16)
    xe = ph.sb("xe", [128, 16 * TB], BF16)
    xs = ph.sb("xs", [128, 16 * TB], BF16)
    wstr = ph.sb("wstrE", [128, 16 * 128 * 2], BF16)
    s1sb = ph.sb("s1sb", [128, TB], BF16)
    s3sb = ph.sb("s3sb", [128, TB], BF16)
    hhsb = ph.sb("hhsb", [128, TB], BF16)
    sgbf = ph.sb("sgbf", [8, TS], BF16)
    stmf = ph.sb("stmf", [128, 8 * 4], F32)
    selt = ph.sb("selt", [128, 8], BF16)
    scr8 = ph.sb("scr8", [128, 4 * 8], F32)
    self_fm = ph.sb("selfm", [8, TS], BF16)
    rowe = ph.sb("rowe", [128, 6], F32)
    wrep = ph.sb("wrep", [128, TB], BF16)
    w2str = ph.sb("w2str", [128, 2 * TS], BF16)
    hh2 = ph.sb("hh2", [128, 2 * TB], BF16)
    o32 = ph.sb("o32", [128, 8 * TS], F32)
    res2t = ph.sb("res2t", [128, 4 * TB], F32)
    lnx32 = ph.sb("lx32", [128, 16 * TS], F32)
    lnxbf = ph.sb("lxbf", [128, 16 * TS], BF16)
    lnscr = ph.sb("lscr", [128, TS], F32)
    lnrows = ph.sb("lrows", [1, 4 * TS], F32)
    lnrowbf = ph.sb("lrowbf", [1, 2 * TS], BF16)

    @blk.gpsimd
    def _(g):
        if "_cpool" in io:
            for ci, val in enumerate(CONSTS_F32):
                g.memset(io["_cpool"][:, ci:ci + 1], val)
                nc.const_aps.register(F32, val, io["_cpool"][:, ci:ci + 1])
        g.memset(ones1[:], 1.0)
        g.memset(ones128[:], 1.0)
        s_pre.inc(g.dma_start(out=identbf[:], in_=io["ident"]), 16)
        for c in range(16):
            s_pre.inc(g.dma_start(out=gatesb[:, c * 8:(c + 1) * 8],
                                  in_=io["gate_t"][c * 128:(c + 1) * 128, :]), 16)
    mk["pre"] = s_pre.v

    for tb in range(NTB):
        with contextlib.ExitStack() as stA:
            pA = Ph(nc, blk, stA, f"e{tb}a_")
            with contextlib.ExitStack() as stLN:
                pLN = Ph(nc, blk, stLN, f"e{tb}ln_")
                lnb = {
                    "x32": lnx32, "xbf": lnxbf, "xsq": xs, "xnbf": xe,
                    "scr": lnscr, "rows": lnrows, "rowbf": lnrowbf,
                    "ones1": ones1, "ones128": ones128,
                    "p_r1": pLN.ps("lpr1", [1, TS]),
                    "p_r2": pLN.ps("lpr2", [1, TS]),
                    "p_a": pLN.ps("lpa", [128, TS]),
                    "p_b": pLN.ps("lpb", [128, TS]),
                }
                for i2 in range(2):
                    i = 2 * tb + i2
                    if i == 0:
                        war = [(s_pre, "pre")]
                    elif i2 == 0:
                        war = [(s_st, ("e", "outst", tb - 1, 3)),
                               (s_v, ("e", "ln", i - 1))]
                    else:
                        war = [(s_v, ("e", "ln", i - 1))]
                    emit_ln_tile(blk, lnb, sems, mk, "e", i, io["res2"], war,
                                 dst_stride=TB, dst_off=i2 * TS)

            p_g = pA.ps("pg", [8, TS])
            p_t8 = pA.ps("pt8", [128, 8], BF16)
            p_t8b = pA.ps("pt8b", [8, 128], BF16)
            p_h = [pA.ps(f"ph{k}", [128, TS]) for k in range(4)]

            for i2 in range(2):
                i = 2 * tb + i2

                @blk.tensor
                def _(t, i2=i2, i=i):
                    t.wait_ge(s_v.h, mk[("e", "ln", i)])
                    if ("e", "sgused") in mk:
                        t.wait_ge(s_a.h, mk[("e", "sgused")])
                    last = None
                    for c in range(16):
                        last = t.matmul(p_g[:], gatesb[:, c * 8:(c + 1) * 8],
                                        xe[:, c * TB + i2 * TS:
                                           c * TB + i2 * TS + TS],
                                        start=(c == 0), stop=(c == 15))
                    s_mm.inc(last, 1)
                mk[("e", "gmm", i)] = s_mm.v

                @blk.scalar
                def _(s, i=i):
                    s.wait_ge(s_mm.h, mk[("e", "gmm", i)])
                    if ("e", "sgused2") in mk:
                        s.wait_ge(s_mm.h, mk[("e", "sgused2")])
                    s_a.inc(s.activation(sgbf[:], p_g[:], AF.Sigmoid), 1)
                    mk[("e", "sgused")] = s_a.v
                mk[("e", "sig", i)] = s_a.v

                for m in range(4):
                    @blk.tensor
                    def _(t, i=i, m=m):
                        t.wait_ge(s_a.h, mk[("e", "sig", i)])
                        if ("e", "t8used") in mk:
                            t.wait_ge(s_v.h, mk[("e", "t8used")])
                        last = t.transpose(p_t8[:], sgbf[:, m * 128:(m + 1) * 128],
                                           identbf[0:8, 0:8])
                        s_mm.inc(last, 1)
                        if m == 3:
                            mk[("e", "sgused2")] = s_mm.v
                    mk[("e", "tr", i, m)] = s_mm.v

                    @blk.vector
                    def _(v, i=i, m=m):
                        v.wait_ge(s_mm.h, mk[("e", "tr", i, m)])
                        s_v.inc(v.tensor_copy(stmf[:, m * 8:(m + 1) * 8],
                                              p_t8[:]), 1)
                        mk[("e", "t8used")] = s_v.v
                    mk[("e", "trv", i, m)] = s_v.v

                for m in range(4):
                    @blk.vector
                    def _(v, i=i, m=m):
                        v.wait_ge(s_v.h, mk[("e", "trv", i, 3)])
                        if ("e", "selused") in mk:
                            v.wait_ge(s_mm.h, mk[("e", "selused")])
                        sl = stmf[:, m * 8:(m + 1) * 8]
                        Rw = lambda k: rowe[:, k:k + 1]
                        v.reduce_max(Rw(0), sl, AX)
                        v.tensor_scalar(selt[:], sl, Rw(0), None, ALU.is_equal)
                        v.scalar_tensor_tensor(scr8[:, 0:8], selt[:], -1e38, sl,
                                               ALU.mult, ALU.add)
                        v.reduce_max(Rw(1), scr8[:, 0:8], AX)
                        v.tensor_scalar(scr8[:, 8:16], scr8[:, 0:8], Rw(1), None,
                                        ALU.is_equal)
                        v.tensor_scalar(scr8[:, 16:24], selt[:], Rw(0), None,
                                        ALU.mult)
                        v.tensor_scalar(scr8[:, 24:32], scr8[:, 8:16], Rw(1), None,
                                        ALU.mult)
                        s_v.inc(v.tensor_add(selt[:], scr8[:, 16:24],
                                             scr8[:, 24:32]), 1)
                    mk[("e", "top2", i, m)] = s_v.v

                    @blk.tensor
                    def _(t, i=i, m=m):
                        t.wait_ge(s_v.h, mk[("e", "top2", i, m)])
                        last = t.transpose(p_t8b[:], selt[:], identbf[:])
                        s_mm.inc(last, 1)
                        mk[("e", "selused")] = s_mm.v
                    mk[("e", "trb", i, m)] = s_mm.v

                    @blk.vector
                    def _(v, i=i, m=m):
                        v.wait_ge(s_mm.h, mk[("e", "trb", i, m)])
                        s_v.inc(v.tensor_copy(self_fm[:, m * 128:(m + 1) * 128],
                                              p_t8b[:]), 1)
                    mk[("e", "selfm", i, m)] = s_v.v

                @blk.gpsimd
                def _(g, i=i):
                    g.wait_ge(s_v.h, mk[("e", "selfm", i, 3)])
                    s_st.inc(g.dma_start(out=io["sel_d"][:, i * TS:(i + 1) * TS],
                                         in_=self_fm[:]), 16)
                mk[("e", "selst", i)] = s_st.v

            for e in range(9):
                if e < 8:
                    @blk.gpsimd
                    def _(g, tb=tb, e=e):
                        g.wait_ge(s_st.h, mk[("e", "selst", 2 * tb + 1)])
                        if ("e", "wrepused") in mk:
                            g.wait_ge(s_v.h, mk[("e", "wrepused")])
                        s_gd.inc(g.dma_start(
                            out=wrep[:],
                            in_=io["sel_d"][e:e + 1, tb * TB:(tb + 1) * TB]
                            .partition_broadcast(128)), 16)
                    mk[("e", "wrep", tb, e)] = s_gd.v

                    @blk.vector
                    def _(v, tb=tb, e=e):
                        v.wait_ge(s_gd.h, mk[("e", "wrep", tb, e)])
                        v.wait_ge(s_v.h, mk[("e", "ln", 2 * tb + 1)])
                        if ("e", "xsused") in mk:
                            v.wait_ge(s_mm.h, mk[("e", "xsused")])
                        last = None
                        for c in range(16):
                            last = v.tensor_mul(xs[:, c * TB:(c + 1) * TB],
                                                xe[:, c * TB:(c + 1) * TB],
                                                wrep[:])
                        s_v.inc(last, 1)
                        mk[("e", "wrepused")] = s_v.v
                    mk[("e", "xs", tb, e)] = s_v.v

                for jt in range(8):
                    for mat in range(2):
                        wsrc = (io["w1_r"] if mat == 0 else io["w3_r"]) if e < 8 \
                            else (io["ws1_r"] if mat == 0 else io["ws3_r"])
                        coff = ((e * 8 + jt) if e < 8 else jt) * 2048
                        wseq = (tb * 9 + e) * 16 + jt * 2 + mat
                        slot = (wseq % 2) if E3_DBUF else 0

                        s_w = s_wt if slot == 0 else s_wtB

                        @blk.gpsimd
                        def _(g, tb=tb, e=e, jt=jt, mat=mat, wsrc=wsrc,
                              coff=coff, wseq=wseq, slot=slot, s_w=s_w):
                            back = 2 if E3_DBUF else 1
                            if ("e", "hmm_seq", wseq - back) in mk:
                                g.wait_ge(s_mm.h, mk[("e", "hmm_seq", wseq - back)])
                            s_w.inc(g.dma_start(
                                out=wstr[:, slot * 2048:(slot + 1) * 2048],
                                in_=wsrc[:, coff:coff + 2048]), 16)
                        mk[("e", "wld", tb, e, jt, mat)] = s_w.v

                        @blk.tensor
                        def _(t, tb=tb, e=e, jt=jt, mat=mat, slot=slot, s_w=s_w):
                            t.wait_ge(s_w.h, mk[("e", "wld", tb, e, jt, mat)])
                            if e < 8:
                                t.wait_ge(s_v.h, mk[("e", "xs", tb, e)])
                            else:
                                t.wait_ge(s_v.h, mk[("e", "ln", 2 * tb + 1)])
                            if ("e", "hev", mat) in mk:
                                t.wait_ge(s_a.h, mk[("e", "hev", mat)])
                            src = xs if e < 8 else xe
                            last = None
                            for t2 in range(2):
                                for c in range(16):
                                    last = t.matmul(
                                        p_h[mat * 2 + t2][:],
                                        wstr[:, slot * 2048 + c * 128:
                                             slot * 2048 + (c + 1) * 128],
                                        src[:, c * TB + t2 * TS:
                                            c * TB + t2 * TS + TS],
                                        start=(c == 0), stop=(c == 15))
                            s_mm.inc(last, 1)
                            if e < 8:
                                mk[("e", "xsused")] = s_mm.v
                        mk[("e", "hmm", tb, e, jt, mat)] = s_mm.v
                        mk[("e", "hmm_seq", wseq)] = s_mm.v
                        mk[("e", "wused")] = s_mm.v

                        @blk.scalar
                        def _(s, tb=tb, e=e, jt=jt, mat=mat):
                            s.wait_ge(s_mm.h, mk[("e", "hmm", tb, e, jt, mat)])
                            if mat == 0 and ("e", "hhv_prev") in mk:
                                s.wait_ge(s_v.h, mk[("e", "hhv_prev")])
                            dst = s1sb if mat == 0 else s3sb
                            last = None
                            for t2 in range(2):
                                if mat == 0:
                                    last = s.activation(
                                        dst[:, t2 * TS:(t2 + 1) * TS],
                                        p_h[t2][:], AF.Silu)
                                else:
                                    last = s.copy(dst[:, t2 * TS:(t2 + 1) * TS],
                                                  p_h[2 + t2][:])
                            s_a.inc(last, 1)
                            mk[("e", "hev", mat)] = s_a.v
                        mk[("e", "hevm", tb, e, jt, mat)] = s_a.v

                    @blk.vector
                    def _(v, tb=tb, e=e, jt=jt):
                        v.wait_ge(s_a.h, mk[("e", "hevm", tb, e, jt, 1)])
                        if ("e", "hhst_prev") in mk:
                            v.wait_ge(s_st.h, mk[("e", "hhst_prev")])
                        s_v.inc(v.tensor_mul(hhsb[:], s1sb[:], s3sb[:]), 1)
                        mk[("e", "hhv_prev")] = s_v.v
                    mk[("e", "hhv", tb, e, jt)] = s_v.v

                    @blk.gpsimd
                    def _(g, tb=tb, e=e, jt=jt):
                        g.wait_ge(s_v.h, mk[("e", "hhv", tb, e, jt)])
                        s_st.inc(g.dma_start(
                            out=io["hh_d"][e * EH + jt * 128:
                                           e * EH + (jt + 1) * 128,
                                           tb * TB:(tb + 1) * TB],
                            in_=hhsb[:]), 16)
                        mk[("e", "hhst_prev")] = s_st.v
                    mk[("e", "hhst", tb, e, jt)] = s_st.v

        with contextlib.ExitStack() as stB:
            pB = Ph(nc, blk, stB, f"e{tb}b_")
            p_ro = [pB.ps(f"pro{k}", [128, TS]) for k in range(8)]

            for hq in range(4):
                @blk.sync
                def _(sync, tb=tb, hq=hq):
                    if ("e", "evdone") in mk:
                        sync.wait_ge(s_v.h, mk[("e", "evdone")])
                    for ot in range(4):
                        ho = hq * 4 + ot
                        s_ld.inc(sync.dma_start(
                            out=res2t[:, ot * TB:(ot + 1) * TB],
                            in_=io["res2"][ho * 128:(ho + 1) * 128,
                                           tb * TB:(tb + 1) * TB]), 16)
                mk[("e", "r2ld", tb, hq)] = s_ld.v

                for e in range(9):
                    for j in range(8):
                        rseq = (tb * 4 + hq) * 72 + e * 8 + j
                        slot = rseq % 2 if E4_DBUF else 0

                        s_e4 = s_e4A if slot == 0 else s_e4B

                        @blk.gpsimd
                        def _(g, tb=tb, hq=hq, e=e, j=j, rseq=rseq, slot=slot,
                              s_e4=s_e4):
                            if hq == 0 and e == 0 and j == 0:
                                g.wait_ge(s_st.h, mk[("e", "hhst", tb, 8, 7)])
                            back = 2 if E4_DBUF else 1
                            if ("e", "romm_seq", rseq - back) in mk:
                                g.wait_ge(s_mm.h, mk[("e", "romm_seq", rseq - back)])
                            if e < 8:
                                s_e4.inc(g.dma_start(
                                    out=w2str[:, slot * TS:(slot + 1) * TS],
                                    in_=io["w2_t"][e * EH + j * 128:
                                                   e * EH + (j + 1) * 128,
                                                   hq * TS:(hq + 1) * TS]), 16)
                            else:
                                s_e4.inc(g.dma_start(
                                    out=w2str[:, slot * TS:(slot + 1) * TS],
                                    in_=io["ws2_t"][j * 128:(j + 1) * 128,
                                                    hq * TS:(hq + 1) * TS]), 16)
                            s_e4.inc(g.dma_start(
                                out=hh2[:, slot * TB:(slot + 1) * TB],
                                in_=io["hh_d"][e * EH + j * 128:
                                               e * EH + (j + 1) * 128,
                                               tb * TB:(tb + 1) * TB]), 16)
                        mk[("e", "w2ld", tb, hq, e, j)] = s_e4.v

                        @blk.tensor
                        def _(t, tb=tb, hq=hq, e=e, j=j, slot=slot, rseq=rseq,
                              s_e4=s_e4):
                            t.wait_ge(s_e4.h, mk[("e", "w2ld", tb, hq, e, j)])
                            if e == 0 and j == 0:
                                t.wait_ge(s_st.h, mk[("e", "hhst", tb, 8, 7)])
                                if ("e", "evdone") in mk:
                                    t.wait_ge(s_v.h, mk[("e", "evdone")])
                            last = None
                            for ot in range(4):
                                for t2 in range(2):
                                    last = t.matmul(
                                        p_ro[ot * 2 + t2][:],
                                        w2str[:, slot * TS + ot * 128:
                                              slot * TS + (ot + 1) * 128],
                                        hh2[:, slot * TB + t2 * TS:
                                            slot * TB + (t2 + 1) * TS],
                                        start=(e == 0 and j == 0),
                                        stop=(e == 8 and j == 7))
                            s_mm.inc(last, 1)
                            mk[("e", "romm_seq", rseq)] = s_mm.v
                        mk[("e", "romm", tb, hq, e, j)] = s_mm.v

                @blk.vector
                def _(v, tb=tb, hq=hq):
                    v.wait_ge(s_mm.h, mk[("e", "romm", tb, hq, 8, 7)])
                    v.wait_ge(s_ld.h, mk[("e", "r2ld", tb, hq)])
                    if ("e", "outst_prev") in mk:
                        v.wait_ge(s_st.h, mk[("e", "outst_prev")])
                    last = None
                    for ot in range(4):
                        for t2 in range(2):
                            last = v.tensor_add(
                                o32[:, (ot * 2 + t2) * TS:(ot * 2 + t2 + 1) * TS],
                                p_ro[ot * 2 + t2][:],
                                res2t[:, ot * TB + t2 * TS:
                                      ot * TB + t2 * TS + TS])
                    s_v.inc(last, 1)
                    mk[("e", "evdone")] = s_v.v
                mk[("e", "evacall", tb, hq)] = s_v.v

                @blk.gpsimd
                def _(g, tb=tb, hq=hq):
                    g.wait_ge(s_v.h, mk[("e", "evacall", tb, hq)])
                    for ot in range(4):
                        ho = hq * 4 + ot
                        for t2 in range(2):
                            s_st.inc(g.dma_start(
                                out=io["out_own"][ho * 128:(ho + 1) * 128,
                                                  tb * TB + t2 * TS:
                                                  tb * TB + (t2 + 1) * TS],
                                in_=o32[:, (ot * 2 + t2) * TS:
                                        (ot * 2 + t2 + 1) * TS]), 16)
                    mk[("e", "outst_prev")] = s_st.v
                mk[("e", "outst", tb, hq)] = s_st.v
        mk[("e", "m2vdone")] = s_v.v

    @blk.sync
    def _(sync):
        sync.wait_ge(s_st.h, mk[("e", "outst", NTB - 1, 3)])

    return mk, sems


def declare_e_io(nc, dbg=False):
    io = {}
    io["res2"] = nc.dram_tensor("res2", [D, OWN], F32, kind="ExternalInput").ap()
    io["gate_t"] = nc.dram_tensor("gate_t", [D, 8], F32, kind="ExternalInput").ap()
    io["w1_r"] = nc.dram_tensor("w1_r", [128, NE * 8 * 2048], F32, kind="ExternalInput").ap()
    io["w3_r"] = nc.dram_tensor("w3_r", [128, NE * 8 * 2048], F32, kind="ExternalInput").ap()
    io["w2_t"] = nc.dram_tensor("w2_t", [NE * EH, D], F32, kind="ExternalInput").ap()
    io["ws1_r"] = nc.dram_tensor("ws1_r", [128, 8 * 2048], F32, kind="ExternalInput").ap()
    io["ws3_r"] = nc.dram_tensor("ws3_r", [128, 8 * 2048], F32, kind="ExternalInput").ap()
    io["ws2_t"] = nc.dram_tensor("ws2_t", [EH, D], F32, kind="ExternalInput").ap()
    io["ident"] = nc.dram_tensor("ident", [128, 128], F32, kind="ExternalInput").ap()
    io["out_own"] = nc.dram_tensor("out_own", [D, OWN], F32, kind="ExternalOutput").ap()
    kd = "ExternalOutput" if dbg else "Internal"
    io["hh_d"] = nc.dram_tensor("hh_d", [9 * EH, OWN], BF16, kind=kd).ap()
    io["sel_d"] = nc.dram_tensor("sel_d", [8, OWN], BF16, kind=kd).ap()
    return io


def build_phase_e(dbg=False):
    nc = bass.Bass("TRN2", target_bir_lowering=False, debug=False)
    io = declare_e_io(nc, dbg)
    with contextlib.ExitStack() as st:
        with nc.Block() as blk:
            ph = Ph(nc, blk, st, "e_")
            io["_cpool"] = ph.sb("cpool", [128, len(CONSTS_F32)], F32)[:]
            emit_phase_e(ph, io, dbg)
    return nc


def prep_e_inputs(inputs, cid, res2_mine):
    d = {}
    d["res2"] = np.ascontiguousarray(np.asarray(res2_mine))
    d["gate_t"] = np.ascontiguousarray(inputs["L1_ffn_gate_w"].T)
    def relay(wt, ne):
        return np.ascontiguousarray(
            wt.reshape(ne, 16, 128, 8, 128).transpose(2, 0, 3, 1, 4)
            .reshape(128, ne * 8 * 2048))
    d["w1_r"] = relay(inputs["L1_ffn_w1"].transpose(0, 2, 1).reshape(NE * D, EH), NE)
    d["w3_r"] = relay(inputs["L1_ffn_w3"].transpose(0, 2, 1).reshape(NE * D, EH), NE)
    d["w2_t"] = np.ascontiguousarray(
        inputs["L1_ffn_w2"].transpose(0, 2, 1).reshape(NE * EH, D))
    d["ws1_r"] = relay(inputs["L1_ffn_ws1"].T.reshape(D, EH), 1)
    d["ws3_r"] = relay(inputs["L1_ffn_ws3"].T.reshape(D, EH), 1)
    d["ws2_t"] = np.ascontiguousarray(inputs["L1_ffn_ws2"].T)
    d["ident"] = np.eye(128, dtype=np.float32)
    return d





def build_fused():
    nc = bass.Bass("TRN2", target_bir_lowering=False, debug=False)
    RG = [[0, 1], [2, 3], [4, 5], [6, 7]]

    eio = {}
    def ein(name, shape, dt):
        eio[name] = nc.dram_tensor(name, shape, dt, kind="ExternalInput").ap()
        return eio[name]
    ein("x_t", [D, L], F32); ein("nw", [128, 16], F32); ein("nb", [128, 16], F32)
    ein("xproj_t", [D, 192], F32); ein("dtproj_t", [DTR, D], F32)
    ein("dtbias", [128, 16], F32); ein("alog", [128, NG * NN], F32)
    ein("x_own", [D, OWN], F32); ein("D_tile", [128, 16], F32)
    ein("opr", [128, 32768], F32)
    ein("wg_r", [128, 131072], U8); ein("wu_r", [128, 131072], U8)
    ein("wd_r", [128, 131072], U8)
    ein("gsc", [128, 64], F32); ein("usc", [128, 64], F32); ein("dsc", [128, 16], F32)
    ein("wq", [D, D], F32); ein("wk", [D, KVD], F32); ein("wv", [D, KVD], F32)
    ein("qa", [D, 16], F32); ein("qb", [16, D], F32)
    ein("ka", [D, 16], F32); ein("kb", [16, KVD], F32)
    ein("va", [D, 16], F32); ein("vb", [16, KVD], F32)
    ein("qg", [1, 16], F32)
    ein("cosq", [32, L], F32); ein("sinq", [32, L], F32)
    ein("cosk", [32, OWN], F32); ein("sink", [32, OWN], F32)
    ein("cmask", [8 * 128, 512], F32); ein("ident", [128, 128], F32)
    ein("wpr", [128, 32768], F32); ein("pa", [D, 16], F32); ein("pb", [16, D], F32)
    ein("gate_t", [D, 8], F32)
    ein("w1_r", [128, NE * 8 * 2048], F32); ein("w3_r", [128, NE * 8 * 2048], F32)
    ein("w2_t", [NE * EH, D], F32)
    ein("ws1_r", [128, 8 * 2048], F32); ein("ws3_r", [128, 8 * 2048], F32)
    ein("ws2_t", [EH, D], F32)

    def itn(name, shape, dt):
        return nc.dram_tensor(name, shape, dt, kind="Internal").ap()
    y_rs = itn("y_rs", [2 * D, OWN], F32)
    y_m = itn("y_m", [D, OWN], F32)
    delta_d = itn("delta_d", [D, L], BF16)
    u_d = itn("u_d", [D, L], BF16)
    bc_d = itn("bc_d", [64, L], BF16)
    resA_d = itn("resA_d", [D, OWN], F32)
    res1 = itn("res1", [D, OWN], F32)
    xa_own = itn("xa_own", [D, OWN], BF16)
    xa_all = itn("xa_all", [2 * D, OWN], BF16)
    qd = itn("qd", [D, 2 * OWN], BF16)
    mneg_d = itn("mneg_d", [1, 16], F32)
    pv_d = itn("pv_d", [2 * 2064, OWN], BF16)
    pv_m = itn("pv_m", [2064, OWN], BF16)
    vfm_d = itn("vfm_d", [KVD, OWN], BF16)
    res2 = itn("res2", [D, OWN], F32)
    hh_d = itn("hh_d", [9 * EH, OWN], BF16)
    sel_d = itn("sel_d", [8, OWN], BF16)
    out_own = nc.dram_tensor("out_own", [D, OWN], F32, kind="ExternalOutput").ap()

    hdl = {t.tensor.name if hasattr(t, "tensor") else None: t
           for t in [y_rs, y_m, pv_d, pv_m, xa_own, xa_all]}

    with contextlib.ExitStack() as st0:
        with nc.Block() as blk:
            ph0 = Ph(nc, blk, st0, "f_")
            s_ph = ph0.sem("phase")
            s_cc = ph0.sem("cc")
            cpool = ph0.sb("cpool", [128, len(CONSTS_F32)], F32)[:]
            t16b = ph0.sb("t16b", [16, 512], BF16)[:]

            def barrier(waits, cc_jobs=()):
                if cc_jobs:
                    @blk.gpsimd
                    def _(g):
                        for h, v in waits:
                            g.wait_ge(h, v)
                        for kind, op, i_, o_ in cc_jobs:
                            s_cc.inc(g.collective_compute(
                                kind, op, replica_groups=RG,
                                ins=[i_.opt()], outs=[o_.opt()]), 1)
                        g.wait_ge(s_cc.h, s_cc.v)
                        s_ph.inc(g.memset(cpool[0:1, 0:1], CONSTS_F32[0]), 1)
                else:
                    @blk.gpsimd
                    def _(g):
                        for h, v in waits:
                            g.wait_ge(h, v)
                        s_ph.inc(g.memset(cpool[0:1, 0:1], CONSTS_F32[0]), 1)
                k = s_ph.v
                for secname in ["sync", "scalar", "vector", "tensor", "gpsimd"]:
                    sec = getattr(blk, secname)
                    @sec
                    def _(e):
                        e.wait_ge(s_ph.h, k)

            def terminal(sems):
                return [(s.h, s.v) for s in sems.values()]

            with contextlib.ExitStack() as stm:
                iom = {k: eio[k] for k in ["x_t", "nw", "nb", "xproj_t",
                                           "dtproj_t", "dtbias", "alog"]}
                iom.update(y_rs=y_rs, delta_d=delta_d, u_d=u_d, bc_d=bc_d,
                           _cpool=cpool)
                mkm, semm = emit_phase_m(Ph(nc, blk, stm, "m_"), iom)
            barrier(terminal(semm),
                    [("ReduceScatter", ALU.add, y_rs, y_m)])

            with contextlib.ExitStack() as stn:
                ion = {k: eio[k] for k in ["x_own", "D_tile", "opr",
                                           "wg_r", "wu_r", "wd_r", "gsc",
                                           "usc", "dsc"]}
                ion.update(y0own=y_m, y1own=None, res1=res1, xa_own=xa_own,
                           resA_d=resA_d)
                mkn, semn = emit_phase_n(Ph(nc, blk, stn, "n_"), ion)
            barrier(terminal(semn),
                    [("AllGather", ALU.bypass, xa_own, xa_all)])

            with contextlib.ExitStack() as sta:
                ioa = {k: eio[k] for k in ["wq", "wk", "wv", "qa", "qb", "ka",
                                           "kb", "va", "vb", "qg", "cosq",
                                           "sinq", "cosk", "sink", "cmask",
                                           "ident"]}
                ioa.update(xa_own=xa_own, xa_all=xa_all, pv_d=pv_d,
                           vfm_d=vfm_d, qd=qd, mneg_d=mneg_d)
                mka, sema = emit_phase_a(Ph(nc, blk, sta, "a_"), ioa)
            barrier(terminal(sema),
                    [("ReduceScatter", ALU.add, pv_d, pv_m)])

            with contextlib.ExitStack() as stam:
                ioam = {k: eio[k] for k in ["wpr", "pa", "pb"]}
                ioam.update(pv_a=pv_m, pv_b=None, vfm_d=vfm_d, res1=res1,
                            res2=res2, _t16b=t16b)
                mkam, semam = emit_phase_am(Ph(nc, blk, stam, "am_"), ioam)
            barrier(terminal(semam))

            with contextlib.ExitStack() as ste:
                ioe = {k: eio[k] for k in ["gate_t", "w1_r", "w3_r", "w2_t",
                                           "ws1_r", "ws3_r", "ws2_t", "ident"]}
                ioe.update(res2=res2, out_own=out_own, hh_d=hh_d, sel_d=sel_d)
                mke, seme = emit_phase_e(Ph(nc, blk, ste, "e_"), ioe)

            @blk.sync
            def _(sync):
                for h, v in terminal(seme):
                    sync.wait_ge(h, v)
    return nc


def prep_fused_inputs(inputs, cid):
    d = {}
    d.update(prep_m_inputs(inputs, cid))
    n = prep_n_inputs(inputs, cid, (np.zeros((2 * D, OWN), np.float32),) * 2)
    for k in ["y0own", "y1own"]:
        n.pop(k)
    d.update(n)
    a = prep_a_inputs(inputs, cid, (np.zeros((D, OWN), ml_dtypes.bfloat16),) * 2)
    for k in ["xa_own", "xa_all"]:
        a.pop(k)
    d.update(a)
    am = prep_am_inputs(inputs, cid,
                        (np.zeros((2 * 2064, OWN), ml_dtypes.bfloat16),) * 2,
                        np.zeros((KVD, OWN), ml_dtypes.bfloat16),
                        np.zeros((D, OWN), np.float32))
    for k in ["pv_a", "pv_b", "vfm_d", "res1"]:
        am.pop(k)
    d.update(am)
    e = prep_e_inputs(inputs, cid, np.zeros((D, OWN), np.float32))
    e.pop("res2")
    d.update(e)
    return d


def kernel_fused(**inputs):
    if "fused" not in _PROGS:
        _PROGS["fused"] = build_fused()
    cores = list(range(8))
    in_f = [prep_fused_inputs(inputs, cid) for cid in cores]
    rf = run_bass_kernel_spmd(_PROGS["fused"], in_f, core_ids=cores).results
    out = np.empty((B, L, D), np.float32)
    for cid in cores:
        b, par = cid // 2, cid % 2
        idx = own_l_index(par)
        out[b, idx, :] = np.asarray(rf[cid]["out_own"], np.float32).T
    return out


_PROGS = {}


def _programs():
    if "m" not in _PROGS:
        _PROGS["m"] = build_phase_m()
        _PROGS["n"] = build_phase_n()
        _PROGS["a"] = build_phase_a()
        _PROGS["am"] = build_phase_am()
        _PROGS["e"] = build_phase_e()
    return _PROGS


def kernel(**inputs):
    return kernel_fused(**inputs)


def kernel_multilaunch(**inputs):
    progs = _programs()
    cores = list(range(8))

    in_m = [prep_m_inputs(inputs, cid) for cid in cores]
    rm = run_bass_kernel_spmd(progs["m"], in_m, core_ids=cores).results
    y = [rm[c]["y_rs"] for c in cores]

    in_n = [prep_n_inputs(inputs, cid, (y[2 * (cid // 2)], y[2 * (cid // 2) + 1]))
            for cid in cores]
    rn = run_bass_kernel_spmd(progs["n"], in_n, core_ids=cores).results
    res1 = [rn[c]["res1"] for c in cores]
    xa = [rn[c]["xa_own"] for c in cores]

    in_a = [prep_a_inputs(inputs, cid, (xa[2 * (cid // 2)], xa[2 * (cid // 2) + 1]))
            for cid in cores]
    ra = run_bass_kernel_spmd(progs["a"], in_a, core_ids=cores).results
    pv = [ra[c]["pv_d"] for c in cores]
    vfm = [ra[c]["vfm_d"] for c in cores]

    in_am = [prep_am_inputs(inputs, cid,
                            (pv[2 * (cid // 2)], pv[2 * (cid // 2) + 1]),
                            vfm[cid], res1[cid]) for cid in cores]
    ram = run_bass_kernel_spmd(progs["am"], in_am, core_ids=cores).results
    res2 = [ram[c]["res2"] for c in cores]

    in_e = [prep_e_inputs(inputs, cid, res2[cid]) for cid in cores]
    re_ = run_bass_kernel_spmd(progs["e"], in_e, core_ids=cores).results

    out = np.empty((B, L, D), np.float32)
    for cid in cores:
        b, par = cid // 2, cid % 2
        idx = own_l_index(par)
        out[b, idx, :] = np.asarray(re_[cid]["out_own"], np.float32).T
    return out


# version 09
# speedup vs baseline: 1.0491x; 1.0491x over previous
import numpy as np
import ml_dtypes
import contextlib

import concourse.bass as bass
import concourse.mybir as mybir
from concourse.bass_utils import run_bass_kernel_spmd

F32 = mybir.dt.float32
BF16 = mybir.dt.bfloat16
U8 = mybir.dt.uint8
AF = mybir.ActivationFunctionType
ALU = mybir.AluOpType
AX = mybir.AxisListType.X
bfdt = ml_dtypes.bfloat16

B, L, D = 4, 4096, 2048
NST, DTR = 64, 128
NN = 32
NG = 16
NH, NKV, HD = 16, 4, 128
KVD = NKV * HD
NE, EH = 8, 1024
I_FFN = 8192
OWN = 2048
E4_DBUF = True
E3_DBUF = True
EPS = 1e-5
F32EPS = float(np.finfo(np.float32).eps)


class SemC:
    def __init__(self, h):
        self.h = h
        self.v = 0

    def inc(self, inst, n=1):
        self.v += n
        return inst.then_inc(self.h, n)


CONSTS_F32 = [1e-5, -0.5, -1.0, 1.0, 0.0883883476483184405, float(np.finfo(np.float32).eps),
              1.0 / 128.0, 1e-24, 1e-30, -128.0, 1.0 / D]


def emit_const_pool(nc, blk, st, sem):
    cp = st.enter_context(nc.sbuf_tensor("constpool", [128, len(CONSTS_F32)], F32))

    @blk.gpsimd
    def _(g):
        last = None
        for i, val in enumerate(CONSTS_F32):
            last = g.memset(cp[:, i:i + 1], val)
            nc.const_aps.register(F32, val, cp[:, i:i + 1])
        sem.inc(last, 1)
    return cp


class Ph:
    def __init__(self, nc, blk, st, tag):
        self.nc, self.blk, self.st, self.tag = nc, blk, st, tag

    def sb(self, name, shape, dt):
        return self.st.enter_context(self.nc.sbuf_tensor(self.tag + name, shape, dt))

    def ps(self, name, shape, dt=F32):
        return self.st.enter_context(self.nc.psum_tensor(self.tag + name, shape, dt))

    def sem(self, name):
        return SemC(self.st.enter_context(self.nc.semaphore(self.tag + name)))



def emit_phase_m(ph, io, dbg=False):
    nc, blk = ph.nc, ph.blk
    TS = 512
    NT = L // TS
    LC = 1024
    NCH = L // LC

    sems = {k: ph.sem(k) for k in
            ["ld", "gd", "pre", "st1", "st2", "yst", "mm", "v", "a"]}
    s_ld, s_gd, s_pre, s_st1, s_st2, s_yst, s_mm, s_v, s_a = (
        sems["ld"], sems["gd"], sems["pre"], sems["st1"], sems["st2"],
        sems["yst"], sems["mm"], sems["v"], sems["a"])

    aneg = ph.sb("aneg", [128, NG * NN], F32)
    mk = {}

    with contextlib.ExitStack() as st1:
        p1 = Ph(nc, blk, st1, "m1_")
        ones1 = p1.sb("ones1", [1, 128], BF16)
        ones128 = p1.sb("ones128", [128, 1], BF16)
        x32 = p1.sb("x32", [128, 16 * TS], F32)
        xbf = p1.sb("xbf", [128, 16 * TS], BF16)
        xsq = p1.sb("xsq", [128, 16 * TS], BF16)
        xn2sb = p1.sb("xn2", [128, 16 * TS], BF16)
        scr = p1.sb("scr", [128, TS], F32)
        rows = p1.sb("rows", [1, 9 * TS], F32)
        rowbf = p1.sb("rowbf", [1, 2 * TS], BF16)
        xprojsb = p1.sb("xproj", [128, 16 * 192], BF16)
        dtprojsb = p1.sb("dtproj", [128, 2048], BF16)
        dtbias_sb = p1.sb("dtbias", [128, 16], F32)
        nwsb = p1.sb("nw", [128, 16], F32)
        nbsb = p1.sb("nb", [128, 16], F32)
        drsb = p1.sb("dr", [128, TS], BF16)
        bcsb = p1.sb("bc", [64, TS], BF16)
        azsb = [p1.sb(f"az{g}", [128, TS], F32) for g in range(8)]
        rzsb = [p1.sb(f"rz{g}", [128, TS], F32) for g in range(8)]
        dsbuf = [p1.sb(f"ds{g}", [128, TS], BF16) for g in range(8)]
        usbuf = [p1.sb(f"us{g}", [128, TS], BF16) for g in range(8)]

        MU, EX2, VAR, RST1, VAR2, RST2, ATOT, BTOT, B1R = range(9)
        R = lambda k: rows[:, k * TS:(k + 1) * TS]
        ABF, BBF = range(2)
        RB = lambda k: rowbf[:, k * TS:(k + 1) * TS]

        p_r1 = p1.ps("pr1", [1, TS])
        p_r2 = p1.ps("pr2", [1, TS])
        p_p1 = p1.ps("pp1", [128, TS])
        p_p2 = p1.ps("pp2", [64, TS])
        p_rep = [p1.ps(f"prep{k}", [128, TS]) for k in range(4)]

        @blk.sync
        def _(sync):
            s_ld.inc(sync.dma_start(out=nwsb[:], in_=io["nw"]), 16)
            s_ld.inc(sync.dma_start(out=nbsb[:], in_=io["nb"]), 16)
            s_ld.inc(sync.dma_start(out=dtbias_sb[:], in_=io["dtbias"]), 16)

        @blk.gpsimd
        def _(g):
            if "_cpool" in io:
                for ci, val in enumerate(CONSTS_F32):
                    g.memset(io["_cpool"][:, ci:ci + 1], val)
                    nc.const_aps.register(F32, val, io["_cpool"][:, ci:ci + 1])
            g.memset(ones1[:], 1.0)
            g.memset(ones128[:], 1.0)
            for c in range(16):
                s_pre.inc(g.dma_start(
                    out=xprojsb[:, c * 192:(c + 1) * 192],
                    in_=io["xproj_t"][c * 128:(c + 1) * 128, :]), 16)
            s_pre.inc(g.dma_start(out=dtprojsb[:], in_=io["dtproj_t"]), 16)
            s_pre.inc(g.dma_start(out=aneg[:], in_=io["alog"]), 16)
            s_pre.inc(g.memset(scr[0:1, 0:1], 0.0), 1)
        mk["mpre"] = s_pre.v

        @blk.scalar
        def _(s):
            s.wait_ge(s_pre.h, mk["mpre"])
            s_a.inc(s.activation(scr[:], aneg[:, 0:TS], AF.Exp), 1)
            s.wait_ge(s_a.h, s_a.v)
            s_a.inc(s.mul(aneg[:, 0:TS], scr[:], -1.0), 1)
            s.wait_ge(s_a.h, s_a.v)
        mk["aneg_a"] = s_a.v

        for i in range(NT):
            @blk.sync
            def _(sync, i=i):
                if i > 0:
                    sync.wait_ge(s_v.h, mk[("m1v", i - 1)])
                for c in range(16):
                    s_ld.inc(sync.dma_start(
                        out=x32[:, c * TS:(c + 1) * TS],
                        in_=io["x_t"][c * 128:(c + 1) * 128, TS * i:TS * (i + 1)]), 16)
            mk[("xld", i)] = s_ld.v

            @blk.gpsimd
            def _(g, i=i):
                if i > 0:
                    g.wait_ge(s_v.h, mk[("m1v", i - 1)])
                for c in range(16):
                    s_gd.inc(g.dma_start(
                        out=xbf[:, c * TS:(c + 1) * TS],
                        in_=io["x_t"][c * 128:(c + 1) * 128, TS * i:TS * (i + 1)]), 16)
            mk[("xbld", i)] = s_gd.v

            @blk.scalar
            def _(s, i=i):
                s.wait_ge(s_ld.h, mk[("xld", i)])
                if i > 0:
                    s.wait_ge(s_mm.h, mk[("sums", i - 1)])
                last = None
                for c in range(16):
                    last = s.activation(xsq[:, c * TS:(c + 1) * TS],
                                        x32[:, c * TS:(c + 1) * TS], AF.Square)
                s_a.inc(last, 1)
            mk[("sq", i)] = s_a.v

            @blk.tensor
            def _(t, i=i):
                if i == 0:
                    t.wait_ge(s_pre.h, mk["mpre"])
                t.wait_ge(s_gd.h, mk[("xbld", i)])
                for c in range(16):
                    t.matmul(p_r1[:], ones128[:], xbf[:, c * TS:(c + 1) * TS],
                             start=(c == 0), stop=(c == 15))
                t.wait_ge(s_a.h, mk[("sq", i)])
                last = None
                for c in range(16):
                    last = t.matmul(p_r2[:], ones128[:], xsq[:, c * TS:(c + 1) * TS],
                                    start=(c == 0), stop=(c == 15))
                s_mm.inc(last, 1)
            mk[("sums", i)] = s_mm.v

            @blk.vector
            def _(v, i=i):
                v.wait_ge(s_mm.h, mk[("sums", i)])
                v.tensor_scalar(R(MU), p_r1[:], 1.0 / D, None, ALU.mult)
                v.tensor_scalar(R(EX2), p_r2[:], 1.0 / D, None, ALU.mult)
                v.tensor_mul(R(VAR), R(MU), R(MU))
                s_v.inc(v.tensor_sub(R(VAR), R(EX2), R(VAR)), 1)
            mk[("var", i)] = s_v.v

            @blk.scalar
            def _(s, i=i):
                s.wait_ge(s_v.h, mk[("var", i)])
                s_a.inc(s.activation(R(RST1), R(VAR), AF.Ln, bias=EPS), 1)
                s.wait_ge(s_a.h, s_a.v)
                s_a.inc(s.activation(R(RST1), R(RST1), AF.Exp, scale=-0.5), 1)
            mk[("rst1", i)] = s_a.v

            @blk.vector
            def _(v, i=i):
                v.wait_ge(s_a.h, mk[("rst1", i)])
                v.tensor_mul(R(VAR2), R(RST1), R(RST1))
                s_v.inc(v.tensor_mul(R(VAR2), R(VAR), R(VAR2)), 1)
            mk[("var2", i)] = s_v.v

            @blk.scalar
            def _(s, i=i):
                s.wait_ge(s_v.h, mk[("var2", i)])
                s_a.inc(s.activation(R(RST2), R(VAR2), AF.Ln, bias=EPS), 1)
                s.wait_ge(s_a.h, s_a.v)
                s_a.inc(s.activation(R(RST2), R(RST2), AF.Exp, scale=-0.5), 1)
            mk[("rst2", i)] = s_a.v

            @blk.vector
            def _(v, i=i):
                v.wait_ge(s_a.h, mk[("rst2", i)])
                v.tensor_mul(R(ATOT), R(RST1), R(RST2))
                v.tensor_mul(R(BTOT), R(MU), R(ATOT))
                v.tensor_copy(RB(ABF), R(ATOT))
                s_v.inc(v.tensor_copy(RB(BBF), R(BTOT)), 1)
            mk[("rbf", i)] = s_v.v

            @blk.tensor
            def _(t, i=i):
                t.wait_ge(s_v.h, mk[("rbf", i)])
                if i > 0:
                    t.wait_ge(s_v.h, mk[("m2v", i - 1, 1)])
                t.matmul(p_rep[2][:], ones1[:], RB(ABF), start=True, stop=True)
                last = t.matmul(p_rep[3][:], ones1[:], RB(BBF), start=True, stop=True)
                s_mm.inc(last, 1)
            mk[("reps", i)] = s_mm.v

            @blk.vector
            def _(v, i=i):
                v.wait_ge(s_mm.h, mk[("reps", i)])
                if i > 0:
                    v.wait_ge(s_mm.h, mk[("proj", i - 1)])
                last = None
                for c in range(16):
                    v.tensor_mul(scr[:], x32[:, c * TS:(c + 1) * TS], p_rep[2][:])
                    v.tensor_sub(scr[:], scr[:], p_rep[3][:])
                    last = v.tensor_scalar(xn2sb[:, c * TS:(c + 1) * TS], scr[:],
                                           nwsb[:, c:c + 1], nbsb[:, c:c + 1],
                                           ALU.mult, ALU.add)
                s_v.inc(last, 1)
            mk[("m1v", i)] = s_v.v

            if dbg:
                @blk.gpsimd
                def _(g, i=i):
                    g.wait_ge(s_v.h, mk[("m1v", i)])
                    for c in range(16):
                        s_st2.inc(g.dma_start(
                            out=io["xn2_d"][c * 128:(c + 1) * 128, TS * i:TS * (i + 1)],
                            in_=xn2sb[:, c * TS:(c + 1) * TS]), 16)
                    if i == 0 and "rows_d" in io:
                        s_st2.inc(g.dma_start(out=io["rows_d"], in_=rows[:]), 16)
                        s_st2.inc(g.dma_start(out=io["cpool_d"],
                                             in_=io["_cpool"][0:1, :]), 16)
                mk[("xnst", i)] = s_st2.v

            @blk.tensor
            def _(t, i=i):
                t.wait_ge(s_v.h, mk[("m1v", i)])
                for c in range(16):
                    t.matmul(p_p1[:], xprojsb[:, c * 192:c * 192 + 128],
                             xn2sb[:, c * TS:(c + 1) * TS], start=(c == 0), stop=(c == 15))
                last = None
                for c in range(16):
                    last = t.matmul(p_p2[:], xprojsb[:, c * 192 + 128:c * 192 + 192],
                                    xn2sb[:, c * TS:(c + 1) * TS],
                                    start=(c == 0), stop=(c == 15))
                s_mm.inc(last, 1)
            mk[("proj", i)] = s_mm.v

            @blk.vector
            def _(v, i=i):
                v.wait_ge(s_mm.h, mk[("proj", i)])
                if i > 0:
                    v.wait_ge(s_mm.h, mk[("dt", i - 1, 15)])
                    v.wait_ge(s_st1.h, mk[("mst", i - 1, 1)])
                v.tensor_copy(drsb[:], p_p1[:])
                s_v.inc(v.tensor_copy(bcsb[:], p_p2[:]), 1)
            mk[("drv", i)] = s_v.v

            for w in range(2):
                for g8 in range(8):
                    gg = 8 * w + g8

                    @blk.tensor
                    def _(t, i=i, gg=gg, w=w, g8=g8):
                        if gg == 0:
                            t.wait_ge(s_v.h, mk[("drv", i)])
                        if gg >= 4:
                            t.wait_ge(s_v.h, mk[("dz", i, gg - 4)])
                        elif i > 0:
                            t.wait_ge(s_v.h, mk[("dz", i - 1, 15 - (3 - gg))])
                        last = t.matmul(p_rep[gg % 4][:],
                                        dtprojsb[:, gg * 128:(gg + 1) * 128],
                                        drsb[:], start=True, stop=True)
                        s_mm.inc(last, 1)
                    mk[("dt", i, gg)] = s_mm.v

                    @blk.vector
                    def _(v, i=i, gg=gg, w=w, g8=g8):
                        v.wait_ge(s_mm.h, mk[("dt", i, gg)])
                        if w == 1 and g8 == 0:
                            v.wait_ge(s_a.h, mk[("m2a", i, 0)])
                            v.wait_ge(s_st1.h, mk[("mst", i, 0)])
                        pp = p_rep[gg % 4]
                        v.tensor_single_scalar(rzsb[g8][:], pp[:],
                                               dtbias_sb[:, gg:gg + 1], ALU.add)
                        v.tensor_single_scalar(azsb[g8][:], rzsb[g8][:], -1.0,
                                               ALU.mult)
                        v.tensor_max(azsb[g8][:], azsb[g8][:], rzsb[g8][:])
                        s_v.inc(v.tensor_relu(rzsb[g8][:], rzsb[g8][:]), 1)
                    mk[("dz", i, gg)] = s_v.v

                @blk.scalar
                def _(s, i=i, w=w):
                    s.wait_ge(s_v.h, mk[("dz", i, 8 * w + 7)])
                    last = None
                    for g8 in range(8):
                        last = s.activation(azsb[g8][:], azsb[g8][:], AF.Exp, scale=-1.0)
                    s_a.inc(last, 1)
                    s.wait_ge(s_a.h, s_a.v)
                    for g8 in range(8):
                        last = s.activation(azsb[g8][:], azsb[g8][:], AF.Ln, bias=1.0)
                    s_a.inc(last, 1)
                mk[("m2a", i, w)] = s_a.v

                @blk.vector
                def _(v, i=i, w=w):
                    v.wait_ge(s_a.h, mk[("m2a", i, w)])
                    if i > 0 and w == 0:
                        v.wait_ge(s_st1.h, mk[("mst", i - 1, 1)])
                    last = None
                    for g8 in range(8):
                        gg = 8 * w + g8
                        v.tensor_add(dsbuf[g8][:], rzsb[g8][:], azsb[g8][:])
                        last = v.tensor_mul(usbuf[g8][:], dsbuf[g8][:],
                                            xn2sb[:, gg * TS:(gg + 1) * TS])
                    s_v.inc(last, 1)
                mk[("m2v", i, w)] = s_v.v

                @blk.gpsimd
                def _(g, i=i, w=w):
                    g.wait_ge(s_v.h, mk[("m2v", i, w)])
                    for g8 in range(8):
                        gg = 8 * w + g8
                        s_st1.inc(g.dma_start(
                            out=io["delta_d"][gg * 128:(gg + 1) * 128, TS * i:TS * (i + 1)],
                            in_=dsbuf[g8][:]), 16)
                        s_st1.inc(g.dma_start(
                            out=io["u_d"][gg * 128:(gg + 1) * 128, TS * i:TS * (i + 1)],
                            in_=usbuf[g8][:]), 16)
                    if w == 1:
                        s_st1.inc(g.dma_start(out=io["bc_d"][:, TS * i:TS * (i + 1)],
                                              in_=bcsb[:]), 16)
                mk[("mst", i, w)] = s_st1.v

    with contextlib.ExitStack() as st4:
        p4 = Ph(nc, blk, st4, "m4_")
        deltasb = p4.sb("delta", [128, NG * LC], BF16)
        usb4 = p4.sb("u", [128, NG * LC], BF16)
        dA = p4.sb("dA", [128, NG * LC], F32)
        dU = p4.sb("dU", [128, LC], BF16)
        hbuf = p4.sb("h", [128, LC], BF16)
        tsc = p4.sb("tsc", [128, LC], BF16)
        brep = p4.sb("brep", [128, LC], BF16)
        crep = p4.sb("crep", [128, LC], BF16)
        ysub = p4.sb("ysub", [128, NG * LC], F32)
        state = p4.sb("state", [128, NG * NN], F32)

        for lc in range(NCH):
            @blk.sync
            def _(sync, lc=lc):
                if lc == 0:
                    sync.wait_ge(s_st1.h, mk[("mst", NT - 1, 1)])
                else:
                    sync.wait_ge(s_v.h, mk[("nd", lc - 1, NN - 1)])
                for gg in range(NG):
                    s_ld.inc(sync.dma_start(
                        out=deltasb[:, gg * LC:(gg + 1) * LC],
                        in_=io["delta_d"][gg * 128:(gg + 1) * 128,
                                          lc * LC:(lc + 1) * LC]), 16)
                    s_ld.inc(sync.dma_start(
                        out=usb4[:, gg * LC:(gg + 1) * LC],
                        in_=io["u_d"][gg * 128:(gg + 1) * 128,
                                      lc * LC:(lc + 1) * LC]), 16)
            mk[("m4ld", lc)] = s_ld.v

            for n in range(NN):
                @blk.gpsimd
                def _(g, lc=lc, n=n):
                    if lc == 0 and n == 0:
                        g.wait_ge(s_st1.h, mk[("mst", NT - 1, 1)])
                    if not (lc == 0 and n == 0):
                        pn, plc = (n - 1, lc) if n >= 1 else (NN - 1, lc - 1)
                        g.wait_ge(s_v.h, mk[("nd", plc, pn)])
                    s_gd.inc(g.dma_start(
                        out=brep[:],
                        in_=io["bc_d"][n:n + 1, lc * LC:(lc + 1) * LC]
                        .partition_broadcast(128)), 16)
                    s_gd.inc(g.dma_start(
                        out=crep[:],
                        in_=io["bc_d"][NN + n:NN + n + 1, lc * LC:(lc + 1) * LC]
                        .partition_broadcast(128)), 16)
                mk[("bc", lc, n)] = s_gd.v

                for w in range(2):
                    @blk.scalar
                    def _(s, lc=lc, n=n, w=w):
                        if n == 0 and w == 0:
                            s.wait_ge(s_ld.h, mk[("m4ld", lc)])
                            if lc == 0:
                                s.wait_ge(s_a.h, mk["aneg_a"])
                        if not (lc == 0 and n == 0):
                            pn, plc = (n - 1, lc) if n >= 1 else (NN - 1, lc - 1)
                            s.wait_ge(s_v.h, mk[("nh" if w == 0 else "nd", plc, pn)])
                        last = None
                        for g8 in range(8):
                            gg = 8 * w + g8
                            last = s.activation(
                                dA[:, gg * LC:(gg + 1) * LC],
                                deltasb[:, gg * LC:(gg + 1) * LC], AF.Exp,
                                scale=aneg[:, gg * NN + n:gg * NN + n + 1])
                        s_a.inc(last, 1)
                    mk[("dA", lc, n, w)] = s_a.v

                    @blk.vector
                    def _(v, lc=lc, n=n, w=w):
                        v.wait_ge(s_a.h, mk[("dA", lc, n, w)])
                        if w == 0:
                            v.wait_ge(s_gd.h, mk[("bc", lc, n)])
                            if n % 8 == 0 and not (lc == 0 and n == 0):
                                psub, plc = (n // 8 - 1, lc) if n >= 8 else (3, lc - 1)
                                v.wait_ge(s_yst.h, mk[("yst", plc, psub)])
                        last = None
                        for g8 in range(8):
                            gg = 8 * w + g8
                            v.tensor_mul(dU[:], usb4[:, gg * LC:(gg + 1) * LC], brep[:])
                            idx = gg * NN + n
                            init = 0.0 if lc == 0 else state[:, idx:idx + 1]
                            v.tensor_tensor_scan(hbuf[:], dA[:, gg * LC:(gg + 1) * LC],
                                                 dU[:], init, ALU.mult, ALU.add)
                            v.tensor_copy(state[:, idx:idx + 1], hbuf[:, LC - 1:LC])
                            ysl = ysub[:, gg * LC:(gg + 1) * LC]
                            if n % 8 == 0:
                                last = v.tensor_mul(ysl, hbuf[:], crep[:])
                            else:
                                v.tensor_mul(tsc[:], hbuf[:], crep[:])
                                last = v.tensor_add(ysl, ysl, tsc[:])
                        s_v.inc(last, 1)
                    if w == 0:
                        mk[("nh", lc, n)] = s_v.v
                    else:
                        mk[("nd", lc, n)] = s_v.v

                if n % 8 == 7:
                    @blk.gpsimd
                    def _(g, lc=lc, n=n):
                        g.wait_ge(s_v.h, mk[("nd", lc, n)])
                        if n >= 8:
                            g.wait_ge(s_yst.h, mk[("yst", lc, n // 8 - 1)])
                        first = (n == 7)
                        for gg in range(NG):
                            kw = {} if first else {"accum_op": ALU.add}
                            src = ysub[:, gg * LC:(gg + 1) * LC].rearrange(
                                "p (nb c) -> p nb c", c=128)
                            for r in range(2):
                                dst = io["y_rs"][r * D + gg * 128:
                                                 r * D + (gg + 1) * 128,
                                                 lc * 512:(lc + 1) * 512].rearrange(
                                    "p (nb c) -> p nb c", c=128)
                                s_yst.inc(g.dma_start(
                                    out=dst, in_=src[:, r::2, :], **kw), 16)
                    mk[("yst", lc, n // 8)] = s_yst.v

    @blk.sync
    def _(sync):
        sync.wait_ge(s_yst.h, mk[("yst", NCH - 1, 3)])
        sync.wait_ge(s_st1.h, mk[("mst", NT - 1, 1)])
        if "aneg_d" in io:
            s_ld.inc(sync.dma_start(out=io["aneg_d"], in_=aneg[:]), 16)
            sync.wait_ge(s_ld.h, s_ld.v)

    return mk, sems


def declare_m_io(nc, dbg=False):
    io = {}
    io["x_t"] = nc.dram_tensor("x_t", [D, L], F32, kind="ExternalInput").ap()
    io["nw"] = nc.dram_tensor("nw", [128, 16], F32, kind="ExternalInput").ap()
    io["nb"] = nc.dram_tensor("nb", [128, 16], F32, kind="ExternalInput").ap()
    io["xproj_t"] = nc.dram_tensor("xproj_t", [D, 192], F32, kind="ExternalInput").ap()
    io["dtproj_t"] = nc.dram_tensor("dtproj_t", [DTR, D], F32, kind="ExternalInput").ap()
    io["dtbias"] = nc.dram_tensor("dtbias", [128, 16], F32, kind="ExternalInput").ap()
    io["alog"] = nc.dram_tensor("alog", [128, NG * NN], F32, kind="ExternalInput").ap()
    io["y_rs"] = nc.dram_tensor("y_rs", [2 * D, OWN], F32, kind="ExternalOutput").ap()
    kd = "ExternalOutput" if dbg else "Internal"
    io["delta_d"] = nc.dram_tensor("delta_d", [D, L], BF16, kind=kd).ap()
    io["u_d"] = nc.dram_tensor("u_d", [D, L], BF16, kind=kd).ap()
    io["bc_d"] = nc.dram_tensor("bc_d", [64, L], BF16, kind=kd).ap()
    if dbg:
        io["xn2_d"] = nc.dram_tensor("xn2_d", [D, L], BF16, kind=kd).ap()
        io["rows_d"] = nc.dram_tensor("rows_d", [1, 9 * 512], F32, kind=kd).ap()
        io["cpool_d"] = nc.dram_tensor("cpool_d", [1, len(CONSTS_F32)], F32, kind=kd).ap()
        io["aneg_d"] = nc.dram_tensor("aneg_d", [128, NG * NN], F32, kind=kd).ap()
    return io


def build_phase_m(dbg=False):
    nc = bass.Bass("TRN2", target_bir_lowering=False, debug=False)
    io = declare_m_io(nc, dbg)
    with contextlib.ExitStack() as st:
        with nc.Block() as blk:
            ph = Ph(nc, blk, st, "m_")
            io["_cpool"] = ph.sb("cpool", [128, len(CONSTS_F32)], F32)[:]
            emit_phase_m(ph, io, dbg)
    return nc



def byteplane(w_int32):
    return np.ascontiguousarray(w_int32.view(np.uint8)[..., 0::4])


def own_l_index(par):
    lb = np.arange(16) * 2 + par
    return (lb[:, None] * 128 + np.arange(128)[None, :]).reshape(-1)


def prep_m_inputs(inputs, cid):
    b, par = cid // 2, cid % 2
    n0 = par * NN
    d = {}
    d["x_t"] = np.ascontiguousarray(inputs["L0_attn_x"][b].T)
    d["nw"] = np.ascontiguousarray(inputs["L0_attn_norm_w"].reshape(16, 128).T)
    d["nb"] = np.ascontiguousarray(inputs["L0_attn_norm_b"].reshape(16, 128).T)
    xp = inputs["L0_attn_x_proj_w"]
    d["xproj_t"] = np.ascontiguousarray(np.concatenate(
        [xp[0:128].T, xp[128 + n0:128 + n0 + NN].T, xp[192 + n0:192 + n0 + NN].T],
        axis=1))
    d["dtproj_t"] = np.ascontiguousarray(inputs["L0_attn_dt_proj_w"].T)
    d["dtbias"] = np.ascontiguousarray(inputs["L0_attn_dt_proj_b"].reshape(16, 128).T)
    d["alog"] = np.ascontiguousarray(
        inputs["L0_attn_A_log"][:, n0:n0 + NN].reshape(NG, 128, NN)
        .transpose(1, 0, 2).reshape(128, NG * NN))
    return d





def emit_ln_tile(blk, bufs, sems, mk, tag, i, src_dram, war_marks=(),
                 dst_stride=512, dst_off=0):
    TS = 512
    x32, xbf, xsq, xnbf = bufs["x32"], bufs["xbf"], bufs["xsq"], bufs["xnbf"]
    scr, rows, rowbf = bufs["scr"], bufs["rows"], bufs["rowbf"]
    p_r1, p_r2, p_a, p_b = bufs["p_r1"], bufs["p_r2"], bufs["p_a"], bufs["p_b"]
    ones1, ones128 = bufs["ones1"], bufs["ones128"]
    s_ld, s_gd, s_mm, s_v, s_a = (sems["ld"], sems["gd"], sems["mm"],
                                  sems["v"], sems["a"])
    MU, VAR, RST, B1 = 0, 1, 2, 3
    R = lambda k: rows[:, k * TS:(k + 1) * TS]
    RB = lambda k: rowbf[:, k * TS:(k + 1) * TS]

    @blk.sync
    def _(sync):
        for sem, key in war_marks:
            sync.wait_ge(sem.h, mk[key])
        for c in range(16):
            s_ld.inc(sync.dma_start(
                out=x32[:, c * TS:(c + 1) * TS],
                in_=src_dram[c * 128:(c + 1) * 128, TS * i:TS * (i + 1)]), 16)
    mk[(tag, "xld", i)] = s_ld.v

    @blk.gpsimd
    def _(g):
        for sem, key in war_marks:
            g.wait_ge(sem.h, mk[key])
        for c in range(16):
            s_gd.inc(g.dma_start(
                out=xbf[:, c * TS:(c + 1) * TS],
                in_=src_dram[c * 128:(c + 1) * 128, TS * i:TS * (i + 1)]), 16)
    mk[(tag, "xbld", i)] = s_gd.v

    @blk.scalar
    def _(s):
        s.wait_ge(s_ld.h, mk[(tag, "xld", i)])
        for sem, key in war_marks:
            s.wait_ge(sem.h, mk[key])
        last = None
        for c in range(16):
            last = s.activation(xsq[:, c * TS:(c + 1) * TS],
                                x32[:, c * TS:(c + 1) * TS], AF.Square)
        s_a.inc(last, 1)
    mk[(tag, "sq", i)] = s_a.v

    @blk.tensor
    def _(t):
        t.wait_ge(s_gd.h, mk[(tag, "xbld", i)])
        for sem, key in war_marks:
            t.wait_ge(sem.h, mk[key])
        for c in range(16):
            t.matmul(p_r1[:], ones128[:], xbf[:, c * TS:(c + 1) * TS],
                     start=(c == 0), stop=(c == 15))
        t.wait_ge(s_a.h, mk[(tag, "sq", i)])
        last = None
        for c in range(16):
            last = t.matmul(p_r2[:], ones128[:], xsq[:, c * TS:(c + 1) * TS],
                            start=(c == 0), stop=(c == 15))
        s_mm.inc(last, 1)
    mk[(tag, "sums", i)] = s_mm.v

    @blk.vector
    def _(v):
        v.wait_ge(s_mm.h, mk[(tag, "sums", i)])
        v.tensor_scalar(R(MU), p_r1[:], 1.0 / D, None, ALU.mult)
        v.tensor_scalar(R(VAR), p_r2[:], 1.0 / D, None, ALU.mult)
        v.tensor_mul(R(B1), R(MU), R(MU))
        s_v.inc(v.tensor_sub(R(VAR), R(VAR), R(B1)), 1)
    mk[(tag, "var", i)] = s_v.v

    @blk.scalar
    def _(s):
        s.wait_ge(s_v.h, mk[(tag, "var", i)])
        s_a.inc(s.activation(R(RST), R(VAR), AF.Ln, bias=EPS), 1)
        s.wait_ge(s_a.h, s_a.v)
        s_a.inc(s.activation(R(RST), R(RST), AF.Exp, scale=-0.5), 1)
    mk[(tag, "rst", i)] = s_a.v

    @blk.vector
    def _(v):
        v.wait_ge(s_a.h, mk[(tag, "rst", i)])
        v.tensor_mul(R(B1), R(MU), R(RST))
        v.tensor_copy(RB(0), R(RST))
        s_v.inc(v.tensor_copy(RB(1), R(B1)), 1)
    mk[(tag, "rbf", i)] = s_v.v

    @blk.tensor
    def _(t):
        t.wait_ge(s_v.h, mk[(tag, "rbf", i)])
        t.matmul(p_a[:], ones1[:], RB(0), start=True, stop=True)
        last = t.matmul(p_b[:], ones1[:], RB(1), start=True, stop=True)
        s_mm.inc(last, 1)
    mk[(tag, "reps", i)] = s_mm.v

    @blk.vector
    def _(v):
        v.wait_ge(s_mm.h, mk[(tag, "reps", i)])
        last = None
        for c in range(16):
            v.tensor_mul(scr[:], x32[:, c * TS:(c + 1) * TS], p_a[:])
            last = v.tensor_sub(
                xnbf[:, c * dst_stride + dst_off:c * dst_stride + dst_off + TS],
                scr[:], p_b[:])
        s_v.inc(last, 1)
    mk[(tag, "ln", i)] = s_v.v
    return (tag, "ln", i)


def mk_ln_bufs(p, xsq=None):
    TS = 512
    return {
        "x32": p.sb("x32", [128, 16 * TS], F32),
        "xbf": p.sb("xbf", [128, 16 * TS], BF16),
        "xsq": xsq if xsq is not None else p.sb("xsq", [128, 16 * TS], BF16),
        "xnbf": p.sb("xnbf", [128, 16 * TS], BF16),
        "scr": p.sb("scr", [128, TS], F32),
        "rows": p.sb("rows", [1, 4 * TS], F32),
        "rowbf": p.sb("rowbf", [1, 2 * TS], BF16),
        "p_r1": p.ps("pr1", [1, TS]),
        "p_r2": p.ps("pr2", [1, TS]),
        "p_a": p.ps("pa", [128, TS]),
        "p_b": p.ps("pb", [128, TS]),
    }


def emit_phase_n(ph, io, dbg=False):
    nc, blk = ph.nc, ph.blk
    TS = 512
    NT = OWN // TS

    sems = {k: ph.sem(k) for k in ["ld", "gd", "pre", "wt", "st", "mm", "v", "a",
                                   "wtB", "e4A", "e4B"]}
    s_ld, s_gd, s_pre, s_wt, s_st, s_mm, s_v, s_a = (
        sems["ld"], sems["gd"], sems["pre"], sems["wt"], sems["st"],
        sems["mm"], sems["v"], sems["a"])
    s_wtB, s_e4A, s_e4B = sems["wtB"], sems["e4A"], sems["e4B"]
    mk = {}

    ones1 = ph.sb("ones1", [1, 128], BF16)
    ones128 = ph.sb("ones128", [128, 1], BF16)
    m128row = ph.sb("m128row", [1, 128], BF16)
    dtile = ph.sb("dtile", [128, 16], F32)
    gscsb = ph.sb("gsc", [128, 64], F32)
    uscsb = ph.sb("usc", [128, 64], F32)
    dscsb = ph.sb("dsc", [128, 16], F32)

    @blk.sync
    def _(sync):
        s_pre.inc(sync.dma_start(out=dtile[:], in_=io["D_tile"]), 16)
        s_pre.inc(sync.dma_start(out=gscsb[:], in_=io["gsc"]), 16)
        s_pre.inc(sync.dma_start(out=uscsb[:], in_=io["usc"]), 16)
        s_pre.inc(sync.dma_start(out=dscsb[:], in_=io["dsc"]), 16)

    @blk.gpsimd
    def _(g):
        if "_cpool" in io:
            for ci, val in enumerate(CONSTS_F32):
                g.memset(io["_cpool"][:, ci:ci + 1], val)
                nc.const_aps.register(F32, val, io["_cpool"][:, ci:ci + 1])
        g.memset(ones1[:], 1.0)
        g.memset(ones128[:], 1.0)
        s_pre.inc(g.memset(m128row[:], -128.0), 1)
    mk["pre"] = s_pre.v

    def pre_wait(eng):
        eng.wait_ge(s_pre.h, mk["pre"])

    with contextlib.ExitStack() as st1:
        p1 = Ph(nc, blk, st1, "n1_")
        bufs = mk_ln_bufs(p1)
        bufs.update(ones1=ones1, ones128=ones128)
        zbf = p1.sb("zbf", [128, 16 * TS], BF16)
        ystg = p1.sb("ystg", [128, 2 * TS], F32)
        wstr = p1.sb("wstr", [128, 2 * 16 * 1024], BF16)
        ra32 = p1.sb("ra32", [128, 8 * TS], F32)
        p_o = [p1.ps(f"po{k}", [128, TS]) for k in range(4)]

        for i in range(NT):
            war = [(s_pre, "pre")] if i == 0 else [(s_st, ("n1", "rast", i - 1, 1))]
            emit_ln_tile(blk, bufs, sems, mk, "n1", i, io["x_own"], war)

            for c in range(16):
                @blk.gpsimd
                def _(g, i=i, c=c):
                    if not (i == 0 and c == 0):
                        pc = (i, c - 1) if c else (i - 1, 15)
                        g.wait_ge(s_v.h, mk[("n1", "zc", *pc)])
                    s_gd.inc(g.dma_start(
                        out=ystg[:, 0:TS],
                        in_=io["y0own"][c * 128:(c + 1) * 128, TS * i:TS * (i + 1)]), 16)
                    if io.get("y1own") is not None:
                        s_gd.inc(g.dma_start(
                            out=ystg[:, TS:2 * TS],
                            in_=io["y1own"][c * 128:(c + 1) * 128,
                                            TS * i:TS * (i + 1)]), 16)
                mk[("n1", "yld", i, c)] = s_gd.v

                @blk.vector
                def _(v, i=i, c=c):
                    if c == 0:
                        v.wait_ge(s_v.h, mk[("n1", "ln", i)])
                        if i == 0:
                            v.wait_ge(s_pre.h, mk["pre"])
                    v.wait_ge(s_gd.h, mk[("n1", "yld", i, c)])
                    last = v.scalar_tensor_tensor(
                        zbf[:, c * TS:(c + 1) * TS],
                        bufs["xnbf"][:, c * TS:(c + 1) * TS],
                        dtile[:, c:c + 1], ystg[:, 0:TS], ALU.mult, ALU.add)
                    if io.get("y1own") is not None:
                        last = v.tensor_add(zbf[:, c * TS:(c + 1) * TS],
                                            zbf[:, c * TS:(c + 1) * TS],
                                            ystg[:, TS:2 * TS])
                    s_v.inc(last, 1)
                mk[("n1", "zc", i, c)] = s_v.v
            mk[("n1", "zv", i)] = s_v.v

            for oh in range(2):
                if i == 0:
                    @blk.gpsimd
                    def _(g, oh=oh):
                        for half in range(2):
                            s_wt.inc(g.dma_start(
                                out=wstr[:, (oh * 2 + half) * 8192:
                                         (oh * 2 + half + 1) * 8192],
                                in_=io["opr"][:, (oh * 2 + half) * 8192:
                                              (oh * 2 + half + 1) * 8192]), 16)
                    mk[("n1", "wld", oh)] = s_wt.v

                for q in range(2):
                    @blk.tensor
                    def _(t, i=i, oh=oh, q=q):
                        if q == 0:
                            if i == 0:
                                t.wait_ge(s_wt.h, mk[("n1", "wld", oh)])
                            t.wait_ge(s_v.h, mk[("n1", "zv", i)])
                        if i > 0 or oh > 0 or q > 0:
                            pq = (i, oh, 0) if q else ((i, oh - 1, 1) if oh else (i - 1, 1, 1))
                            t.wait_ge(s_v.h, mk[("n1", "evac", *pq)])
                        last = None
                        for ot4 in range(4):
                            ot = q * 4 + ot4
                            for c in range(16):
                                last = t.matmul(
                                    p_o[ot4][:],
                                    wstr[:, (oh * 16 + c) * 1024 + ot * 128:
                                         (oh * 16 + c) * 1024 + (ot + 1) * 128],
                                    zbf[:, c * TS:(c + 1) * TS],
                                    start=(c == 0), stop=(c == 15))
                        s_mm.inc(last, 1)
                    mk[("n1", "mmq", i, oh, q)] = s_mm.v
                    if q == 1:
                        mk[("n1", "mm", i, oh)] = s_mm.v

                    @blk.vector
                    def _(v, i=i, oh=oh, q=q):
                        v.wait_ge(s_mm.h, mk[("n1", "mmq", i, oh, q)])
                        if q == 0:
                            if oh == 1:
                                v.wait_ge(s_st.h, mk[("n1", "rast", i, 0)])
                            elif i > 0:
                                v.wait_ge(s_st.h, mk[("n1", "rast", i - 1, 1)])
                        last = None
                        for ot4 in range(4):
                            ot = q * 4 + ot4
                            go = oh * 8 + ot
                            last = v.tensor_add(ra32[:, ot * TS:(ot + 1) * TS],
                                                p_o[ot4][:],
                                                bufs["x32"][:, go * TS:(go + 1) * TS])
                        s_v.inc(last, 1)
                    mk[("n1", "evac", i, oh, q)] = s_v.v

                @blk.gpsimd
                def _(g, i=i, oh=oh):
                    g.wait_ge(s_v.h, mk[("n1", "evac", i, oh, 1)])
                    for ot in range(8):
                        go = oh * 8 + ot
                        s_st.inc(g.dma_start(
                            out=io["resA_d"][go * 128:(go + 1) * 128,
                                             TS * i:TS * (i + 1)],
                            in_=ra32[:, ot * TS:(ot + 1) * TS]), 16)
                mk[("n1", "rast", i, oh)] = s_st.v

    with contextlib.ExitStack() as st2:
        p2 = Ph(nc, blk, st2, "n2_")
        hsb = p2.sb("hsb", [128, 64 * TS], BF16)
        bufs2 = mk_ln_bufs(p2, xsq=hsb)
        bufs2.update(ones1=ones1, ones128=ones128)
        gsb = p2.sb("gsb", [128, 4 * TS], BF16)
        usb = p2.sb("usb", [128, 4 * TS], BF16)
        wstr2 = p2.sb("wstr2", [128, 16 * 1024], BF16)
        trow = p2.sb("trow", [1, 4 * TS], BF16)
        r132 = p2.sb("r132", [128, 4 * TS], F32)
        p_f = [p2.ps(f"pf{k}", [128, TS]) for k in range(4)]

        for i in range(NT):
            war = [(s_st, ("n1", "rast", NT - 1, 1))] if i == 0 else \
                  [(s_st, ("n2", "r1st", i - 1, 3))]
            emit_ln_tile(blk, bufs2, sems, mk, "n2", i, io["resA_d"], war)

            @blk.tensor
            def _(t, i=i):
                t.wait_ge(s_v.h, mk[("n2", "ln", i)])
                last = None
                for c in range(16):
                    last = t.matmul(bufs2["p_r1"][:], ones128[:],
                                    bufs2["xnbf"][:, c * TS:(c + 1) * TS],
                                    start=(c == 0), stop=(c == 15))
                s_mm.inc(last, 1)
            mk[("n2", "trow", i)] = s_mm.v

            @blk.vector
            def _(v, i=i):
                v.wait_ge(s_mm.h, mk[("n2", "trow", i)])
                R3 = bufs2["rows"][:, 3 * TS:4 * TS]
                v.tensor_copy(trow[:, 0:TS], bufs2["p_r1"][:])
                v.tensor_copy(R3, trow[:, 0:TS])
                v.tensor_sub(R3, bufs2["p_r1"][:], R3)
                s_v.inc(v.tensor_copy(trow[:, TS:2 * TS], R3), 1)
            mk[("n2", "trowv", i)] = s_v.v

            for band in range(16):
                for mat in range(2):
                    wsrc = io["wg_r"] if mat == 0 else io["wu_r"]
                    wseq = mk.get(("n2", "wseq_ctr"), 0)
                    mk[("n2", "wseq_ctr")] = wseq + 1
                    slot = wseq % 2
                    s_w = s_wtB if slot == 0 else s_e4A

                    @blk.gpsimd
                    def _(g, i=i, band=band, mat=mat, wsrc=wsrc, wseq=wseq,
                          slot=slot, s_w=s_w):
                        if ("n2", "wmm_seq", wseq - 2) in mk:
                            g.wait_ge(s_mm.h, mk[("n2", "wmm_seq", wseq - 2)])
                        s_w.inc(g.dma_start(
                            out=wstr2[:, slot * 8192:(slot + 1) * 8192],
                            in_=wsrc[:, band * 8192:(band + 1) * 8192]), 16)
                    mk[("n2", "gwld", i, band, mat)] = s_w.v

                    @blk.tensor
                    def _(t, i=i, band=band, mat=mat, wseq=wseq, slot=slot,
                          s_w=s_w):
                        t.wait_ge(s_w.h, mk[("n2", "gwld", i, band, mat)])
                        if mat == 0 and band == 0:
                            t.wait_ge(s_v.h, mk[("n2", "trowv", i)])
                            if ("n2", "dev_prev") in mk:
                                t.wait_ge(s_v.h, mk[("n2", "dev_prev")])
                        if ("n2", "gev_prev") in mk:
                            t.wait_ge(s_a.h, mk[("n2", "gev_prev")])
                        last = None
                        for ot in range(4):
                            for c in range(16):
                                t.matmul(p_f[ot][:],
                                         wstr2[:, slot * 8192 + c * 512 + ot * 128:
                                               slot * 8192 + c * 512 + (ot + 1) * 128],
                                         bufs2["xnbf"][:, c * TS:(c + 1) * TS],
                                         start=(c == 0), stop=False)
                            t.matmul(p_f[ot][:], m128row[:], trow[0:1, 0:TS],
                                     start=False, stop=False)
                            last = t.matmul(p_f[ot][:], m128row[:], trow[0:1, TS:2 * TS],
                                            start=False, stop=True)
                        s_mm.inc(last, 1)
                        mk[("n2", "wmm_seq", wseq)] = s_mm.v
                    mk[("n2", "gmm", i, band, mat)] = s_mm.v
                    mk[("n2", "gmm_prev")] = s_mm.v

                    @blk.scalar
                    def _(s, i=i, band=band, mat=mat):
                        s.wait_ge(s_mm.h, mk[("n2", "gmm", i, band, mat)])
                        if mat == 1 and ("n2", "hprev") in mk:
                            s.wait_ge(s_v.h, mk[("n2", "hprev")])
                        last = None
                        for ot in range(4):
                            it = band * 4 + ot
                            if mat == 0:
                                last = s.activation(gsb[:, ot * TS:(ot + 1) * TS],
                                                    p_f[ot][:], AF.Silu,
                                                    scale=gscsb[:, it:it + 1])
                            else:
                                last = s.activation(usb[:, ot * TS:(ot + 1) * TS],
                                                    p_f[ot][:], AF.Copy,
                                                    scale=uscsb[:, it:it + 1])
                        s_a.inc(last, 1)
                    mk[("n2", "gev", i, band, mat)] = s_a.v
                    mk[("n2", "gev_prev")] = s_a.v

                @blk.vector
                def _(v, i=i, band=band):
                    v.wait_ge(s_a.h, mk[("n2", "gev", i, band, 1)])
                    last = None
                    for ot in range(4):
                        it = band * 4 + ot
                        last = v.tensor_mul(hsb[:, it * TS:(it + 1) * TS],
                                            gsb[:, ot * TS:(ot + 1) * TS],
                                            usb[:, ot * TS:(ot + 1) * TS])
                    s_v.inc(last, 1)
                mk[("n2", "h", i, band)] = s_v.v
                mk[("n2", "hprev")] = s_v.v

                @blk.tensor
                def _(t, i=i, band=band):
                    t.wait_ge(s_v.h, mk[("n2", "h", i, band)])
                    last = None
                    for ot in range(4):
                        it = band * 4 + ot
                        last = t.matmul(bufs2["p_r2"][:], ones128[:],
                                        hsb[:, it * TS:(it + 1) * TS],
                                        start=(band == 0 and ot == 0),
                                        stop=(band == 15 and ot == 3))
                    s_mm.inc(last, 1)
                mk[("n2", "hsum", i, band)] = s_mm.v

            @blk.vector
            def _(v, i=i):
                v.wait_ge(s_mm.h, mk[("n2", "hsum", i, 15)])
                R3 = bufs2["rows"][:, 3 * TS:4 * TS]
                v.tensor_copy(trow[:, 2 * TS:3 * TS], bufs2["p_r2"][:])
                v.tensor_copy(R3, trow[:, 2 * TS:3 * TS])
                v.tensor_sub(R3, bufs2["p_r2"][:], R3)
                s_v.inc(v.tensor_copy(trow[:, 3 * TS:4 * TS], R3), 1)
            mk[("n2", "hsv", i)] = s_v.v

            for hq in range(4):
                for iq in range(4):
                    wseq = mk.get(("n2", "wseq_ctr"), 0)
                    mk[("n2", "wseq_ctr")] = wseq + 1
                    slot = wseq % 2
                    s_w = s_wtB if slot == 0 else s_e4A

                    @blk.gpsimd
                    def _(g, i=i, hq=hq, iq=iq, wseq=wseq, slot=slot, s_w=s_w):
                        if ("n2", "wmm_seq", wseq - 2) in mk:
                            g.wait_ge(s_mm.h, mk[("n2", "wmm_seq", wseq - 2)])
                        s_w.inc(g.dma_start(
                            out=wstr2[:, slot * 8192:(slot + 1) * 8192],
                            in_=io["wd_r"][:, (hq * 4 + iq) * 8192:
                                           (hq * 4 + iq + 1) * 8192]), 16)
                    mk[("n2", "dwld", i, hq, iq)] = s_w.v

                    @blk.tensor
                    def _(t, i=i, hq=hq, iq=iq, wseq=wseq, slot=slot, s_w=s_w):
                        t.wait_ge(s_w.h, mk[("n2", "dwld", i, hq, iq)])
                        if iq == 0:
                            t.wait_ge(s_v.h, mk[("n2", "hsv", i)])
                            if ("n2", "dev_prev") in mk:
                                t.wait_ge(s_v.h, mk[("n2", "dev_prev")])
                        last = None
                        for ot in range(4):
                            for j in range(16):
                                git = iq * 16 + j
                                last = t.matmul(
                                    p_f[ot][:],
                                    wstr2[:, slot * 8192 + j * 512 + ot * 128:
                                          slot * 8192 + j * 512 + (ot + 1) * 128],
                                    hsb[:, git * TS:(git + 1) * TS],
                                    start=(iq == 0 and j == 0), stop=False)
                            if iq == 3:
                                t.matmul(p_f[ot][:], m128row[:],
                                         trow[0:1, 2 * TS:3 * TS],
                                         start=False, stop=False)
                                last = t.matmul(p_f[ot][:], m128row[:],
                                                trow[0:1, 3 * TS:4 * TS],
                                                start=False, stop=True)
                        s_mm.inc(last, 1)
                        mk[("n2", "wmm_seq", wseq)] = s_mm.v
                        mk[("n2", "gmm_prev")] = s_mm.v
                    mk[("n2", "dmm", i, hq, iq)] = s_mm.v

                @blk.vector
                def _(v, i=i, hq=hq):
                    v.wait_ge(s_mm.h, mk[("n2", "dmm", i, hq, 3)])
                    if ("n2", "r1st_prev") in mk:
                        v.wait_ge(s_st.h, mk[("n2", "r1st_prev")])
                    last = None
                    for ot in range(4):
                        ho = hq * 4 + ot
                        last = v.scalar_tensor_tensor(
                            r132[:, ot * TS:(ot + 1) * TS], p_f[ot][:],
                            dscsb[:, ho:ho + 1],
                            bufs2["x32"][:, ho * TS:(ho + 1) * TS],
                            ALU.mult, ALU.add)
                    s_v.inc(last, 1)
                mk[("n2", "dev", i, hq)] = s_v.v
                mk[("n2", "dev_prev")] = s_v.v

                @blk.gpsimd
                def _(g, i=i, hq=hq):
                    g.wait_ge(s_v.h, mk[("n2", "dev", i, hq)])
                    for ot in range(4):
                        ho = hq * 4 + ot
                        s_st.inc(g.dma_start(
                            out=io["res1"][ho * 128:(ho + 1) * 128, TS * i:TS * (i + 1)],
                            in_=r132[:, ot * TS:(ot + 1) * TS]), 16)
                mk[("n2", "r1st", i, hq)] = s_st.v
                mk[("n2", "r1st_prev")] = s_st.v

    with contextlib.ExitStack() as st3:
        p3 = Ph(nc, blk, st3, "n3_")
        bufs3 = mk_ln_bufs(p3)
        bufs3.update(ones1=ones1, ones128=ones128)
        for i in range(NT):
            war = [(s_st, ("n2", "r1st", NT - 1, 3))] if i == 0 else \
                  [(s_st, ("n3", "xast", i - 1))]
            emit_ln_tile(blk, bufs3, sems, mk, "n3", i, io["res1"], war)

            @blk.gpsimd
            def _(g, i=i):
                g.wait_ge(s_v.h, mk[("n3", "ln", i)])
                for c in range(16):
                    s_st.inc(g.dma_start(
                        out=io["xa_own"][c * 128:(c + 1) * 128, TS * i:TS * (i + 1)],
                        in_=bufs3["xnbf"][:, c * TS:(c + 1) * TS]), 16)
            mk[("n3", "xast", i)] = s_st.v

    @blk.sync
    def _(sync):
        sync.wait_ge(s_st.h, mk[("n3", "xast", NT - 1)])

    return mk, sems


def declare_n_io(nc, dbg=False):
    io = {}
    io["x_own"] = nc.dram_tensor("x_own", [D, OWN], F32, kind="ExternalInput").ap()
    io["y0own"] = nc.dram_tensor("y0own", [D, OWN], F32, kind="ExternalInput").ap()
    io["y1own"] = nc.dram_tensor("y1own", [D, OWN], F32, kind="ExternalInput").ap()
    io["D_tile"] = nc.dram_tensor("D_tile", [128, 16], F32, kind="ExternalInput").ap()
    io["opr"] = nc.dram_tensor("opr", [128, 32768], F32, kind="ExternalInput").ap()
    io["wg_r"] = nc.dram_tensor("wg_r", [128, 131072], U8, kind="ExternalInput").ap()
    io["wu_r"] = nc.dram_tensor("wu_r", [128, 131072], U8, kind="ExternalInput").ap()
    io["wd_r"] = nc.dram_tensor("wd_r", [128, 131072], U8, kind="ExternalInput").ap()
    io["gsc"] = nc.dram_tensor("gsc", [128, 64], F32, kind="ExternalInput").ap()
    io["usc"] = nc.dram_tensor("usc", [128, 64], F32, kind="ExternalInput").ap()
    io["dsc"] = nc.dram_tensor("dsc", [128, 16], F32, kind="ExternalInput").ap()
    io["res1"] = nc.dram_tensor("res1", [D, OWN], F32, kind="ExternalOutput").ap()
    io["xa_own"] = nc.dram_tensor("xa_own", [D, OWN], BF16, kind="ExternalOutput").ap()
    kd = "ExternalOutput" if dbg else "Internal"
    io["resA_d"] = nc.dram_tensor("resA_d", [D, OWN], F32, kind=kd).ap()
    return io


def build_phase_n(dbg=False):
    nc = bass.Bass("TRN2", target_bir_lowering=False, debug=False)
    io = declare_n_io(nc, dbg)
    with contextlib.ExitStack() as st:
        with nc.Block() as blk:
            ph = Ph(nc, blk, st, "n_")
            io["_cpool"] = ph.sb("cpool", [128, len(CONSTS_F32)], F32)[:]
            emit_phase_n(ph, io, dbg)
    return nc


def prep_n_inputs(inputs, cid, y_pair):
    b, par = cid // 2, cid % 2
    idx = own_l_index(par)
    d = {}
    d["x_own"] = np.ascontiguousarray(inputs["L0_attn_x"][b].T[:, idx])
    d["y0own"] = np.ascontiguousarray(np.asarray(y_pair[0])[par * D:(par + 1) * D])
    d["y1own"] = np.ascontiguousarray(np.asarray(y_pair[1])[par * D:(par + 1) * D])
    d["D_tile"] = np.ascontiguousarray(inputs["L0_attn_D_param"].reshape(16, 128).T)
    d["opr"] = np.ascontiguousarray(
        inputs["L0_attn_out_proj_w"].T.reshape(16, 128, 2, 1024)
        .transpose(1, 2, 0, 3).reshape(128, 32768))
    d["wg_r"] = np.ascontiguousarray(
        byteplane(inputs["L0_ffn_gate_w"]).T.reshape(16, 128, 16, 512)
        .transpose(1, 2, 0, 3).reshape(128, 131072))
    d["wu_r"] = np.ascontiguousarray(
        byteplane(inputs["L0_ffn_up_w"]).T.reshape(16, 128, 16, 512)
        .transpose(1, 2, 0, 3).reshape(128, 131072))
    d["wd_r"] = np.ascontiguousarray(
        byteplane(inputs["L0_ffn_down_w"]).T.reshape(4, 16, 128, 4, 512)
        .transpose(2, 3, 0, 1, 4).reshape(128, 131072))
    d["gsc"] = np.ascontiguousarray(inputs["L0_ffn_gate_scale"].reshape(64, 128).T)
    d["usc"] = np.ascontiguousarray(inputs["L0_ffn_up_scale"].reshape(64, 128).T)
    d["dsc"] = np.ascontiguousarray(inputs["L0_ffn_down_scale"].reshape(16, 128).T)
    return d


SOFTMAX_SCALE = 0.08838834764831845
NEG_SQRT_HD = -11.313708498984761


def emit_phase_a(ph, io, dbg=False):
    nc, blk = ph.nc, ph.blk
    TS = 512
    NT = OWN // TS

    sems = {k: ph.sem(k) for k in ["ld", "gd", "pre", "wt", "st", "mm", "v", "a",
                                   "wtB", "e4A", "e4B"]}
    s_ld, s_gd, s_pre, s_wt, s_st, s_mm, s_v, s_a = (
        sems["ld"], sems["gd"], sems["pre"], sems["wt"], sems["st"],
        sems["mm"], sems["v"], sems["a"])
    s_wtB, s_e4A, s_e4B = sems["wtB"], sems["e4A"], sems["e4B"]
    mk = {}

    ones1 = ph.sb("ones1", [1, 128], BF16)
    ones128 = ph.sb("ones128", [128, 1], BF16)
    identbf = ph.sb("identbf", [128, 128], BF16)
    ksb = [ph.sb(f"ksb{k}", [128, OWN], BF16) for k in range(NKV)]
    vfm = [ph.sb(f"vfm{k}", [128, OWN], BF16) for k in range(NKV)]
    vtm = [ph.sb(f"vtm{k}", [128, OWN], BF16) for k in range(NKV)]
    xat = ph.sb("xat", [128, 16 * TS], BF16)
    wstr = ph.sb("wstr", [128, 16 * 1024], BF16)
    lora = ph.sb("lora", [128, 16 * 16 * 3], BF16)
    lorb = ph.sb("lorb", [16, 512 * 2], BF16)
    qbsb = ph.sb("qbsb", [16, D], BF16)
    qgsb = ph.sb("qgsb", [1, 16], F32)
    mnegsb = ph.sb("mneg", [128, 16], F32)
    cmaskbf = ph.sb("cmaskbf", [128, 8 * TS], BF16)
    coskq = ph.sb("coskq", [32, 2 * L], F32)
    coskk = ph.sb("coskk", [32, 2 * OWN], F32)
    rowq = ph.sb("rowq", [1, 4 * TS], F32)
    rowbfq = ph.sb("rowbfq", [1, TS], BF16)
    t16 = ph.sb("t16", [16, TS], BF16)
    qlt = ph.sb("qlt", [16, 8 * TS], BF16)
    ev32 = ph.sb("ev32", [128, TS], BF16)
    sqst = ph.sb("sqst", [128, TS], BF16)
    ropesc = ph.sb("ropesc", [32, 4 * TS], F32)
    qout = ph.sb("qout", [128, TS], BF16)
    qtile = ph.sb("qtile", [128, 2 * TS], BF16)
    est = ph.sb("est", [128, 3 * TS], BF16)
    denbf = ph.sb("denbf", [1, TS], BF16)
    pvbf = ph.sb("pvbf", [128, TS], BF16)

    p_kv = ph.ps("pkv", [128, TS])
    p_16 = ph.ps("p16", [16, TS])
    p_row = ph.ps("prow", [1, TS])
    p_rep = ph.ps("prep", [128, TS])
    p_tr = ph.ps("ptr", [128, 128], BF16)
    p_st = [ph.ps(f"pst{k}", [128, TS]) for k in range(2)]
    p_pv = p_rep
    p_den = p_row

    @blk.sync
    def _(sync):
        s_pre.inc(sync.dma_start(out=qgsb[:], in_=io["qg"]), 16)
        s_pre.inc(sync.dma_start(out=coskq[:, 0:L], in_=io["cosq"]), 16)
        s_pre.inc(sync.dma_start(out=coskq[:, L:2 * L], in_=io["sinq"]), 16)
        s_pre.inc(sync.dma_start(out=coskk[:, 0:OWN], in_=io["cosk"]), 16)
        s_pre.inc(sync.dma_start(out=coskk[:, OWN:2 * OWN], in_=io["sink"]), 16)

    @blk.gpsimd
    def _(g):
        if "_cpool" in io:
            for ci, val in enumerate(CONSTS_F32):
                g.memset(io["_cpool"][:, ci:ci + 1], val)
                nc.const_aps.register(F32, val, io["_cpool"][:, ci:ci + 1])
        g.memset(ones1[:], 1.0)
        g.memset(ones128[:], 1.0)
        s_pre.inc(g.dma_start(out=identbf[:], in_=io["ident"]), 16)
        for c in range(16):
            s_pre.inc(g.dma_start(out=lora[:, c * 16:(c + 1) * 16],
                                  in_=io["ka"][c * 128:(c + 1) * 128, :]), 16)
            s_pre.inc(g.dma_start(out=lora[:, (16 + c) * 16:(17 + c) * 16],
                                  in_=io["va"][c * 128:(c + 1) * 128, :]), 16)
            s_pre.inc(g.dma_start(out=lora[:, (32 + c) * 16:(33 + c) * 16],
                                  in_=io["qa"][c * 128:(c + 1) * 128, :]), 16)
        s_pre.inc(g.dma_start(out=lorb[:, 0:512], in_=io["kb"]), 16)
        s_pre.inc(g.dma_start(out=lorb[:, 512:1024], in_=io["vb"]), 16)
        s_pre.inc(g.dma_start(out=qbsb[:], in_=io["qb"]), 16)
        for m in range(8):
            s_pre.inc(g.dma_start(out=cmaskbf[:, m * TS:(m + 1) * TS],
                                  in_=io["cmask"][m * 128:(m + 1) * 128, :]), 16)
    mk["pre"] = s_pre.v

    @blk.vector
    def _(v):
        v.wait_ge(s_pre.h, mk["pre"])
        v.tensor_single_scalar(rowq[0:1, 0:16], qgsb[:], -1.0, ALU.mult)
        v.tensor_max(rowq[0:1, 0:16], rowq[0:1, 0:16], qgsb[:])
        s_v.inc(v.tensor_single_scalar(rowq[0:1, 0:16], rowq[0:1, 0:16],
                                       NEG_SQRT_HD, ALU.mult), 1)
    mk["mnegv"] = s_v.v

    @blk.gpsimd
    def _(g):
        g.wait_ge(s_v.h, mk["mnegv"])
        s_gd.inc(g.dma_start(out=io["mneg_d"], in_=rowq[0:1, 0:16]), 16)
        g.wait_ge(s_gd.h, s_gd.v)
        s_pre.inc(g.dma_start(out=mnegsb[:],
                              in_=io["mneg_d"].partition_broadcast(128)), 16)
    mk["mnegbc"] = s_pre.v

    def rope_apply(v, dst, cof, sif):
        x1 = dst[0:32, :]
        x2 = dst[32:64, :]
        v.tensor_copy(ropesc[:, 0:TS], x2)
        v.tensor_mul(ropesc[:, TS:2 * TS], x1, cof)
        v.tensor_mul(ropesc[:, 2 * TS:3 * TS], ropesc[:, 0:TS], sif)
        v.tensor_mul(ropesc[:, 3 * TS:4 * TS], ropesc[:, 0:TS], cof)
        v.tensor_mul(ropesc[:, 0:TS], x1, sif)
        v.tensor_add(x1, ropesc[:, TS:2 * TS], ropesc[:, 2 * TS:3 * TS])
        v.tensor_sub(ropesc[:, TS:2 * TS], ropesc[:, 3 * TS:4 * TS],
                     ropesc[:, 0:TS])
        return v.tensor_copy(x2, ropesc[0:32, TS:2 * TS])

    @blk.gpsimd
    def _(g):
        g.wait_ge(s_pre.h, mk["pre"])
        for c in range(16):
            s_wt.inc(g.dma_start(out=wstr[:, c * 1024:c * 1024 + 512],
                                 in_=io["wk"][c * 128:(c + 1) * 128, :]), 16)
            s_wt.inc(g.dma_start(out=wstr[:, c * 1024 + 512:(c + 1) * 1024],
                                 in_=io["wv"][c * 128:(c + 1) * 128, :]), 16)
    mk[("a1", "wld")] = s_wt.v

    for ti in range(NT):
        @blk.gpsimd
        def _(g, ti=ti):
            if ti > 0:
                g.wait_ge(s_mm.h, mk[("a1", "xaused", ti - 1)])
            for c in range(16):
                s_gd.inc(g.dma_start(
                    out=xat[:, c * TS:(c + 1) * TS],
                    in_=io["xa_own"][c * 128:(c + 1) * 128, ti * TS:(ti + 1) * TS]), 16)
        mk[("a1", "xald", ti)] = s_gd.v

        for mat in range(2):
            @blk.tensor
            def _(t, ti=ti, mat=mat):
                if ti == 0 and mat == 0:
                    t.wait_ge(s_wt.h, mk[("a1", "wld")])
                t.wait_ge(s_gd.h, mk[("a1", "xald", ti)])
                if ("a1", "t16ev") in mk:
                    t.wait_ge(s_v.h, mk[("a1", "t16ev")])
                last = None
                for c in range(16):
                    last = t.matmul(p_16[:],
                                    lora[:, (mat * 16 + c) * 16:(mat * 16 + c + 1) * 16],
                                    xat[:, c * TS:(c + 1) * TS],
                                    start=(c == 0), stop=(c == 15))
                s_mm.inc(last, 1)
            mk[("a1", "t16mm", ti, mat)] = s_mm.v

            @blk.vector
            def _(v, ti=ti, mat=mat):
                v.wait_ge(s_mm.h, mk[("a1", "t16mm", ti, mat)])
                s_v.inc(v.tensor_copy(t16[:], p_16[:]), 1)
            mk[("a1", "t16ev")] = s_v.v
            mk[("a1", "t16", ti, mat)] = s_v.v

            for kvh in range(NKV):
                @blk.tensor
                def _(t, ti=ti, mat=mat, kvh=kvh):
                    if ("a1", "evprev") in mk:
                        t.wait_ge(s_a.h, mk[("a1", "evprev")])
                    for c in range(16):
                        t.matmul(p_kv[:],
                                 wstr[:, c * 1024 + mat * 512 + kvh * 128:
                                      c * 1024 + mat * 512 + (kvh + 1) * 128],
                                 xat[:, c * TS:(c + 1) * TS],
                                 start=(c == 0), stop=False)
                    t.wait_ge(s_v.h, mk[("a1", "t16", ti, mat)])
                    last = t.matmul(p_kv[:],
                                    lorb[:, mat * 512 + kvh * 128:
                                         mat * 512 + (kvh + 1) * 128],
                                    t16[:], start=False, stop=True)
                    s_mm.inc(last, 1)
                mk[("a1", "mm", ti, mat, kvh)] = s_mm.v
                if kvh == NKV - 1 and mat == 1:
                    mk[("a1", "xaused", ti)] = s_mm.v

                if mat == 0:
                    @blk.scalar
                    def _(s, ti=ti, kvh=kvh):
                        s.wait_ge(s_mm.h, mk[("a1", "mm", ti, 0, kvh)])
                        if ("sqused",) in mk:
                            s.wait_ge(s_mm.h, mk[("sqused",)])
                        if ("ev32used",) in mk:
                            s.wait_ge(s_v.h, mk[("ev32used",)])
                        s.activation(sqst[:], p_kv[:], AF.Square)
                        s_a.inc(s.copy(ev32[:], p_kv[:]), 1)
                    mk[("a1", "kev", ti, kvh)] = s_a.v
                    mk[("a1", "evprev")] = s_a.v

                    @blk.tensor
                    def _(t, ti=ti, kvh=kvh):
                        t.wait_ge(s_a.h, mk[("a1", "kev", ti, kvh)])
                        last = t.matmul(p_row[:], ones128[:], sqst[:],
                                        start=True, stop=True)
                        s_mm.inc(last, 1)
                    mk[("a1", "ksqs", ti, kvh)] = s_mm.v
                    mk[("sqused",)] = s_mm.v

                    @blk.scalar
                    def _(s, ti=ti, kvh=kvh):
                        s.wait_ge(s_mm.h, mk[("a1", "ksqs", ti, kvh)])
                        if ("rbfprev",) in mk:
                            s.wait_ge(s_v.h, mk[("rbfprev",)])
                        s_a.inc(s.activation(rowq[0:1, TS:2 * TS], p_row[:], AF.Ln,
                                             scale=1.0 / 128.0, bias=F32EPS), 1)
                        s.wait_ge(s_a.h, s_a.v)
                        s_a.inc(s.activation(rowq[0:1, TS:2 * TS],
                                             rowq[0:1, TS:2 * TS],
                                             AF.Exp, scale=-0.5), 1)
                    mk[("a1", "krst", ti, kvh)] = s_a.v

                    @blk.vector
                    def _(v, ti=ti, kvh=kvh):
                        v.wait_ge(s_a.h, mk[("a1", "krst", ti, kvh)])
                        if ("repprev",) in mk:
                            v.wait_ge(s_mm.h, mk[("repprev",)])
                        s_v.inc(v.tensor_copy(rowbfq[:], rowq[0:1, TS:2 * TS]), 1)
                    mk[("a1", "krbf", ti, kvh)] = s_v.v
                    mk[("rbfprev",)] = s_v.v

                    @blk.tensor
                    def _(t, ti=ti, kvh=kvh):
                        t.wait_ge(s_v.h, mk[("a1", "krbf", ti, kvh)])
                        last = t.matmul(p_rep[:], ones1[:], rowbfq[:],
                                        start=True, stop=True)
                        s_mm.inc(last, 1)
                    mk[("a1", "krep", ti, kvh)] = s_mm.v
                    mk[("repprev",)] = s_mm.v

                    @blk.vector
                    def _(v, ti=ti, kvh=kvh):
                        v.wait_ge(s_mm.h, mk[("a1", "krep", ti, kvh)])
                        dst = ksb[kvh][:, ti * TS:(ti + 1) * TS]
                        v.tensor_mul(dst, ev32[:], p_rep[:])
                        last = rope_apply(v, dst,
                                          coskk[:, ti * TS:(ti + 1) * TS],
                                          coskk[:, OWN + ti * TS:OWN + (ti + 1) * TS])
                        s_v.inc(last, 1)
                    mk[("a1", "kdone", ti, kvh)] = s_v.v
                    mk[("ev32used",)] = s_v.v
                else:
                    @blk.scalar
                    def _(s, ti=ti, kvh=kvh):
                        s.wait_ge(s_mm.h, mk[("a1", "mm", ti, 1, kvh)])
                        last = s.copy(vfm[kvh][:, ti * TS:(ti + 1) * TS], p_kv[:])
                        s_a.inc(last, 1)
                    mk[("a1", "vev", ti, kvh)] = s_a.v
                    mk[("a1", "evprev")] = s_a.v

                    @blk.gpsimd
                    def _(g, ti=ti, kvh=kvh):
                        g.wait_ge(s_a.h, mk[("a1", "vev", ti, kvh)])
                        s_st.inc(g.dma_start(
                            out=io["vfm_d"][kvh * 128:(kvh + 1) * 128,
                                            ti * TS:(ti + 1) * TS],
                            in_=vfm[kvh][:, ti * TS:(ti + 1) * TS]), 16)
                    mk[("a1", "vst", ti, kvh)] = s_st.v

                    for jb in range(4):
                        @blk.tensor
                        def _(t, ti=ti, kvh=kvh, jb=jb):
                            t.wait_ge(s_a.h, mk[("a1", "vev", ti, kvh)])
                            if ("a1", "trev") in mk:
                                t.wait_ge(s_v.h, mk[("a1", "trev")])
                            last = t.transpose(
                                p_tr[:],
                                vfm[kvh][:, ti * TS + jb * 128:
                                         ti * TS + (jb + 1) * 128],
                                identbf[:])
                            s_mm.inc(last, 1)
                        mk[("a1", "tr", ti, kvh, jb)] = s_mm.v

                        @blk.vector
                        def _(v, ti=ti, kvh=kvh, jb=jb):
                            v.wait_ge(s_mm.h, mk[("a1", "tr", ti, kvh, jb)])
                            last = v.tensor_copy(
                                vtm[kvh][:, ti * TS + jb * 128:
                                         ti * TS + (jb + 1) * 128],
                                p_tr[:])
                            s_v.inc(last, 1)
                        mk[("a1", "trev")] = s_v.v

    for hh in range(2):
        @blk.gpsimd
        def _(g, hh=hh):
            if hh == 0:
                g.wait_ge(s_mm.h, mk[("a1", "xaused", NT - 1)])
            else:
                g.wait_ge(s_mm.h, mk[("a2", "wqused", 0)])
            for c in range(16):
                s_wt.inc(g.dma_start(
                    out=wstr[:, c * 1024:(c + 1) * 1024],
                    in_=io["wq"][c * 128:(c + 1) * 128,
                                 hh * 1024:(hh + 1) * 1024]), 16)
        mk[("a2", "wld", hh)] = s_wt.v

        for r in range(2):
            for ti in range(NT):
                @blk.gpsimd
                def _(g, hh=hh, r=r, ti=ti):
                    if not (hh == 0 and r == 0 and ti == 0):
                        g.wait_ge(s_mm.h, mk[("a2", "xaused_prev")])
                    elif ("a1", "xaused", NT - 1) in mk:
                        g.wait_ge(s_mm.h, mk[("a1", "xaused", NT - 1)])
                    for c in range(16):
                        s_gd.inc(g.dma_start(
                            out=xat[:, c * TS:(c + 1) * TS],
                            in_=io["xa_all"][r * D + c * 128:r * D + (c + 1) * 128,
                                             ti * TS:(ti + 1) * TS]), 16)
                mk[("a2", "xald", hh, r, ti)] = s_gd.v

                if hh == 0:
                    @blk.tensor
                    def _(t, r=r, ti=ti):
                        t.wait_ge(s_gd.h, mk[("a2", "xald", 0, r, ti)])
                        if ("a2", "qltev") in mk:
                            t.wait_ge(s_v.h, mk[("a2", "qltev")])
                        last = None
                        for c in range(16):
                            last = t.matmul(p_16[:],
                                            lora[:, (32 + c) * 16:(33 + c) * 16],
                                            xat[:, c * TS:(c + 1) * TS],
                                            start=(c == 0), stop=(c == 15))
                        s_mm.inc(last, 1)
                    mk[("a2", "qlmm", r, ti)] = s_mm.v

                    @blk.vector
                    def _(v, r=r, ti=ti):
                        v.wait_ge(s_mm.h, mk[("a2", "qlmm", r, ti)])
                        s_v.inc(v.tensor_copy(
                            qlt[:, (r * NT + ti) * TS:(r * NT + ti + 1) * TS],
                            p_16[:]), 1)
                    mk[("a2", "qltev")] = s_v.v
                    mk[("a2", "qlt", r, ti)] = s_v.v

                for h4 in range(8):
                    h = hh * 8 + h4

                    @blk.tensor
                    def _(t, hh=hh, r=r, ti=ti, h4=h4, h=h):
                        if h4 == 0:
                            t.wait_ge(s_wt.h, mk[("a2", "wld", hh)])
                        if ("a2", "qevprev") in mk:
                            t.wait_ge(s_a.h, mk[("a2", "qevprev")])
                        if h4 == 0:
                            t.wait_ge(s_v.h, mk[("a2", "qlt", r, ti)])
                        for c in range(16):
                            t.matmul(p_kv[:],
                                     wstr[:, c * 1024 + h4 * 128:
                                          c * 1024 + (h4 + 1) * 128],
                                     xat[:, c * TS:(c + 1) * TS],
                                     start=(c == 0), stop=False)
                        last = t.matmul(p_kv[:],
                                        qbsb[:, h * 128:(h + 1) * 128],
                                        qlt[:, (r * NT + ti) * TS:
                                            (r * NT + ti + 1) * TS],
                                        start=False, stop=True)
                        s_mm.inc(last, 1)
                        if h4 == 7:
                            mk[("a2", "wqused", hh)] = s_mm.v
                            mk[("a2", "xaused_prev")] = s_mm.v
                    mk[("a2", "qmm", hh, r, ti, h4)] = s_mm.v

                    @blk.scalar
                    def _(s, hh=hh, r=r, ti=ti, h4=h4):
                        s.wait_ge(s_mm.h, mk[("a2", "qmm", hh, r, ti, h4)])
                        if ("sqused",) in mk:
                            s.wait_ge(s_mm.h, mk[("sqused",)])
                        if ("ev32used",) in mk:
                            s.wait_ge(s_v.h, mk[("ev32used",)])
                        s.activation(sqst[:], p_kv[:], AF.Square)
                        s_a.inc(s.copy(ev32[:], p_kv[:]), 1)
                        mk[("a2", "qevprev")] = s_a.v
                    mk[("a2", "qev", hh, r, ti, h4)] = s_a.v

                    @blk.tensor
                    def _(t, hh=hh, r=r, ti=ti, h4=h4):
                        t.wait_ge(s_a.h, mk[("a2", "qev", hh, r, ti, h4)])
                        last = t.matmul(p_row[:], ones128[:], sqst[:],
                                        start=True, stop=True)
                        s_mm.inc(last, 1)
                        mk[("sqused",)] = s_mm.v
                    mk[("a2", "qsqs", hh, r, ti, h4)] = s_mm.v

                    @blk.scalar
                    def _(s, hh=hh, r=r, ti=ti, h4=h4):
                        s.wait_ge(s_mm.h, mk[("a2", "qsqs", hh, r, ti, h4)])
                        if ("rbfprev",) in mk:
                            s.wait_ge(s_v.h, mk[("rbfprev",)])
                        s_a.inc(s.activation(rowq[0:1, TS:2 * TS], p_row[:], AF.Ln,
                                             scale=1.0 / 128.0, bias=F32EPS), 1)
                        s.wait_ge(s_a.h, s_a.v)
                        s_a.inc(s.activation(rowq[0:1, TS:2 * TS],
                                             rowq[0:1, TS:2 * TS],
                                             AF.Exp, scale=-0.5), 1)
                    mk[("a2", "qrst", hh, r, ti, h4)] = s_a.v

                    @blk.vector
                    def _(v, hh=hh, r=r, ti=ti, h4=h4, h=h):
                        v.wait_ge(s_a.h, mk[("a2", "qrst", hh, r, ti, h4)])
                        if ("repprev",) in mk:
                            v.wait_ge(s_mm.h, mk[("repprev",)])
                        v.tensor_single_scalar(rowq[0:1, TS:2 * TS],
                                               rowq[0:1, TS:2 * TS],
                                               qgsb[0:1, h:h + 1], ALU.mult)
                        s_v.inc(v.tensor_copy(rowbfq[:], rowq[0:1, TS:2 * TS]), 1)
                    mk[("a2", "qrbf", hh, r, ti, h4)] = s_v.v
                    mk[("rbfprev",)] = s_v.v

                    @blk.tensor
                    def _(t, hh=hh, r=r, ti=ti, h4=h4):
                        t.wait_ge(s_v.h, mk[("a2", "qrbf", hh, r, ti, h4)])
                        last = t.matmul(p_rep[:], ones1[:], rowbfq[:],
                                        start=True, stop=True)
                        s_mm.inc(last, 1)
                    mk[("a2", "qrep", hh, r, ti, h4)] = s_mm.v
                    mk[("repprev",)] = s_mm.v

                    @blk.vector
                    def _(v, hh=hh, r=r, ti=ti, h4=h4):
                        v.wait_ge(s_mm.h, mk[("a2", "qrep", hh, r, ti, h4)])
                        if ("a2", "qoutst") in mk:
                            v.wait_ge(s_st.h, mk[("a2", "qoutst")])
                        v.tensor_mul(qout[:], ev32[:], p_rep[:])
                        last = rope_apply(
                            v, qout[:],
                            coskq[:, r * OWN + ti * TS:r * OWN + (ti + 1) * TS],
                            coskq[:, L + r * OWN + ti * TS:L + r * OWN + (ti + 1) * TS])
                        s_v.inc(last, 1)
                    mk[("a2", "qrope", hh, r, ti, h4)] = s_v.v
                    mk[("ev32used",)] = s_v.v

                    @blk.gpsimd
                    def _(g, hh=hh, r=r, ti=ti, h4=h4, h=h):
                        g.wait_ge(s_v.h, mk[("a2", "qrope", hh, r, ti, h4)])
                        s_st.inc(g.dma_start(
                            out=io["qd"][h * 128:(h + 1) * 128,
                                         r * OWN + ti * TS:r * OWN + (ti + 1) * TS],
                            in_=qout[:]), 16)
                        mk[("a2", "qoutst")] = s_st.v
                    mk[("a2", "qst", hh, r, ti, h4)] = s_st.v

    gq = 0
    for h in range(NH):
        kvh = h // 4
        for r in range(2):
            for jt in range(NT):
                nblk = 4 * jt + 4
                slot = gq % 2

                @blk.gpsimd
                def _(g, h=h, r=r, jt=jt, slot=slot, gq=gq):
                    if gq == 0:
                        g.wait_ge(s_st.h, mk[("a2", "qst", 1, 1, NT - 1, 7)])
                    if gq >= 2:
                        g.wait_ge(s_mm.h, mk[("a3", "qused", gq - 2)])
                    s_gd.inc(g.dma_start(
                        out=qtile[:, slot * TS:(slot + 1) * TS],
                        in_=io["qd"][h * 128:(h + 1) * 128,
                                     r * OWN + jt * TS:r * OWN + (jt + 1) * TS]), 16)
                mk[("a3", "qld", gq)] = s_gd.v

                for i in range(nblk):
                    @blk.tensor
                    def _(t, h=h, r=r, jt=jt, i=i, slot=slot, gq=gq, kvh=kvh,
                          nblk=nblk):
                        if i == 0:
                            t.wait_ge(s_gd.h, mk[("a3", "qld", gq)])
                            if gq == 0:
                                t.wait_ge(s_pre.h, mk["mnegbc"])
                                t.wait_ge(s_v.h, mk[("a1", "kdone", NT - 1, NKV - 1)])
                                t.wait_ge(s_v.h, mk[("a1", "trev")])
                        if ("a3", "exp", gq, i - 2) in mk:
                            t.wait_ge(s_a.h, mk[("a3", "exp", gq, i - 2)])
                        if i < 2 and gq > 0:
                            t.wait_ge(s_a.h, mk[("a3", "lastexp", gq - 1)])
                        last = t.matmul(p_st[i % 2][:],
                                        ksb[kvh][:, i * 128:(i + 1) * 128],
                                        qtile[:, slot * TS:(slot + 1) * TS],
                                        start=True, stop=True)
                        s_mm.inc(last, 1)
                        if i == nblk - 1:
                            mk[("a3", "qused", gq)] = s_mm.v
                    mk[("a3", "st", gq, i)] = s_mm.v

                    @blk.scalar
                    def _(s, h=h, i=i, gq=gq, nblk=nblk):
                        s.wait_ge(s_mm.h, mk[("a3", "st", gq, i)])
                        if ("a3", "dpv", gq, i - 2) in mk:
                            s.wait_ge(s_mm.h, mk[("a3", "dpv", gq, i - 2)])
                        elif i < 2 and gq > 0:
                            s.wait_ge(s_mm.h, mk[("a3", "lastdpv", gq - 1)])
                        last = s.activation(est[:, (i % 2) * TS:(i % 2 + 1) * TS],
                                            p_st[i % 2][:], AF.Exp,
                                            scale=SOFTMAX_SCALE,
                                            bias=mnegsb[:, h:h + 1])
                        s_a.inc(last, 1)
                        if i == nblk - 1:
                            mk[("a3", "lastexp", gq)] = s_a.v
                    mk[("a3", "exp", gq, i)] = s_a.v

                    e = i - 4 * jt
                    if e >= 0:
                        @blk.vector
                        def _(v, r=r, i=i, e=e, gq=gq):
                            v.wait_ge(s_a.h, mk[("a3", "exp", gq, i)])
                            if ("a3", "mask", gq, i - 1) in mk:
                                v.wait_ge(s_mm.h, mk[("a3", "dpv", gq, i - 1)])
                            elif gq > 0:
                                v.wait_ge(s_mm.h, mk[("a3", "lastdpv", gq - 1)])
                            last = v.tensor_mul(est[:, 2 * TS:3 * TS],
                                                est[:, (i % 2) * TS:(i % 2 + 1) * TS],
                                                cmaskbf[:, (r * 4 + e) * TS:
                                                        (r * 4 + e + 1) * TS])
                            s_v.inc(last, 1)
                        mk[("a3", "mask", gq, i)] = s_v.v

                    @blk.tensor
                    def _(t, i=i, e=e, gq=gq, kvh=kvh, nblk=nblk):
                        if e >= 0:
                            t.wait_ge(s_v.h, mk[("a3", "mask", gq, i)])
                            src = est[:, 2 * TS:3 * TS]
                        else:
                            t.wait_ge(s_a.h, mk[("a3", "exp", gq, i)])
                            src = est[:, (i % 2) * TS:(i % 2 + 1) * TS]
                        if i == 0:
                            if ("a3", "pvev", gq - 1) in mk:
                                t.wait_ge(s_v.h, mk[("a3", "pvev", gq - 1)])
                                t.wait_ge(s_v.h, mk[("a3", "denev", gq - 1)])
                        t.matmul(p_den[:], ones128[:], src,
                                 start=(i == 0), stop=(i == nblk - 1))
                        last = t.matmul(p_pv[:],
                                        vtm[kvh][:, i * 128:(i + 1) * 128],
                                        src, start=(i == 0), stop=(i == nblk - 1))
                        s_mm.inc(last, 1)
                        if e >= 0:
                            mk[("a3", "mused", gq, i)] = s_mm.v
                        if i == nblk - 1:
                            mk[("a3", "lastdpv", gq)] = s_mm.v
                    mk[("a3", "dpv", gq, i)] = s_mm.v

                @blk.vector
                def _(v, gq=gq):
                    v.wait_ge(s_mm.h, mk[("a3", "lastdpv", gq)])
                    if ("a3", "denst", gq - 1) in mk:
                        v.wait_ge(s_st.h, mk[("a3", "denst", gq - 1)])
                    s_v.inc(v.tensor_copy(denbf[:], p_den[:]), 1)
                mk[("a3", "denev", gq)] = s_v.v

                @blk.vector
                def _(v, gq=gq):
                    v.wait_ge(s_mm.h, mk[("a3", "lastdpv", gq)])
                    if ("a3", "denst", gq - 1) in mk:
                        v.wait_ge(s_st.h, mk[("a3", "denst", gq - 1)])
                    s_v.inc(v.tensor_copy(pvbf[:], p_pv[:]), 1)
                mk[("a3", "pvev", gq)] = s_v.v

                @blk.gpsimd
                def _(g, h=h, r=r, jt=jt, gq=gq):
                    g.wait_ge(s_v.h, mk[("a3", "denev", gq)])
                    g.wait_ge(s_v.h, mk[("a3", "pvev", gq)])
                    s_st.inc(g.dma_start(
                        out=io["pv_d"][r * 2064 + h * 128:r * 2064 + (h + 1) * 128,
                                       jt * TS:(jt + 1) * TS],
                        in_=pvbf[:]), 16)
                    s_st.inc(g.dma_start(
                        out=io["pv_d"][r * 2064 + 2048 + h:r * 2064 + 2048 + h + 1,
                                       jt * TS:(jt + 1) * TS],
                        in_=denbf[:]), 16)
                mk[("a3", "denst", gq)] = s_st.v
                gq += 1

    @blk.sync
    def _(sync):
        sync.wait_ge(s_st.h, mk[("a3", "denst", gq - 1)])
        sync.wait_ge(s_st.h, mk[("a1", "vst", NT - 1, NKV - 1)])

    return mk, sems


def declare_a_io(nc, dbg=False):
    io = {}
    io["xa_own"] = nc.dram_tensor("xa_own", [D, OWN], BF16, kind="ExternalInput").ap()
    io["xa_all"] = nc.dram_tensor("xa_all", [2 * D, OWN], BF16, kind="ExternalInput").ap()
    io["wq"] = nc.dram_tensor("wq", [D, D], F32, kind="ExternalInput").ap()
    io["wk"] = nc.dram_tensor("wk", [D, KVD], F32, kind="ExternalInput").ap()
    io["wv"] = nc.dram_tensor("wv", [D, KVD], F32, kind="ExternalInput").ap()
    io["qa"] = nc.dram_tensor("qa", [D, 16], F32, kind="ExternalInput").ap()
    io["qb"] = nc.dram_tensor("qb", [16, D], F32, kind="ExternalInput").ap()
    io["ka"] = nc.dram_tensor("ka", [D, 16], F32, kind="ExternalInput").ap()
    io["kb"] = nc.dram_tensor("kb", [16, KVD], F32, kind="ExternalInput").ap()
    io["va"] = nc.dram_tensor("va", [D, 16], F32, kind="ExternalInput").ap()
    io["vb"] = nc.dram_tensor("vb", [16, KVD], F32, kind="ExternalInput").ap()
    io["qg"] = nc.dram_tensor("qg", [1, 16], F32, kind="ExternalInput").ap()
    io["cosq"] = nc.dram_tensor("cosq", [32, L], F32, kind="ExternalInput").ap()
    io["sinq"] = nc.dram_tensor("sinq", [32, L], F32, kind="ExternalInput").ap()
    io["cosk"] = nc.dram_tensor("cosk", [32, OWN], F32, kind="ExternalInput").ap()
    io["sink"] = nc.dram_tensor("sink", [32, OWN], F32, kind="ExternalInput").ap()
    io["cmask"] = nc.dram_tensor("cmask", [8 * 128, 512], F32, kind="ExternalInput").ap()
    io["ident"] = nc.dram_tensor("ident", [128, 128], F32, kind="ExternalInput").ap()
    io["pv_d"] = nc.dram_tensor("pv_d", [2 * 2064, OWN], BF16, kind="ExternalOutput").ap()
    io["vfm_d"] = nc.dram_tensor("vfm_d", [KVD, OWN], BF16, kind="ExternalOutput").ap()
    kd = "ExternalOutput" if dbg else "Internal"
    io["qd"] = nc.dram_tensor("qd", [D, 2 * OWN], BF16, kind=kd).ap()
    io["mneg_d"] = nc.dram_tensor("mneg_d", [1, 16], F32, kind="Internal").ap()
    return io


def build_phase_a(dbg=False):
    nc = bass.Bass("TRN2", target_bir_lowering=False, debug=False)
    io = declare_a_io(nc, dbg)
    with contextlib.ExitStack() as st:
        with nc.Block() as blk:
            ph = Ph(nc, blk, st, "a_")
            io["_cpool"] = ph.sb("cpool", [128, len(CONSTS_F32)], F32)[:]
            emit_phase_a(ph, io, dbg)
    return nc


def prep_a_inputs(inputs, cid, xa_pair):
    b, par = cid // 2, cid % 2
    d = {}
    d["xa_own"] = np.ascontiguousarray(np.asarray(xa_pair[par]))
    d["xa_all"] = np.ascontiguousarray(
        np.concatenate([np.asarray(xa_pair[0]), np.asarray(xa_pair[1])], axis=0))
    d["wq"] = inputs["L1_attn_Wq"]
    d["wk"] = inputs["L1_attn_Wk"]
    d["wv"] = inputs["L1_attn_Wv"]
    d["qa"] = inputs["L1_attn_q_A"]
    d["qb"] = inputs["L1_attn_q_B"]
    d["ka"] = inputs["L1_attn_k_A"]
    d["kb"] = inputs["L1_attn_k_B"]
    d["va"] = inputs["L1_attn_v_A"]
    d["vb"] = inputs["L1_attn_v_B"]
    d["qg"] = np.ascontiguousarray(inputs["L1_attn_q_gain"].reshape(1, 16))
    inv = 1.0 / 10000.0 ** (np.arange(0, 64, 2, dtype=np.float64) / 64.0)
    lq = np.concatenate([own_l_index(0), own_l_index(1)])
    fq = lq[None, :].astype(np.float64) * inv[:, None]
    d["cosq"] = np.cos(fq).astype(np.float32)
    d["sinq"] = np.sin(fq).astype(np.float32)
    lk = own_l_index(par)
    fk = lk[None, :].astype(np.float64) * inv[:, None]
    d["cosk"] = np.cos(fk).astype(np.float32)
    d["sink"] = np.sin(fk).astype(np.float32)
    cm = np.zeros((2, 4, 128, 512), np.float32)
    kk = np.arange(128)[:, None]
    qq = np.arange(128)[None, :]
    for r in range(2):
        for e in range(4):
            for m in range(4):
                a = 2 * e + par
                bq = 2 * m + r
                if a < bq:
                    cm[r, e, :, m * 128:(m + 1) * 128] = 1.0
                elif a == bq:
                    cm[r, e, :, m * 128:(m + 1) * 128] = (kk <= qq)
    d["cmask"] = cm.reshape(8 * 128, 512)
    d["ident"] = np.eye(128, dtype=np.float32)
    return d



def emit_phase_am(ph, io, dbg=False):
    nc, blk = ph.nc, ph.blk
    TS = 512
    NT = OWN // TS

    sems = {k: ph.sem(k) for k in ["ld", "gd", "pre", "wt", "st", "mm", "v", "a",
                                   "wtB", "e4A", "e4B"]}
    s_ld, s_gd, s_pre, s_wt, s_st, s_mm, s_v, s_a = (
        sems["ld"], sems["gd"], sems["pre"], sems["wt"], sems["st"],
        sems["mm"], sems["v"], sems["a"])
    s_wtB, s_e4A, s_e4B = sems["wtB"], sems["e4A"], sems["e4B"]
    mk = {}

    ones1 = ph.sb("ones1", [1, 128], BF16)
    ones128 = ph.sb("ones128", [128, 1], BF16)
    denh = ph.sb("denh", [1, 16 * TS], F32)
    recbf = ph.sb("recbf", [1, 16 * TS], BF16)
    denstg = ph.sb("denstg", [1, 2 * TS], BF16)
    pvstg = ph.sb("pvstg", [128, 2 * TS], BF16)
    pvsum = ph.sb("pvsum", [128, TS], BF16)
    vft = ph.sb("vft", [128, 4 * TS], BF16)
    vsqr = ph.sb("vsqr", [1, 4 * TS], F32)
    sqstg = ph.sb("sqstg", [128, TS], BF16)
    tpstg = ph.sb("tpstg", [128, TS], BF16)
    srowbf = ph.sb("srowbf", [1, TS], BF16)
    srow32 = ph.sb("srow32", [1, TS], F32)
    ypbuf = ph.sb("ypbuf", [128, 16 * TS], BF16)
    wstr = ph.sb("wstr", [128, 2 * 16 * 1024], BF16)
    r232 = ph.sb("r232", [128, 8 * TS], F32)
    res1t = ph.sb("res1t", [128, 16 * TS], F32)
    pbsb = ph.sb("pbsb", [16, D], BF16)
    pasb = ph.sb("pasb", [128, 16 * 16], BF16)

    p_16 = ph.ps("p16", [16, TS])
    p_row = ph.ps("prow", [1, TS])
    p_rep = ph.ps("prep", [128, TS])
    p_o = [ph.ps(f"po{k}", [128, TS]) for k in range(4)]

    @blk.gpsimd
    def _(g):
        if "_cpool" in io:
            for ci, val in enumerate(CONSTS_F32):
                g.memset(io["_cpool"][:, ci:ci + 1], val)
                nc.const_aps.register(F32, val, io["_cpool"][:, ci:ci + 1])
        g.memset(ones1[:], 1.0)
        g.memset(ones128[:], 1.0)
        for c in range(16):
            s_pre.inc(g.dma_start(out=pasb[:, c * 16:(c + 1) * 16],
                                  in_=io["pa"][c * 128:(c + 1) * 128, :]), 16)
        s_pre.inc(g.dma_start(out=pbsb[:], in_=io["pb"]), 16)
    mk["pre"] = s_pre.v

    for ti in range(NT):
        @blk.sync
        def _(sync, ti=ti):
            if ti > 0:
                sync.wait_ge(s_st.h, mk[("am", "r2st", ti - 1, 1)])
            for c in range(16):
                s_ld.inc(sync.dma_start(
                    out=res1t[:, c * TS:(c + 1) * TS],
                    in_=io["res1"][c * 128:(c + 1) * 128, ti * TS:(ti + 1) * TS]), 16)
        mk[("am", "r1ld", ti)] = s_ld.v

        @blk.gpsimd
        def _(g, ti=ti):
            if ti > 0:
                g.wait_ge(s_v.h, mk[("am", "vfused", ti - 1)])
            for k in range(NKV):
                s_gd.inc(g.dma_start(
                    out=vft[:, k * TS:(k + 1) * TS],
                    in_=io["vfm_d"][k * 128:(k + 1) * 128, ti * TS:(ti + 1) * TS]), 16)
        mk[("am", "vfld", ti)] = s_gd.v

        for h in range(16):
            @blk.gpsimd
            def _(g, ti=ti, h=h):
                if not (ti == 0 and h == 0):
                    ph_, pt = (h - 1, ti) if h else (15, ti - 1)
                    g.wait_ge(s_v.h, mk[("am", "denv", pt, ph_)])
                s_gd.inc(g.dma_start(
                    out=denstg[0:1, 0:TS],
                    in_=io["pv_a"][2048 + h:2049 + h, ti * TS:(ti + 1) * TS]), 16)
                if io.get("pv_b") is not None:
                    s_gd.inc(g.dma_start(
                        out=denstg[0:1, TS:2 * TS],
                        in_=io["pv_b"][2048 + h:2049 + h, ti * TS:(ti + 1) * TS]), 16)
            mk[("am", "denld", ti, h)] = s_gd.v

            @blk.vector
            def _(v, ti=ti, h=h):
                v.wait_ge(s_gd.h, mk[("am", "denld", ti, h)])
                if ti > 0 and h == 0:
                    v.wait_ge(s_mm.h, mk[("am", "recused", ti - 1)])
                if io.get("pv_b") is not None:
                    s_v.inc(v.tensor_add(denh[0:1, h * TS:(h + 1) * TS],
                                         denstg[0:1, 0:TS],
                                         denstg[0:1, TS:2 * TS]), 1)
                else:
                    s_v.inc(v.tensor_copy(denh[0:1, h * TS:(h + 1) * TS],
                                          denstg[0:1, 0:TS]), 1)
            mk[("am", "denv", ti, h)] = s_v.v

        @blk.vector
        def _(v, ti=ti):
            v.wait_ge(s_v.h, mk[("am", "denv", ti, 15)])
            v.tensor_scalar_add(denh[:], denh[:], 1e-30)
            v.reciprocal(denh[:], denh[:])
            s_v.inc(v.tensor_copy(recbf[:], denh[:]), 1)
        mk[("am", "rec", ti)] = s_v.v

        for k in range(NKV):
            @blk.scalar
            def _(s, ti=ti, k=k):
                s.wait_ge(s_gd.h, mk[("am", "vfld", ti)])
                if ("am", "sqused") in mk:
                    s.wait_ge(s_mm.h, mk[("am", "sqused")])
                s_a.inc(s.activation(sqstg[:], vft[:, k * TS:(k + 1) * TS],
                                     AF.Square), 1)
            mk[("am", "vsq", ti, k)] = s_a.v

            @blk.tensor
            def _(t, ti=ti, k=k):
                t.wait_ge(s_a.h, mk[("am", "vsq", ti, k)])
                if ("am", "vsqrv_prev") in mk:
                    t.wait_ge(s_v.h, mk[("am", "vsqrv_prev")])
                last = t.matmul(p_row[:], ones128[:], sqstg[:],
                                start=True, stop=True)
                s_mm.inc(last, 1)
                mk[("am", "sqused")] = s_mm.v
            mk[("am", "vsqs", ti, k)] = s_mm.v

            @blk.vector
            def _(v, ti=ti, k=k):
                v.wait_ge(s_mm.h, mk[("am", "vsqs", ti, k)])
                v.tensor_scalar_max(vsqr[0:1, k * TS:(k + 1) * TS], p_row[:], 1e-24)
                s_v.inc(v.reciprocal(vsqr[0:1, k * TS:(k + 1) * TS],
                                     vsqr[0:1, k * TS:(k + 1) * TS]), 1)
                mk[("am", "vsqrv_prev")] = s_v.v
            mk[("am", "vsqr", ti, k)] = s_v.v

        for h in range(16):
            kvh = h // 4

            @blk.gpsimd
            def _(g, ti=ti, h=h):
                if not (ti == 0 and h == 0):
                    ph_, pt = (h - 1, ti) if h else (15, ti - 1)
                    g.wait_ge(s_v.h, mk[("am", "pvused", pt, ph_)])
                s_gd.inc(g.dma_start(
                    out=pvstg[:, 0:TS],
                    in_=io["pv_a"][h * 128:(h + 1) * 128, ti * TS:(ti + 1) * TS]), 16)
                if io.get("pv_b") is not None:
                    s_gd.inc(g.dma_start(
                        out=pvstg[:, TS:2 * TS],
                        in_=io["pv_b"][h * 128:(h + 1) * 128,
                                       ti * TS:(ti + 1) * TS]), 16)
            mk[("am", "pvld", ti, h)] = s_gd.v

            @blk.vector
            def _(v, ti=ti, h=h):
                v.wait_ge(s_gd.h, mk[("am", "pvld", ti, h)])
                v.wait_ge(s_v.h, mk[("am", "rec", ti)])
                if io.get("pv_b") is not None:
                    s_v.inc(v.tensor_add(pvsum[:], pvstg[:, 0:TS],
                                         pvstg[:, TS:2 * TS]), 1)
                else:
                    s_v.inc(v.tensor_copy(pvsum[:], pvstg[:, 0:TS]), 1)
                mk[("am", "pvused", ti, h)] = s_v.v
            mk[("am", "pvv", ti, h)] = s_v.v

            @blk.tensor
            def _(t, ti=ti, h=h):
                t.wait_ge(s_v.h, mk[("am", "pvv", ti, h)])
                if ("am", "repused") in mk:
                    t.wait_ge(s_v.h, mk[("am", "repused")])
                last = t.matmul(p_rep[:], ones1[:],
                                recbf[0:1, h * TS:(h + 1) * TS],
                                start=True, stop=True)
                s_mm.inc(last, 1)
            mk[("am", "rep1", ti, h)] = s_mm.v
            mk[("am", "recused", ti)] = s_mm.v

            @blk.vector
            def _(v, ti=ti, h=h, kvh=kvh):
                v.wait_ge(s_mm.h, mk[("am", "rep1", ti, h)])
                yslot = ypbuf[:, h * TS:(h + 1) * TS]
                v.tensor_mul(yslot, pvsum[:], p_rep[:])
                s_v.inc(v.tensor_mul(tpstg[:], yslot,
                                     vft[:, kvh * TS:(kvh + 1) * TS]), 1)
                mk[("am", "repused")] = s_v.v
            mk[("am", "ymul", ti, h)] = s_v.v

            @blk.tensor
            def _(t, ti=ti, h=h):
                t.wait_ge(s_v.h, mk[("am", "ymul", ti, h)])
                last = t.matmul(p_row[:], ones128[:], tpstg[:],
                                start=True, stop=True)
                s_mm.inc(last, 1)
                mk[("am", "sqused")] = s_mm.v
            mk[("am", "dot", ti, h)] = s_mm.v

            @blk.vector
            def _(v, ti=ti, h=h, kvh=kvh):
                v.wait_ge(s_mm.h, mk[("am", "dot", ti, h)])
                v.tensor_mul(srow32[:], p_row[:], vsqr[0:1, kvh * TS:(kvh + 1) * TS])
                s_v.inc(v.tensor_copy(srowbf[:], srow32[:]), 1)
            mk[("am", "srow", ti, h)] = s_v.v

            @blk.tensor
            def _(t, ti=ti, h=h):
                t.wait_ge(s_v.h, mk[("am", "srow", ti, h)])
                last = t.matmul(p_rep[:], ones1[:], srowbf[:],
                                start=True, stop=True)
                s_mm.inc(last, 1)
            mk[("am", "rep2", ti, h)] = s_mm.v

            @blk.vector
            def _(v, ti=ti, h=h, kvh=kvh):
                v.wait_ge(s_mm.h, mk[("am", "rep2", ti, h)])
                yslot = ypbuf[:, h * TS:(h + 1) * TS]
                v.tensor_mul(tpstg[:], vft[:, kvh * TS:(kvh + 1) * TS], p_rep[:])
                s_v.inc(v.tensor_sub(yslot, yslot, tpstg[:]), 1)
                mk[("am", "repused")] = s_v.v
                if h == 15:
                    mk[("am", "vfused", ti)] = s_v.v
            mk[("am", "yp", ti, h)] = s_v.v
        mk[("am", "ypall", ti)] = s_v.v

        @blk.tensor
        def _(t, ti=ti):
            t.wait_ge(s_v.h, mk[("am", "ypall", ti)])
            if ti == 0:
                t.wait_ge(s_pre.h, mk["pre"])
            if ("am", "t16used") in mk:
                t.wait_ge(s_v.h, mk[("am", "t16used")])
            last = None
            for c in range(16):
                last = t.matmul(p_16[:], pasb[:, c * 16:(c + 1) * 16],
                                ypbuf[:, c * TS:(c + 1) * TS],
                                start=(c == 0), stop=(c == 15))
            s_mm.inc(last, 1)
        mk[("am", "lmm", ti)] = s_mm.v

        @blk.vector
        def _(v, ti=ti):
            v.wait_ge(s_mm.h, mk[("am", "lmm", ti)])
            s_v.inc(v.tensor_copy(io["_t16b"][:], p_16[:]), 1)
            mk[("am", "t16used")] = s_v.v
        mk[("am", "lev", ti)] = s_v.v

        for oh in range(2):
            if ti == 0:
                @blk.gpsimd
                def _(g, oh=oh):
                    for half in range(2):
                        s_wt.inc(g.dma_start(
                            out=wstr[:, (oh * 2 + half) * 8192:
                                     (oh * 2 + half + 1) * 8192],
                            in_=io["wpr"][:, (oh * 2 + half) * 8192:
                                          (oh * 2 + half + 1) * 8192]), 16)
                mk[("am", "wld", oh)] = s_wt.v

            for q in range(2):
                @blk.tensor
                def _(t, ti=ti, oh=oh, q=q):
                    if q == 0:
                        if ti == 0:
                            t.wait_ge(s_wt.h, mk[("am", "wld", oh)])
                        t.wait_ge(s_v.h, mk[("am", "lev", ti)])
                    if ti > 0 or oh > 0 or q > 0:
                        pq = (ti, oh, 0) if q else ((ti, oh - 1, 1) if oh
                                                    else (ti - 1, 1, 1))
                        t.wait_ge(s_v.h, mk[("am", "evac", *pq)])
                    last = None
                    for ot4 in range(4):
                        ot = q * 4 + ot4
                        go = oh * 8 + ot
                        for c in range(16):
                            t.matmul(p_o[ot4][:],
                                     wstr[:, (oh * 16 + c) * 1024 + ot * 128:
                                          (oh * 16 + c) * 1024 + (ot + 1) * 128],
                                     ypbuf[:, c * TS:(c + 1) * TS],
                                     start=(c == 0), stop=False)
                        last = t.matmul(p_o[ot4][:],
                                        pbsb[:, go * 128:(go + 1) * 128],
                                        io["_t16b"][:], start=False, stop=True)
                    s_mm.inc(last, 1)
                mk[("am", "mmq", ti, oh, q)] = s_mm.v
                if q == 1:
                    mk[("am", "mm", ti, oh)] = s_mm.v

                @blk.vector
                def _(v, ti=ti, oh=oh, q=q):
                    v.wait_ge(s_mm.h, mk[("am", "mmq", ti, oh, q)])
                    v.wait_ge(s_ld.h, mk[("am", "r1ld", ti)])
                    if q == 0:
                        if oh == 1:
                            v.wait_ge(s_st.h, mk[("am", "r2st", ti, 0)])
                        elif ti > 0:
                            v.wait_ge(s_st.h, mk[("am", "r2st", ti - 1, 1)])
                    last = None
                    for ot4 in range(4):
                        ot = q * 4 + ot4
                        go = oh * 8 + ot
                        last = v.tensor_add(r232[:, ot * TS:(ot + 1) * TS],
                                            p_o[ot4][:],
                                            res1t[:, go * TS:(go + 1) * TS])
                    s_v.inc(last, 1)
                mk[("am", "evac", ti, oh, q)] = s_v.v

            @blk.gpsimd
            def _(g, ti=ti, oh=oh):
                g.wait_ge(s_v.h, mk[("am", "evac", ti, oh, 1)])
                for ot in range(8):
                    go = oh * 8 + ot
                    s_st.inc(g.dma_start(
                        out=io["res2"][go * 128:(go + 1) * 128, TS * ti:TS * (ti + 1)],
                        in_=r232[:, ot * TS:(ot + 1) * TS]), 16)
            mk[("am", "r2st", ti, oh)] = s_st.v

    @blk.sync
    def _(sync):
        sync.wait_ge(s_st.h, mk[("am", "r2st", NT - 1, 1)])

    return mk, sems


def declare_am_io(nc, dbg=False):
    io = {}
    io["pv_a"] = nc.dram_tensor("pv_a", [2064, OWN], BF16, kind="ExternalInput").ap()
    io["pv_b"] = nc.dram_tensor("pv_b", [2064, OWN], BF16, kind="ExternalInput").ap()
    io["vfm_d"] = nc.dram_tensor("vfm_d", [KVD, OWN], BF16, kind="ExternalInput").ap()
    io["res1"] = nc.dram_tensor("res1", [D, OWN], F32, kind="ExternalInput").ap()
    io["wpr"] = nc.dram_tensor("wpr", [128, 32768], F32, kind="ExternalInput").ap()
    io["pa"] = nc.dram_tensor("pa", [D, 16], F32, kind="ExternalInput").ap()
    io["pb"] = nc.dram_tensor("pb", [16, D], F32, kind="ExternalInput").ap()
    io["res2"] = nc.dram_tensor("res2", [D, OWN], F32, kind="ExternalOutput").ap()
    return io


def build_phase_am(dbg=False):
    nc = bass.Bass("TRN2", target_bir_lowering=False, debug=False)
    io = declare_am_io(nc, dbg)
    with contextlib.ExitStack() as st:
        with nc.Block() as blk:
            ph = Ph(nc, blk, st, "am_")
            io["_cpool"] = ph.sb("cpool", [128, len(CONSTS_F32)], F32)[:]
            io["_t16b"] = ph.sb("t16b", [16, 512], BF16)[:]
            emit_phase_am(ph, io, dbg)
    return nc


def prep_am_inputs(inputs, cid, pv_pair, vfm_mine, res1_mine):
    b, par = cid // 2, cid % 2
    d = {}
    d["res1"] = np.ascontiguousarray(np.asarray(res1_mine))
    d["pv_a"] = np.ascontiguousarray(np.asarray(pv_pair[par])[par * 2064:(par + 1) * 2064])
    d["pv_b"] = np.ascontiguousarray(np.asarray(pv_pair[1 - par])[par * 2064:(par + 1) * 2064])
    d["vfm_d"] = np.ascontiguousarray(np.asarray(vfm_mine))
    d["wpr"] = np.ascontiguousarray(
        np.asarray(inputs["L1_attn_Wproj"]).reshape(16, 128, 2, 1024)
        .transpose(1, 2, 0, 3).reshape(128, 32768))
    d["pa"] = inputs["L1_attn_p_A"]
    d["pb"] = inputs["L1_attn_p_B"]
    return d



def emit_phase_e(ph, io, dbg=False):
    nc, blk = ph.nc, ph.blk
    TS = 512
    TB = 1024
    NTB = OWN // TB

    sems = {k: ph.sem(k) for k in ["ld", "gd", "pre", "wt", "st", "mm", "v", "a",
                                   "wtB", "e4A", "e4B"]}
    s_ld, s_gd, s_pre, s_wt, s_st, s_mm, s_v, s_a = (
        sems["ld"], sems["gd"], sems["pre"], sems["wt"], sems["st"],
        sems["mm"], sems["v"], sems["a"])
    s_wtB, s_e4A, s_e4B = sems["wtB"], sems["e4A"], sems["e4B"]
    mk = {}

    ones1 = ph.sb("ones1", [1, 128], BF16)
    ones128 = ph.sb("ones128", [128, 1], BF16)
    identbf = ph.sb("identbf", [128, 128], BF16)
    gatesb = ph.sb("gatesb", [128, 16 * 8], BF16)
    xe = ph.sb("xe", [128, 16 * TB], BF16)
    xs = ph.sb("xs", [128, 16 * TB], BF16)
    wstr = ph.sb("wstrE", [128, 16 * 128 * 2], BF16)
    s1sb = ph.sb("s1sb", [128, TB], BF16)
    s3sb = ph.sb("s3sb", [128, TB], BF16)
    hhsb = ph.sb("hhsb", [128, TB], BF16)
    sgbf = ph.sb("sgbf", [8, TS], BF16)
    stmf = ph.sb("stmf", [128, 8 * 4], F32)
    selt = ph.sb("selt", [128, 8], BF16)
    scr8 = ph.sb("scr8", [128, 4 * 8], F32)
    self_fm = ph.sb("selfm", [8, TS], BF16)
    rowe = ph.sb("rowe", [128, 6], F32)
    wrep = ph.sb("wrep", [128, TB], BF16)
    w2str = ph.sb("w2str", [128, 2 * TS], BF16)
    hh2 = ph.sb("hh2", [128, 2 * TB], BF16)
    o32 = ph.sb("o32", [128, 8 * TS], F32)
    res2t = ph.sb("res2t", [128, 4 * TB], F32)
    lnx32 = ph.sb("lx32", [128, 16 * TS], F32)
    lnxbf = ph.sb("lxbf", [128, 16 * TS], BF16)
    lnscr = ph.sb("lscr", [128, TS], F32)
    lnrows = ph.sb("lrows", [1, 4 * TS], F32)
    lnrowbf = ph.sb("lrowbf", [1, 2 * TS], BF16)

    @blk.gpsimd
    def _(g):
        if "_cpool" in io:
            for ci, val in enumerate(CONSTS_F32):
                g.memset(io["_cpool"][:, ci:ci + 1], val)
                nc.const_aps.register(F32, val, io["_cpool"][:, ci:ci + 1])
        g.memset(ones1[:], 1.0)
        g.memset(ones128[:], 1.0)
        s_pre.inc(g.dma_start(out=identbf[:], in_=io["ident"]), 16)
        for c in range(16):
            s_pre.inc(g.dma_start(out=gatesb[:, c * 8:(c + 1) * 8],
                                  in_=io["gate_t"][c * 128:(c + 1) * 128, :]), 16)
    mk["pre"] = s_pre.v

    for tb in range(NTB):
        with contextlib.ExitStack() as stA:
            pA = Ph(nc, blk, stA, f"e{tb}a_")
            with contextlib.ExitStack() as stLN:
                pLN = Ph(nc, blk, stLN, f"e{tb}ln_")
                lnb = {
                    "x32": lnx32, "xbf": lnxbf, "xsq": xs, "xnbf": xe,
                    "scr": lnscr, "rows": lnrows, "rowbf": lnrowbf,
                    "ones1": ones1, "ones128": ones128,
                    "p_r1": pLN.ps("lpr1", [1, TS]),
                    "p_r2": pLN.ps("lpr2", [1, TS]),
                    "p_a": pLN.ps("lpa", [128, TS]),
                    "p_b": pLN.ps("lpb", [128, TS]),
                }
                for i2 in range(2):
                    i = 2 * tb + i2
                    if i == 0:
                        war = [(s_pre, "pre")]
                    elif i2 == 0:
                        war = [(s_st, ("e", "outst", tb - 1, 3)),
                               (s_v, ("e", "ln", i - 1))]
                    else:
                        war = [(s_v, ("e", "ln", i - 1))]
                    emit_ln_tile(blk, lnb, sems, mk, "e", i, io["res2"], war,
                                 dst_stride=TB, dst_off=i2 * TS)

            p_g = pA.ps("pg", [8, TS])
            p_t8 = pA.ps("pt8", [128, 8], BF16)
            p_t8b = pA.ps("pt8b", [8, 128], BF16)
            p_h = [pA.ps(f"ph{k}", [128, TS]) for k in range(4)]

            for i2 in range(2):
                i = 2 * tb + i2

                @blk.tensor
                def _(t, i2=i2, i=i):
                    t.wait_ge(s_v.h, mk[("e", "ln", i)])
                    if ("e", "sgused") in mk:
                        t.wait_ge(s_a.h, mk[("e", "sgused")])
                    last = None
                    for c in range(16):
                        last = t.matmul(p_g[:], gatesb[:, c * 8:(c + 1) * 8],
                                        xe[:, c * TB + i2 * TS:
                                           c * TB + i2 * TS + TS],
                                        start=(c == 0), stop=(c == 15))
                    s_mm.inc(last, 1)
                mk[("e", "gmm", i)] = s_mm.v

                @blk.scalar
                def _(s, i=i):
                    s.wait_ge(s_mm.h, mk[("e", "gmm", i)])
                    if ("e", "sgused2") in mk:
                        s.wait_ge(s_mm.h, mk[("e", "sgused2")])
                    s_a.inc(s.activation(sgbf[:], p_g[:], AF.Sigmoid), 1)
                    mk[("e", "sgused")] = s_a.v
                mk[("e", "sig", i)] = s_a.v

                for m in range(4):
                    @blk.tensor
                    def _(t, i=i, m=m):
                        t.wait_ge(s_a.h, mk[("e", "sig", i)])
                        if ("e", "t8used") in mk:
                            t.wait_ge(s_v.h, mk[("e", "t8used")])
                        last = t.transpose(p_t8[:], sgbf[:, m * 128:(m + 1) * 128],
                                           identbf[0:8, 0:8])
                        s_mm.inc(last, 1)
                        if m == 3:
                            mk[("e", "sgused2")] = s_mm.v
                    mk[("e", "tr", i, m)] = s_mm.v

                    @blk.vector
                    def _(v, i=i, m=m):
                        v.wait_ge(s_mm.h, mk[("e", "tr", i, m)])
                        s_v.inc(v.tensor_copy(stmf[:, m * 8:(m + 1) * 8],
                                              p_t8[:]), 1)
                        mk[("e", "t8used")] = s_v.v
                    mk[("e", "trv", i, m)] = s_v.v

                for m in range(4):
                    @blk.vector
                    def _(v, i=i, m=m):
                        v.wait_ge(s_v.h, mk[("e", "trv", i, 3)])
                        if ("e", "selused") in mk:
                            v.wait_ge(s_mm.h, mk[("e", "selused")])
                        sl = stmf[:, m * 8:(m + 1) * 8]
                        Rw = lambda k: rowe[:, k:k + 1]
                        v.reduce_max(Rw(0), sl, AX)
                        v.tensor_scalar(selt[:], sl, Rw(0), None, ALU.is_equal)
                        v.scalar_tensor_tensor(scr8[:, 0:8], selt[:], -1e38, sl,
                                               ALU.mult, ALU.add)
                        v.reduce_max(Rw(1), scr8[:, 0:8], AX)
                        v.tensor_scalar(scr8[:, 8:16], scr8[:, 0:8], Rw(1), None,
                                        ALU.is_equal)
                        v.tensor_scalar(scr8[:, 16:24], selt[:], Rw(0), None,
                                        ALU.mult)
                        v.tensor_scalar(scr8[:, 24:32], scr8[:, 8:16], Rw(1), None,
                                        ALU.mult)
                        s_v.inc(v.tensor_add(selt[:], scr8[:, 16:24],
                                             scr8[:, 24:32]), 1)
                    mk[("e", "top2", i, m)] = s_v.v

                    @blk.tensor
                    def _(t, i=i, m=m):
                        t.wait_ge(s_v.h, mk[("e", "top2", i, m)])
                        last = t.transpose(p_t8b[:], selt[:], identbf[:])
                        s_mm.inc(last, 1)
                        mk[("e", "selused")] = s_mm.v
                    mk[("e", "trb", i, m)] = s_mm.v

                    @blk.vector
                    def _(v, i=i, m=m):
                        v.wait_ge(s_mm.h, mk[("e", "trb", i, m)])
                        s_v.inc(v.tensor_copy(self_fm[:, m * 128:(m + 1) * 128],
                                              p_t8b[:]), 1)
                    mk[("e", "selfm", i, m)] = s_v.v

                @blk.gpsimd
                def _(g, i=i):
                    g.wait_ge(s_v.h, mk[("e", "selfm", i, 3)])
                    s_st.inc(g.dma_start(out=io["sel_d"][:, i * TS:(i + 1) * TS],
                                         in_=self_fm[:]), 16)
                mk[("e", "selst", i)] = s_st.v

            for e in range(9):
                if e < 8:
                    @blk.gpsimd
                    def _(g, tb=tb, e=e):
                        g.wait_ge(s_st.h, mk[("e", "selst", 2 * tb + 1)])
                        if ("e", "wrepused") in mk:
                            g.wait_ge(s_v.h, mk[("e", "wrepused")])
                        s_gd.inc(g.dma_start(
                            out=wrep[:],
                            in_=io["sel_d"][e:e + 1, tb * TB:(tb + 1) * TB]
                            .partition_broadcast(128)), 16)
                    mk[("e", "wrep", tb, e)] = s_gd.v

                    @blk.vector
                    def _(v, tb=tb, e=e):
                        v.wait_ge(s_gd.h, mk[("e", "wrep", tb, e)])
                        v.wait_ge(s_v.h, mk[("e", "ln", 2 * tb + 1)])
                        if ("e", "xsused") in mk:
                            v.wait_ge(s_mm.h, mk[("e", "xsused")])
                        last = None
                        for c in range(16):
                            last = v.tensor_mul(xs[:, c * TB:(c + 1) * TB],
                                                xe[:, c * TB:(c + 1) * TB],
                                                wrep[:])
                        s_v.inc(last, 1)
                        mk[("e", "wrepused")] = s_v.v
                    mk[("e", "xs", tb, e)] = s_v.v

                for jt in range(8):
                    for mat in range(2):
                        wsrc = (io["w1_r"] if mat == 0 else io["w3_r"]) if e < 8 \
                            else (io["ws1_r"] if mat == 0 else io["ws3_r"])
                        coff = ((e * 8 + jt) if e < 8 else jt) * 2048
                        wseq = (tb * 9 + e) * 16 + jt * 2 + mat
                        slot = (wseq % 2) if E3_DBUF else 0

                        s_w = s_wt if slot == 0 else s_wtB

                        @blk.gpsimd
                        def _(g, tb=tb, e=e, jt=jt, mat=mat, wsrc=wsrc,
                              coff=coff, wseq=wseq, slot=slot, s_w=s_w):
                            back = 2 if E3_DBUF else 1
                            if ("e", "hmm_seq", wseq - back) in mk:
                                g.wait_ge(s_mm.h, mk[("e", "hmm_seq", wseq - back)])
                            s_w.inc(g.dma_start(
                                out=wstr[:, slot * 2048:(slot + 1) * 2048],
                                in_=wsrc[:, coff:coff + 2048]), 16)
                        mk[("e", "wld", tb, e, jt, mat)] = s_w.v

                        @blk.tensor
                        def _(t, tb=tb, e=e, jt=jt, mat=mat, slot=slot, s_w=s_w):
                            t.wait_ge(s_w.h, mk[("e", "wld", tb, e, jt, mat)])
                            if e < 8:
                                t.wait_ge(s_v.h, mk[("e", "xs", tb, e)])
                            else:
                                t.wait_ge(s_v.h, mk[("e", "ln", 2 * tb + 1)])
                            if ("e", "hev", mat) in mk:
                                t.wait_ge(s_a.h, mk[("e", "hev", mat)])
                            src = xs if e < 8 else xe
                            last = None
                            for t2 in range(2):
                                for c in range(16):
                                    last = t.matmul(
                                        p_h[mat * 2 + t2][:],
                                        wstr[:, slot * 2048 + c * 128:
                                             slot * 2048 + (c + 1) * 128],
                                        src[:, c * TB + t2 * TS:
                                            c * TB + t2 * TS + TS],
                                        start=(c == 0), stop=(c == 15))
                            s_mm.inc(last, 1)
                            if e < 8:
                                mk[("e", "xsused")] = s_mm.v
                        mk[("e", "hmm", tb, e, jt, mat)] = s_mm.v
                        mk[("e", "hmm_seq", wseq)] = s_mm.v
                        mk[("e", "wused")] = s_mm.v

                        @blk.scalar
                        def _(s, tb=tb, e=e, jt=jt, mat=mat):
                            s.wait_ge(s_mm.h, mk[("e", "hmm", tb, e, jt, mat)])
                            if mat == 0 and ("e", "hhv_prev") in mk:
                                s.wait_ge(s_v.h, mk[("e", "hhv_prev")])
                            dst = s1sb if mat == 0 else s3sb
                            last = None
                            for t2 in range(2):
                                if mat == 0:
                                    last = s.activation(
                                        dst[:, t2 * TS:(t2 + 1) * TS],
                                        p_h[t2][:], AF.Silu)
                                else:
                                    last = s.copy(dst[:, t2 * TS:(t2 + 1) * TS],
                                                  p_h[2 + t2][:])
                            s_a.inc(last, 1)
                            mk[("e", "hev", mat)] = s_a.v
                        mk[("e", "hevm", tb, e, jt, mat)] = s_a.v

                    @blk.vector
                    def _(v, tb=tb, e=e, jt=jt):
                        v.wait_ge(s_a.h, mk[("e", "hevm", tb, e, jt, 1)])
                        if ("e", "hhst_prev") in mk:
                            v.wait_ge(s_st.h, mk[("e", "hhst_prev")])
                        s_v.inc(v.tensor_mul(hhsb[:], s1sb[:], s3sb[:]), 1)
                        mk[("e", "hhv_prev")] = s_v.v
                    mk[("e", "hhv", tb, e, jt)] = s_v.v

                    @blk.gpsimd
                    def _(g, tb=tb, e=e, jt=jt):
                        g.wait_ge(s_v.h, mk[("e", "hhv", tb, e, jt)])
                        s_st.inc(g.dma_start(
                            out=io["hh_d"][e * EH + jt * 128:
                                           e * EH + (jt + 1) * 128,
                                           tb * TB:(tb + 1) * TB],
                            in_=hhsb[:]), 16)
                        mk[("e", "hhst_prev")] = s_st.v
                    mk[("e", "hhst", tb, e, jt)] = s_st.v

        with contextlib.ExitStack() as stB:
            pB = Ph(nc, blk, stB, f"e{tb}b_")
            p_ro = [pB.ps(f"pro{k}", [128, TS]) for k in range(8)]

            for hq in range(4):
                @blk.sync
                def _(sync, tb=tb, hq=hq):
                    if ("e", "evdone") in mk:
                        sync.wait_ge(s_v.h, mk[("e", "evdone")])
                    for ot in range(4):
                        ho = hq * 4 + ot
                        s_ld.inc(sync.dma_start(
                            out=res2t[:, ot * TB:(ot + 1) * TB],
                            in_=io["res2"][ho * 128:(ho + 1) * 128,
                                           tb * TB:(tb + 1) * TB]), 16)
                mk[("e", "r2ld", tb, hq)] = s_ld.v

                for e in range(9):
                    for j in range(8):
                        rseq = (tb * 4 + hq) * 72 + e * 8 + j
                        slot = rseq % 2 if E4_DBUF else 0

                        s_e4 = s_e4A if slot == 0 else s_e4B

                        @blk.gpsimd
                        def _(g, tb=tb, hq=hq, e=e, j=j, rseq=rseq, slot=slot,
                              s_e4=s_e4):
                            if hq == 0 and e == 0 and j == 0:
                                g.wait_ge(s_st.h, mk[("e", "hhst", tb, 8, 7)])
                            back = 2 if E4_DBUF else 1
                            if ("e", "romm_seq", rseq - back) in mk:
                                g.wait_ge(s_mm.h, mk[("e", "romm_seq", rseq - back)])
                            if e < 8:
                                s_e4.inc(g.dma_start(
                                    out=w2str[:, slot * TS:(slot + 1) * TS],
                                    in_=io["w2_t"][e * EH + j * 128:
                                                   e * EH + (j + 1) * 128,
                                                   hq * TS:(hq + 1) * TS]), 16)
                            else:
                                s_e4.inc(g.dma_start(
                                    out=w2str[:, slot * TS:(slot + 1) * TS],
                                    in_=io["ws2_t"][j * 128:(j + 1) * 128,
                                                    hq * TS:(hq + 1) * TS]), 16)
                            s_e4.inc(g.dma_start(
                                out=hh2[:, slot * TB:(slot + 1) * TB],
                                in_=io["hh_d"][e * EH + j * 128:
                                               e * EH + (j + 1) * 128,
                                               tb * TB:(tb + 1) * TB]), 16)
                        mk[("e", "w2ld", tb, hq, e, j)] = s_e4.v

                        @blk.tensor
                        def _(t, tb=tb, hq=hq, e=e, j=j, slot=slot, rseq=rseq,
                              s_e4=s_e4):
                            t.wait_ge(s_e4.h, mk[("e", "w2ld", tb, hq, e, j)])
                            if e == 0 and j == 0:
                                t.wait_ge(s_st.h, mk[("e", "hhst", tb, 8, 7)])
                                if ("e", "evdone") in mk:
                                    t.wait_ge(s_v.h, mk[("e", "evdone")])
                            last = None
                            for ot in range(4):
                                for t2 in range(2):
                                    last = t.matmul(
                                        p_ro[ot * 2 + t2][:],
                                        w2str[:, slot * TS + ot * 128:
                                              slot * TS + (ot + 1) * 128],
                                        hh2[:, slot * TB + t2 * TS:
                                            slot * TB + (t2 + 1) * TS],
                                        start=(e == 0 and j == 0),
                                        stop=(e == 8 and j == 7))
                            s_mm.inc(last, 1)
                            mk[("e", "romm_seq", rseq)] = s_mm.v
                        mk[("e", "romm", tb, hq, e, j)] = s_mm.v

                @blk.vector
                def _(v, tb=tb, hq=hq):
                    v.wait_ge(s_mm.h, mk[("e", "romm", tb, hq, 8, 7)])
                    v.wait_ge(s_ld.h, mk[("e", "r2ld", tb, hq)])
                    if ("e", "outst_prev") in mk:
                        v.wait_ge(s_st.h, mk[("e", "outst_prev")])
                    last = None
                    for ot in range(4):
                        for t2 in range(2):
                            last = v.tensor_add(
                                o32[:, (ot * 2 + t2) * TS:(ot * 2 + t2 + 1) * TS],
                                p_ro[ot * 2 + t2][:],
                                res2t[:, ot * TB + t2 * TS:
                                      ot * TB + t2 * TS + TS])
                    s_v.inc(last, 1)
                    mk[("e", "evdone")] = s_v.v
                mk[("e", "evacall", tb, hq)] = s_v.v

                @blk.gpsimd
                def _(g, tb=tb, hq=hq):
                    g.wait_ge(s_v.h, mk[("e", "evacall", tb, hq)])
                    for ot in range(4):
                        ho = hq * 4 + ot
                        for t2 in range(2):
                            s_st.inc(g.dma_start(
                                out=io["out_own"][ho * 128:(ho + 1) * 128,
                                                  tb * TB + t2 * TS:
                                                  tb * TB + (t2 + 1) * TS],
                                in_=o32[:, (ot * 2 + t2) * TS:
                                        (ot * 2 + t2 + 1) * TS]), 16)
                    mk[("e", "outst_prev")] = s_st.v
                mk[("e", "outst", tb, hq)] = s_st.v
        mk[("e", "m2vdone")] = s_v.v

    @blk.sync
    def _(sync):
        sync.wait_ge(s_st.h, mk[("e", "outst", NTB - 1, 3)])

    return mk, sems


def declare_e_io(nc, dbg=False):
    io = {}
    io["res2"] = nc.dram_tensor("res2", [D, OWN], F32, kind="ExternalInput").ap()
    io["gate_t"] = nc.dram_tensor("gate_t", [D, 8], F32, kind="ExternalInput").ap()
    io["w1_r"] = nc.dram_tensor("w1_r", [128, NE * 8 * 2048], F32, kind="ExternalInput").ap()
    io["w3_r"] = nc.dram_tensor("w3_r", [128, NE * 8 * 2048], F32, kind="ExternalInput").ap()
    io["w2_t"] = nc.dram_tensor("w2_t", [NE * EH, D], F32, kind="ExternalInput").ap()
    io["ws1_r"] = nc.dram_tensor("ws1_r", [128, 8 * 2048], F32, kind="ExternalInput").ap()
    io["ws3_r"] = nc.dram_tensor("ws3_r", [128, 8 * 2048], F32, kind="ExternalInput").ap()
    io["ws2_t"] = nc.dram_tensor("ws2_t", [EH, D], F32, kind="ExternalInput").ap()
    io["ident"] = nc.dram_tensor("ident", [128, 128], F32, kind="ExternalInput").ap()
    io["out_own"] = nc.dram_tensor("out_own", [D, OWN], F32, kind="ExternalOutput").ap()
    kd = "ExternalOutput" if dbg else "Internal"
    io["hh_d"] = nc.dram_tensor("hh_d", [9 * EH, OWN], BF16, kind=kd).ap()
    io["sel_d"] = nc.dram_tensor("sel_d", [8, OWN], BF16, kind=kd).ap()
    return io


def build_phase_e(dbg=False):
    nc = bass.Bass("TRN2", target_bir_lowering=False, debug=False)
    io = declare_e_io(nc, dbg)
    with contextlib.ExitStack() as st:
        with nc.Block() as blk:
            ph = Ph(nc, blk, st, "e_")
            io["_cpool"] = ph.sb("cpool", [128, len(CONSTS_F32)], F32)[:]
            emit_phase_e(ph, io, dbg)
    return nc


def prep_e_inputs(inputs, cid, res2_mine):
    d = {}
    d["res2"] = np.ascontiguousarray(np.asarray(res2_mine))
    d["gate_t"] = np.ascontiguousarray(inputs["L1_ffn_gate_w"].T)
    def relay(wt, ne):
        return np.ascontiguousarray(
            wt.reshape(ne, 16, 128, 8, 128).transpose(2, 0, 3, 1, 4)
            .reshape(128, ne * 8 * 2048))
    d["w1_r"] = relay(inputs["L1_ffn_w1"].transpose(0, 2, 1).reshape(NE * D, EH), NE)
    d["w3_r"] = relay(inputs["L1_ffn_w3"].transpose(0, 2, 1).reshape(NE * D, EH), NE)
    d["w2_t"] = np.ascontiguousarray(
        inputs["L1_ffn_w2"].transpose(0, 2, 1).reshape(NE * EH, D))
    d["ws1_r"] = relay(inputs["L1_ffn_ws1"].T.reshape(D, EH), 1)
    d["ws3_r"] = relay(inputs["L1_ffn_ws3"].T.reshape(D, EH), 1)
    d["ws2_t"] = np.ascontiguousarray(inputs["L1_ffn_ws2"].T)
    d["ident"] = np.eye(128, dtype=np.float32)
    return d





def build_fused():
    nc = bass.Bass("TRN2", target_bir_lowering=False, debug=False)
    RG = [[0, 1], [2, 3], [4, 5], [6, 7]]

    eio = {}
    def ein(name, shape, dt):
        eio[name] = nc.dram_tensor(name, shape, dt, kind="ExternalInput").ap()
        return eio[name]
    ein("x_t", [D, L], F32); ein("nw", [128, 16], F32); ein("nb", [128, 16], F32)
    ein("xproj_t", [D, 192], F32); ein("dtproj_t", [DTR, D], F32)
    ein("dtbias", [128, 16], F32); ein("alog", [128, NG * NN], F32)
    ein("x_own", [D, OWN], F32); ein("D_tile", [128, 16], F32)
    ein("opr", [128, 32768], F32)
    ein("wg_r", [128, 131072], U8); ein("wu_r", [128, 131072], U8)
    ein("wd_r", [128, 131072], U8)
    ein("gsc", [128, 64], F32); ein("usc", [128, 64], F32); ein("dsc", [128, 16], F32)
    ein("wq", [D, D], F32); ein("wk", [D, KVD], F32); ein("wv", [D, KVD], F32)
    ein("qa", [D, 16], F32); ein("qb", [16, D], F32)
    ein("ka", [D, 16], F32); ein("kb", [16, KVD], F32)
    ein("va", [D, 16], F32); ein("vb", [16, KVD], F32)
    ein("qg", [1, 16], F32)
    ein("cosq", [32, L], F32); ein("sinq", [32, L], F32)
    ein("cosk", [32, OWN], F32); ein("sink", [32, OWN], F32)
    ein("cmask", [8 * 128, 512], F32); ein("ident", [128, 128], F32)
    ein("wpr", [128, 32768], F32); ein("pa", [D, 16], F32); ein("pb", [16, D], F32)
    ein("gate_t", [D, 8], F32)
    ein("w1_r", [128, NE * 8 * 2048], F32); ein("w3_r", [128, NE * 8 * 2048], F32)
    ein("w2_t", [NE * EH, D], F32)
    ein("ws1_r", [128, 8 * 2048], F32); ein("ws3_r", [128, 8 * 2048], F32)
    ein("ws2_t", [EH, D], F32)

    def itn(name, shape, dt):
        return nc.dram_tensor(name, shape, dt, kind="Internal").ap()
    y_rs = itn("y_rs", [2 * D, OWN], F32)
    y_m = itn("y_m", [D, OWN], F32)
    delta_d = itn("delta_d", [D, L], BF16)
    u_d = itn("u_d", [D, L], BF16)
    bc_d = itn("bc_d", [64, L], BF16)
    resA_d = itn("resA_d", [D, OWN], F32)
    res1 = itn("res1", [D, OWN], F32)
    xa_own = itn("xa_own", [D, OWN], BF16)
    xa_all = itn("xa_all", [2 * D, OWN], BF16)
    qd = itn("qd", [D, 2 * OWN], BF16)
    mneg_d = itn("mneg_d", [1, 16], F32)
    pv_d = itn("pv_d", [2 * 2064, OWN], BF16)
    pv_m = itn("pv_m", [2064, OWN], BF16)
    vfm_d = itn("vfm_d", [KVD, OWN], BF16)
    res2 = itn("res2", [D, OWN], F32)
    hh_d = itn("hh_d", [9 * EH, OWN], BF16)
    sel_d = itn("sel_d", [8, OWN], BF16)
    out_own = nc.dram_tensor("out_own", [D, OWN], F32, kind="ExternalOutput").ap()

    hdl = {t.tensor.name if hasattr(t, "tensor") else None: t
           for t in [y_rs, y_m, pv_d, pv_m, xa_own, xa_all]}

    with contextlib.ExitStack() as st0:
        with nc.Block() as blk:
            ph0 = Ph(nc, blk, st0, "f_")
            s_ph = ph0.sem("phase")
            s_cc = ph0.sem("cc")
            cpool = ph0.sb("cpool", [128, len(CONSTS_F32)], F32)[:]
            t16b = ph0.sb("t16b", [16, 512], BF16)[:]

            def barrier(waits, cc_jobs=()):
                if cc_jobs:
                    @blk.gpsimd
                    def _(g):
                        for h, v in waits:
                            g.wait_ge(h, v)
                        for kind, op, i_, o_ in cc_jobs:
                            s_cc.inc(g.collective_compute(
                                kind, op, replica_groups=RG,
                                ins=[i_.opt()], outs=[o_.opt()]), 1)
                        g.wait_ge(s_cc.h, s_cc.v)
                        s_ph.inc(g.memset(cpool[0:1, 0:1], CONSTS_F32[0]), 1)
                else:
                    @blk.gpsimd
                    def _(g):
                        for h, v in waits:
                            g.wait_ge(h, v)
                        s_ph.inc(g.memset(cpool[0:1, 0:1], CONSTS_F32[0]), 1)
                k = s_ph.v
                for secname in ["sync", "scalar", "vector", "tensor", "gpsimd"]:
                    sec = getattr(blk, secname)
                    @sec
                    def _(e):
                        e.wait_ge(s_ph.h, k)

            def terminal(sems):
                return [(s.h, s.v) for s in sems.values()]

            with contextlib.ExitStack() as stm:
                iom = {k: eio[k] for k in ["x_t", "nw", "nb", "xproj_t",
                                           "dtproj_t", "dtbias", "alog"]}
                iom.update(y_rs=y_rs, delta_d=delta_d, u_d=u_d, bc_d=bc_d,
                           _cpool=cpool)
                mkm, semm = emit_phase_m(Ph(nc, blk, stm, "m_"), iom)
            barrier(terminal(semm),
                    [("ReduceScatter", ALU.add, y_rs, y_m)])

            with contextlib.ExitStack() as stn:
                ion = {k: eio[k] for k in ["x_own", "D_tile", "opr",
                                           "wg_r", "wu_r", "wd_r", "gsc",
                                           "usc", "dsc"]}
                ion.update(y0own=y_m, y1own=None, res1=res1, xa_own=xa_own,
                           resA_d=resA_d)
                mkn, semn = emit_phase_n(Ph(nc, blk, stn, "n_"), ion)
            barrier(terminal(semn),
                    [("AllGather", ALU.bypass, xa_own, xa_all)])

            with contextlib.ExitStack() as sta:
                ioa = {k: eio[k] for k in ["wq", "wk", "wv", "qa", "qb", "ka",
                                           "kb", "va", "vb", "qg", "cosq",
                                           "sinq", "cosk", "sink", "cmask",
                                           "ident"]}
                ioa.update(xa_own=xa_own, xa_all=xa_all, pv_d=pv_d,
                           vfm_d=vfm_d, qd=qd, mneg_d=mneg_d)
                mka, sema = emit_phase_a(Ph(nc, blk, sta, "a_"), ioa)
            barrier(terminal(sema),
                    [("ReduceScatter", ALU.add, pv_d, pv_m)])

            with contextlib.ExitStack() as stam:
                ioam = {k: eio[k] for k in ["wpr", "pa", "pb"]}
                ioam.update(pv_a=pv_m, pv_b=None, vfm_d=vfm_d, res1=res1,
                            res2=res2, _t16b=t16b)
                mkam, semam = emit_phase_am(Ph(nc, blk, stam, "am_"), ioam)
            barrier(terminal(semam))

            with contextlib.ExitStack() as ste:
                ioe = {k: eio[k] for k in ["gate_t", "w1_r", "w3_r", "w2_t",
                                           "ws1_r", "ws3_r", "ws2_t", "ident"]}
                ioe.update(res2=res2, out_own=out_own, hh_d=hh_d, sel_d=sel_d)
                mke, seme = emit_phase_e(Ph(nc, blk, ste, "e_"), ioe)

            @blk.sync
            def _(sync):
                for h, v in terminal(seme):
                    sync.wait_ge(h, v)
    return nc


def prep_fused_inputs(inputs, cid):
    d = {}
    d.update(prep_m_inputs(inputs, cid))
    n = prep_n_inputs(inputs, cid, (np.zeros((2 * D, OWN), np.float32),) * 2)
    for k in ["y0own", "y1own"]:
        n.pop(k)
    d.update(n)
    a = prep_a_inputs(inputs, cid, (np.zeros((D, OWN), ml_dtypes.bfloat16),) * 2)
    for k in ["xa_own", "xa_all"]:
        a.pop(k)
    d.update(a)
    am = prep_am_inputs(inputs, cid,
                        (np.zeros((2 * 2064, OWN), ml_dtypes.bfloat16),) * 2,
                        np.zeros((KVD, OWN), ml_dtypes.bfloat16),
                        np.zeros((D, OWN), np.float32))
    for k in ["pv_a", "pv_b", "vfm_d", "res1"]:
        am.pop(k)
    d.update(am)
    e = prep_e_inputs(inputs, cid, np.zeros((D, OWN), np.float32))
    e.pop("res2")
    d.update(e)
    return d


def kernel_fused(**inputs):
    if "fused" not in _PROGS:
        _PROGS["fused"] = build_fused()
    cores = list(range(8))
    in_f = [prep_fused_inputs(inputs, cid) for cid in cores]
    rf = run_bass_kernel_spmd(_PROGS["fused"], in_f, core_ids=cores).results
    out = np.empty((B, L, D), np.float32)
    for cid in cores:
        b, par = cid // 2, cid % 2
        idx = own_l_index(par)
        out[b, idx, :] = np.asarray(rf[cid]["out_own"], np.float32).T
    return out


_PROGS = {}


def _programs():
    if "m" not in _PROGS:
        _PROGS["m"] = build_phase_m()
        _PROGS["n"] = build_phase_n()
        _PROGS["a"] = build_phase_a()
        _PROGS["am"] = build_phase_am()
        _PROGS["e"] = build_phase_e()
    return _PROGS


def kernel(**inputs):
    return kernel_fused(**inputs)


def kernel_multilaunch(**inputs):
    progs = _programs()
    cores = list(range(8))

    in_m = [prep_m_inputs(inputs, cid) for cid in cores]
    rm = run_bass_kernel_spmd(progs["m"], in_m, core_ids=cores).results
    y = [rm[c]["y_rs"] for c in cores]

    in_n = [prep_n_inputs(inputs, cid, (y[2 * (cid // 2)], y[2 * (cid // 2) + 1]))
            for cid in cores]
    rn = run_bass_kernel_spmd(progs["n"], in_n, core_ids=cores).results
    res1 = [rn[c]["res1"] for c in cores]
    xa = [rn[c]["xa_own"] for c in cores]

    in_a = [prep_a_inputs(inputs, cid, (xa[2 * (cid // 2)], xa[2 * (cid // 2) + 1]))
            for cid in cores]
    ra = run_bass_kernel_spmd(progs["a"], in_a, core_ids=cores).results
    pv = [ra[c]["pv_d"] for c in cores]
    vfm = [ra[c]["vfm_d"] for c in cores]

    in_am = [prep_am_inputs(inputs, cid,
                            (pv[2 * (cid // 2)], pv[2 * (cid // 2) + 1]),
                            vfm[cid], res1[cid]) for cid in cores]
    ram = run_bass_kernel_spmd(progs["am"], in_am, core_ids=cores).results
    res2 = [ram[c]["res2"] for c in cores]

    in_e = [prep_e_inputs(inputs, cid, res2[cid]) for cid in cores]
    re_ = run_bass_kernel_spmd(progs["e"], in_e, core_ids=cores).results

    out = np.empty((B, L, D), np.float32)
    for cid in cores:
        b, par = cid // 2, cid % 2
        idx = own_l_index(par)
        out[b, idx, :] = np.asarray(re_[cid]["out_own"], np.float32).T
    return out
